# version 06
# baseline (speedup 1.0000x reference)
import math
import contextlib
import numpy as np
import ml_dtypes

import concourse.bass as bass
import concourse.mybir as mybir
from concourse.bass_utils import run_bass_kernel_spmd

F32 = mybir.dt.float32
BF16 = mybir.dt.bfloat16
U32 = mybir.dt.uint32
I32 = mybir.dt.int32
AF = mybir.ActivationFunctionType
ALU = mybir.AluOpType
AX = mybir.AxisListType

B, L, D, H, DK, U = 2, 8192, 1024, 16, 64, 50
TSL = L // 4
NT = 16
TT = 512
KC = 8
HPC = 4
MSL = HPC * DK
SCALE = 0.125
EPS = 1e-5
NBIS = 26
BIG = 65536.0
GROUPS = [[0, 1, 2, 3], [4, 5, 6, 7]]
bfdt = ml_dtypes.bfloat16


def build_nc(upto=2, probes=(), p2mode='full'):
    nc = bass.Bass("TRN2", target_bir_lowering=False, debug=False)

    din = lambda n, s, d: nc.dram_tensor(n, s, d, kind="ExternalInput").ap()
    xT = din("xT", [D, L], F32)
    xTsl = din("xTsl", [D, TSL], F32)
    xR = din("xR", [L, D], F32)
    wqT = din("wqT", [D, MSL], F32)
    wkT = din("wkT", [D, MSL], F32)
    wvT = din("wvT", [D, MSL], F32)
    bq_in = din("bq", [1, MSL], F32)
    bk_in = din("bk", [1, MSL], F32)
    bv_in = din("bv", [1, MSL], F32)
    bo_in = din("bo", [1, D], F32)
    idxk_in = din("idxk", [U], I32)
    identf_in = din("identf", [128, 128], F32)
    identb_in = din("identb", [128, 128], BF16)
    iota_sl_in = din("iota_sl", [128, TSL], F32)
    iota_j_in = din("iota_j", [128, 64], F32)
    iota_ml_in = din("iota_ml", [128, 64], F32)
    woT_in = din("woT", [D, D], F32)
    woR_in = din("woR", [MSL, D], F32)
    cw1_in = din("cw1", [D, D], F32)
    cw2_in = din("cw2", [D, D // 2], F32)
    cw3_in = din("cw3", [D // 2, D // 4], F32)
    cw4_in = din("cw4", [128, 2], F32)
    gw1_in = din("gw1", [D, D // 2], F32)
    gw2_in = din("gw2", [D // 2, D // 4], F32)
    gw3_in = din("gw3", [128, 2], F32)
    b1c_in = din("b1c", [128, 8], F32)
    b2c_in = din("b2c", [128, 4], F32)
    b3c_in = din("b3c", [128, 2], F32)
    b4c_in = din("b4c", [1, 1], F32)
    gb1c_in = din("gb1c", [128, 4], F32)
    gb2c_in = din("gb2c", [128, 2], F32)
    gb3c_in = din("gb3c", [1, 1], F32)
    ar_in = din("arin", [1, 1], F32)

    out = nc.dram_tensor("out", [D, TSL], F32, kind="ExternalOutput").ap()

    e0in = nc.dram_tensor("e0in", [2, TSL], F32)
    e0out = nc.dram_tensor("e0out", [8, TSL], F32)
    murow_d = nc.dram_tensor("murow_d", [L], F32)
    rrec_d = nc.dram_tensor("rrec_d", [L], F32)
    e1ain = nc.dram_tensor("e1ain", [1, MSL], F32)
    e1aout = nc.dram_tensor("e1aout", [4, MSL], F32)
    base_d = nc.dram_tensor("base_d", [L // 8], F32)
    e1din = nc.dram_tensor("e1din", [HPC * U, D], BF16)
    e1dout = nc.dram_tensor("e1dout", [4 * HPC * U, D], BF16)
    e1iin = nc.dram_tensor("e1iin", [1, 224], F32)
    e1iout = nc.dram_tensor("e1iout", [4, 224], F32)
    e2in = nc.dram_tensor("e2in", [128, 8], F32)
    e2out = nc.dram_tensor("e2out", [512, 8], F32)
    woT_in2 = None

    probe_specs = {name: (shape, dtp) for (name, shape, dtp) in probes}
    probe_aps = {}
    for pn, (shape, dtp) in probe_specs.items():
        probe_aps[pn] = nc.dram_tensor("probe_" + pn, shape, dtp, kind="ExternalOutput").ap()
    probe_srcs = {}

    def reg_probe(name, ap):
        if name in probe_aps:
            probe_srcs[name] = ap

    es = contextlib.ExitStack()
    sb = lambda n, s, d: es.enter_context(nc.sbuf_tensor(n, s, d))
    pst = lambda n, s, d: es.enter_context(nc.psum_tensor(n, s, d))
    sem = lambda n: es.enter_context(nc.semaphore(n))

    with es:
        esA = contextlib.ExitStack()
        sbA = lambda n, s, d: esA.enter_context(nc.sbuf_tensor(n, s, d))
        wq_s = sb("wq_s", [128, KC * MSL], BF16)
        kT = [sb("kT0", [128, L], BF16), sb("kT1", [128, L], BF16)]
        vs_all = sb("vs_all", [128, 2 * 64 * 130], BF16)
        m_buf = sb("m_buf", [128, HPC * 64], F32)
        r_ml = sb("r_ml", [128, 64], F32)
        r8_ml = sb("r8_ml", [128, 64], F32)
        kst = [sb("kst%d" % h, [128, 51], BF16) for h in range(HPC)]
        drows = sb("drows", [1, HPC * 52], BF16)
        bvrow = sb("bvrow", [1, MSL], F32)
        borow = sb("borow", [1, D], F32)
        onesc = sb("onesc", [128, 1], BF16)
        ones_row = sb("ones_row", [1, 64], BF16)
        identf_s = sb("identf_s", [128, 128], F32)
        identb_s = sb("identb_s", [128, 128], BF16)
        iota_ml_s = sb("iota_ml_s", [128, 64], F32)
        iota_j_s = sb("iota_j_s", [128, 64], F32)
        xg = sb("xg", [128, D], F32)
        xg2 = sb("xg2", [4, 64], F32)
        xgn = sb("xgn", [128, D], BF16)
        xsampT = sb("xsampT", [128, KC * U], BF16)
        bn6 = sb("bn6", [128, 12], F32)
        bnag = sb("bnag", [128, 2], F32)
        bqcol = sb("bqcol", [128, HPC], BF16)
        kbar_f = sb("kbar_f", [128, 1], F32)
        idxk_s = sb("idxk_s", [U, 1], U32)
        scmax_v = sb("scmax_v", [128, 16], F32)
        epscol = sb("epscol", [128, 1], F32)

        wk_s = sbA("wk_s", [128, KC * MSL], BF16)
        wv_s = sbA("wv_s", [128, KC * MSL], BF16)
        stat2 = sbA("stat2", [2, L], BF16)
        rrec_row = sbA("rrec_row", [1, L], BF16)
        fixq = sbA("fixq", [2, MSL], BF16)
        fixk = sbA("fixk", [1, MSL], BF16)
        fixv = sbA("fixv", [1, MSL], BF16)
        bkrow = sbA("bkrow", [1, MSL], BF16)
        xbf = [sbA("xbf0", [128, KC * TT], BF16), sbA("xbf1", [128, KC * TT], BF16)]
        qt_tmp = sbA("qt_tmp", [128, 4 * TT], BF16)
        stat_sum = sbA("stat_sum", [1, TSL], F32)
        stat_sq = sbA("stat_sq", [1, TSL], F32)
        sums_t = sbA("sums_t", [64, 128], F32)
        sumsq_t = sbA("sumsq_t", [64, 128], F32)
        mu_t = sbA("mu_t", [64, 128], F32)
        var_t = sbA("var_t", [64, 128], F32)
        r_t = sbA("r_t", [64, 128], F32)
        rrec_t = sbA("rrec_t", [64, 128], F32)
        ps_q = pst("ps_q", [128, 1024], F32)
        ps_k = pst("ps_k", [128, 1024], F32)
        ps_v = pst("ps_v", [128, 1024], F32)

        s_ld = sem("s_ld")
        s_gl = sem("s_gl")
        s_xl = [sem("s_xl0"), sem("s_xl1")]
        s_pv = sem("s_pv")
        s_ps = sem("s_ps")
        s_ptc = sem("s_ptc")
        s_ptt = sem("s_ptt")
        s_ptk = sem("s_ptk")
        s_ptd = sem("s_ptd")
        s_psk = sem("s_psk")
        s_pskb = sem("s_pskb")
        s_1v = sem("s_1v")
        s_1t = sem("s_1t")
        s_1q = sem("s_1q")
        s_1s = sem("s_1s")
        s_1g = sem("s_1g")
        s_cc = sem("s_cc")
        s_1r = sem("s_1r")
        s_e0 = sem("s_e0")
        s_kd = sem("s_kd")
        s_qd = sem("s_qd")
        s_vd = sem("s_vd")
        s_scp = sem("s_scp")
        s_evk = sem("s_evk")
        s_evq = sem("s_evq")
        s_evv = sem("s_evv")
        s_mst = sem("s_mst")
        s_st = sem("s_st")
        ch_v = sem("ch_v")
        s_mv = sem("s_mv")
        s_b0 = sem("s_b0")
        s_b1 = sem("s_b1")
        s_b2 = sem("s_b2")
        s_b3 = sem("s_b3")
        s_b4 = sem("s_b4")
        s_r0 = sem("s_r0")
        s_r1 = sem("s_r1")
        s_r2 = sem("s_r2")
        s_cmv = sem("s_cmv")
        s_cmt = sem("s_cmt")
        s_cms = sem("s_cms")
        s_vm = sem("s_vm")
        s_vms = sem("s_vms")
        s_bs = sem("s_bs")
        s_g2 = sem("s_g2")
        s_e1a = sem("s_e1a")
        s_vm8 = sem("s_vm8")
        s_bd = sem("s_bd")
        s_wo = sem("s_wo")
        s_wo2 = sem("s_wo2")
        ch_s = sem("ch_s")
        ch_g = sem("ch_g")
        s_4g = sem("s_4g")
        s_4w = sem("s_4w")
        s_4w2 = sem("s_4w2")
        s_4v = sem("s_4v")
        s_4a = sem("s_4a")
        s_4tt = sem("s_4tt")
        s_4sk = sem("s_4sk")
        s_4tq = sem("s_4tq")
        s_4sq = sem("s_4sq")
        s_4tb = sem("s_4tb")
        s_4se = sem("s_4se")
        s_4tc = sem("s_4tc")
        s_4tr = sem("s_4tr")
        s_4cv = sem("s_4cv")
        s_4ds = sem("s_4ds")
        s_4td = sem("s_4td")
        s_4dd = sem("s_4dd")
        s_4vc = sem("s_4vc")
        s_4vs = sem("s_4vs")
        s_5z = sem("s_5z")
        s_5d = sem("s_5d")
        s_5ix = sem("s_5ix")
        s_5it = sem("s_5it")
        s_5is = sem("s_5is")
        s_5m = sem("s_5m")
        s_5t = sem("s_5t")
        s_5v = sem("s_5v")
        s_6w = sem("s_6w")
        s_6wb = sem("s_6wb")
        s_6v = sem("s_6v")
        s_6t = sem("s_6t")
        s_6s = sem("s_6s")
        s_6rt = sem("s_6rt")
        s_6re = sem("s_6re")
        s_6y = sem("s_6y")
        s_6yg = sem("s_6yg")
        s_6ys = sem("s_6ys")
        s_6g = sem("s_6g")
        s_6gq = sem("s_6gq")
        s_6gt = sem("s_6gt")
        s_6gs = sem("s_6gs")
        s_7t = sem("s_7t")
        s_7e = sem("s_7e")
        s_7tf = sem("s_7tf")
        s_7v = sem("s_7v")
        s_7g = sem("s_7g")
        s_7c = sem("s_7c")
        s_8d = sem("s_8d")

        block = es.enter_context(nc.Block())

        def wchunk(w, k, lo=0, n=MSL):
            return w[:, k * MSL + lo:k * MSL + lo + n]

        with nc.psum_tensor("ps_trb", [128, 128], BF16) as ps_trb, \
             nc.psum_tensor("ps_tr", [128, 128], F32) as ps_tr:

            @block.sync
            def _(sync):
                sync.dma_start(out=identf_s[:, :], in_=identf_in).then_inc(s_ld, 16)
                sync.dma_start(out=iota_ml_s[:, :], in_=iota_ml_in).then_inc(s_ld, 16)
                sync.dma_start(out=iota_j_s[:, :], in_=iota_j_in).then_inc(s_ld, 16)
                sync.dma_start(out=bvrow[:, :], in_=bv_in).then_inc(s_ld, 16)
                sync.dma_start(out=borow[:, :], in_=bo_in).then_inc(s_ld, 16)
                sync.dma_start(out=xg2[0:4, 0:64], in_=bq_in.rearrange("o (h k) -> (o h) k", k=64)).then_inc(s_ld, 16)

            @block.gpsimd
            def _(gpsimd):
                gpsimd.dma_start(out=wq_s[:, :].rearrange("p (k m) -> p k m", m=MSL),
                                 in_=wqT.rearrange("(k p) m -> p k m", p=128)).then_inc(s_gl, 16)
                gpsimd.dma_start(out=wk_s[:, :].rearrange("p (k m) -> p k m", m=MSL),
                                 in_=wkT.rearrange("(k p) m -> p k m", p=128)).then_inc(s_gl, 16)
                gpsimd.dma_start(out=wv_s[:, :].rearrange("p (k m) -> p k m", m=MSL),
                                 in_=wvT.rearrange("(k p) m -> p k m", p=128)).then_inc(s_gl, 16)
                gpsimd.dma_start(out=identb_s[:, :], in_=identb_in).then_inc(s_gl, 16)
                gpsimd.dma_start(out=bkrow[:, :], in_=bk_in).then_inc(s_gl, 16)
                gpsimd.dma_start(out=idxk_s[:, :], in_=idxk_in.rearrange("(p o) -> p o", o=1)).then_inc(s_gl, 16)
                gpsimd.wait_ge(s_gl, 16 * 6)
                gpsimd.indirect_dma_start(
                    out=xg[0:U, :], out_offset=None, in_=xR,
                    in_offset=bass.IndirectOffsetOnAxis(idxk_s[:, 0:1], 0),
                ).then_inc(s_gl, 16)

            @block.vector
            def _(vector):
                vector.memset(onesc[:, :], 1.0)
                vector.memset(epscol[:, :], EPS)
                vector.memset(ones_row[:, :], 1.0)
                vector.memset(stat2[0:2, :], 1.0)
                vector.memset(drows[:, :], 0.0)
                for h in range(HPC):
                    vector.memset(kst[h][:, :], 0.0)
                vector.memset(bqcol[:, :], 0.0)
                vector.memset(vs_all[:, :].rearrange("p (r c a y) -> p r c a y", r=2, c=64, y=65)[:, :, :, :, 64:65],
                              1.0).then_inc(s_pv, 1)
                vector.wait_ge(s_gl, 112)
                vector.bn_stats(bn6[0:U, 0:6], xg[0:U, 0:512]).then_inc(ch_v, 1)
                vector.bn_stats(bn6[0:U, 6:12], xg[0:U, 512:1024]).then_inc(ch_v, 1)
                vector.wait_ge(ch_v, 2)
                vector.bn_aggr(bnag[0:U, :], bn6[0:U, :]).then_inc(s_pv, 1)
                vector.wait_ge(s_ps, 1)
                vector.tensor_scalar(xgn[0:U, :], xg[0:U, :], bnag[0:U, 0:1],
                                     bnag[0:U, 1:2], ALU.subtract, ALU.mult).then_inc(s_pv, 1)

            @block.scalar
            def _(scalar):
                scalar.wait_ge(s_pv, 2)
                scalar.activation(bnag[0:U, 1:2], bnag[0:U, 1:2], AF.Ln, bias=epscol[0:U, 0:1]).then_inc(ch_s, 1)
                scalar.wait_ge(ch_s, 1)
                scalar.activation(bnag[0:U, 1:2], bnag[0:U, 1:2], AF.Exp, scale=-0.5).then_inc(s_ps, 1)

            @block.tensor
            def _(tensor):
                tensor.wait_ge(s_gl, 16 * 6)
                tensor.wait_ge(s_pv, 1)
                for wi, wsb in enumerate((wq_s, wk_s, wv_s)):
                    if wi > 0:
                        tensor.wait_ge(s_ps, 1 + wi)
                    for k in range(KC):
                        mm = tensor.matmul(ps_v[0:1, 0:MSL], onesc[:, :], wchunk(wsb, k),
                                           start=(k == 0), stop=(k == KC - 1))
                    mm.then_inc(s_ptc, 1)
                tensor.wait_ge(s_pv, 3)
                for k in range(KC):
                    if k > 0:
                        tensor.wait_ge(s_psk, k)
                    tensor.transpose(ps_trb[:, 0:U], xgn[0:U, k * 128:(k + 1) * 128],
                                     identb_s[0:U, 0:U]).then_inc(s_ptt, 1)
                tensor.wait_ge(s_ld, 96)
                tensor.transpose(ps_tr[0:64, 0:4], xg2[0:4, 0:64],
                                 identf_s[0:4, 0:4]).then_inc(s_ptt, 1)
                tensor.wait_ge(s_psk, 8)
                for h in range(HPC):
                    if h > 0:
                        tensor.wait_ge(s_psk, 8 + 1 + h)
                    for k in range(KC):
                        tensor.matmul(ps_q[0:64, 0:U], wchunk(wk_s, k, h * 64, 64),
                                      xsampT[:, k * U:(k + 1) * U], start=(k == 0), stop=False)
                    tensor.matmul(ps_q[0:64, 0:U], bkrow[0:1, h * 64:(h + 1) * 64],
                                  ones_row[0:1, 0:U], start=False, stop=True).then_inc(s_ptk, 1)
                tensor.wait_ge(s_pskb, 1)
                for h in range(HPC):
                    if h > 0:
                        tensor.wait_ge(s_psk, 12 + 1 + h)
                    tensor.matmul(ps_q[0:1, 0:51], bqcol[:, h:h + 1],
                                  kst[h][:, 0:51], start=True,
                                  stop=True).then_inc(s_ptd, 1)

            @block.scalar
            def _(scalar):
                for wi, fx in enumerate((fixq, fixk, fixv)):
                    scalar.wait_ge(s_ptc, 1 + wi)
                    scalar.activation(fx[0:1, :], ps_v[0:1, 0:MSL], AF.Copy, scale=-1.0).then_inc(s_ps, 1)
                for k in range(KC):
                    scalar.wait_ge(s_ptt, 1 + k)
                    scalar.copy(xsampT[:, k * U:(k + 1) * U], ps_trb[:, 0:U]).then_inc(s_psk, 1)
                scalar.wait_ge(s_ptt, 9)
                scalar.wait_ge(s_pv, 1)
                scalar.copy(bqcol[0:64, 0:HPC].rearrange("p (a c) -> p a c", c=2)[:, :, 0:1],
                            ps_tr[0:64, 0:4].rearrange("p (a c) -> p a c", c=2)[:, :, 0:1])
                scalar.copy(bqcol[64:128, 0:HPC].rearrange("p (a c) -> p a c", c=2)[:, :, 1:2],
                            ps_tr[0:64, 0:4].rearrange("p (a c) -> p a c", c=2)[:, :, 1:2]).then_inc(s_psk, 1)
                for h in range(HPC):
                    odd = h % 2
                    scalar.wait_ge(s_ptk, 1 + h)
                    scalar.copy(kst[h][odd * 64:odd * 64 + 64, 0:50], ps_q[0:64, 0:U]).then_inc(s_psk, 1)
                for h in range(HPC):
                    scalar.wait_ge(s_ptd, 1 + h)
                    scalar.copy(drows[0:1, h * 52:h * 52 + 51], ps_q[0:1, 0:51]).then_inc(s_psk, 1)

            @block.vector
            def _(vector):
                vector.wait_ge(s_psk, 13)
                KB0, nkb = 2, 0
                for h in range(HPC):
                    rows = slice((h % 2) * 64, (h % 2) * 64 + 64)
                    vector.tensor_reduce(kbar_f[rows, 0:1], kst[h][rows, 0:50], AX.X, ALU.add).then_inc(ch_v, 1)
                    nkb += 1
                    vector.wait_ge(ch_v, KB0 + nkb)
                    ts = vector.tensor_scalar(kst[h][rows, 50:51], kbar_f[rows, 0:1], 1.0 / U, None, ALU.mult)
                ts.then_inc(s_pskb, 1)

        if upto >= 1:
            with nc.sbuf_tensor("sqtmp", [128, 2 * TT], BF16) as sqtmp:

                @block.gpsimd
                def _(gpsimd):
                    for j in range(4):
                        if j >= 2:
                            gpsimd.wait_ge(s_1t, 8 * (j - 1))
                        gpsimd.dma_start(out=xbf[j % 2][:, :].rearrange("p (k n) -> p k n", n=TT),
                                         in_=xTsl.rearrange("(k p) n -> p k n", p=128)[:, :, j * TT:(j + 1) * TT]
                                         ).then_inc(s_xl[j % 2], 16)

                @block.vector
                def _(vector):
                    for pc in range(32):
                        n, k = pc // KC, pc % KC
                        if k == 0:
                            vector.wait_ge(s_xl[n % 2], 16 * (n // 2 + 1))
                        if pc >= 2:
                            vector.wait_ge(s_1t, pc - 1)
                        xv = xbf[n % 2][:, k * TT:(k + 1) * TT]
                        vector.tensor_tensor(sqtmp[:, (pc % 2) * TT:(pc % 2 + 1) * TT],
                                             xv, xv, ALU.mult).then_inc(s_1v, 1)

                @block.tensor
                def _(tensor):
                    tensor.wait_ge(s_psk, 17)
                    for pc in range(32):
                        n, k = pc // KC, pc % KC
                        if k == 0 and n > 0:
                            tensor.wait_ge(s_1s, n)
                        tensor.wait_ge(s_1v, pc + 1)
                        tensor.matmul(ps_q[0:1, 0:TT], onesc[:, :], xbf[n % 2][:, k * TT:(k + 1) * TT],
                                      start=(k == 0), stop=(k == KC - 1)).then_inc(s_1q, 1)
                        tensor.matmul(ps_k[0:1, 0:TT], onesc[:, :],
                                      sqtmp[:, (pc % 2) * TT:(pc % 2 + 1) * TT],
                                      start=(k == 0), stop=(k == KC - 1)).then_inc(s_1t, 1)

                @block.scalar
                def _(scalar):
                    for n in range(4):
                        scalar.wait_ge(s_1t, (n + 1) * KC)
                        scalar.wait_ge(s_1q, (n + 1) * KC)
                        scalar.copy(stat_sum[0:1, n * TT:(n + 1) * TT], ps_q[0:1, 0:TT])
                        scalar.copy(stat_sq[0:1, n * TT:(n + 1) * TT], ps_k[0:1, 0:TT]).then_inc(s_1s, 1)

                @block.gpsimd
                def _(gpsimd):
                    gpsimd.wait_ge(s_1s, 4)
                    gpsimd.dma_start(out=e0in[0:1, :], in_=stat_sum[:, :]).then_inc(s_1g, 16)
                    gpsimd.dma_start(out=e0in[1:2, :], in_=stat_sq[:, :]).then_inc(s_1g, 16)
                    gpsimd.wait_ge(s_1g, 32)
                    gpsimd.collective_compute(
                        "AllGather", ALU.bypass, replica_groups=GROUPS,
                        ins=[e0in.ap().opt()], outs=[e0out.ap().opt()],
                    ).then_inc(s_cc, 1)
                    gpsimd.wait_ge(s_cc, 1)
                    for r in range(4):
                        gpsimd.dma_start(out=sums_t[16 * r:16 * (r + 1), :],
                                         in_=e0out[2 * r:2 * r + 1, :].rearrange("o (a p) -> (o a) p", p=128)
                                         ).then_inc(s_1g, 16)
                        gpsimd.dma_start(out=sumsq_t[16 * r:16 * (r + 1), :],
                                         in_=e0out[2 * r + 1:2 * r + 2, :].rearrange("o (a p) -> (o a) p", p=128)
                                         ).then_inc(s_1g, 16)
                    gpsimd.wait_ge(s_1g, 160)

                @block.vector
                def _(vector):
                    vector.wait_ge(s_1g, 160)
                    vector.tensor_scalar(mu_t[:, :], sums_t[:, :], 1.0 / D, None, ALU.mult).then_inc(ch_v, 1)
                    vector.tensor_scalar(var_t[:, :], sumsq_t[:, :], 1.0 / D, None, ALU.mult).then_inc(ch_v, 1)
                    vector.wait_ge(ch_v, 8)
                    vector.tensor_tensor(sums_t[:, :], mu_t[:, :], mu_t[:, :], ALU.mult).then_inc(ch_v, 1)
                    vector.wait_ge(ch_v, 9)
                    vector.tensor_tensor(var_t[:, :], var_t[:, :], sums_t[:, :],
                                         ALU.subtract).then_inc(s_1v, 1)

                @block.scalar
                def _(scalar):
                    scalar.wait_ge(s_1v, 33)
                    scalar.activation(r_t[:, :], var_t[:, :], AF.Ln, bias=epscol[0:64, 0:1]).then_inc(ch_s, 1)
                    scalar.wait_ge(ch_s, 2)
                    scalar.activation(r_t[:, :], r_t[:, :], AF.Exp, scale=-0.5).then_inc(s_1s, 1)

                @block.vector
                def _(vector):
                    vector.wait_ge(s_1s, 5)
                    vector.reciprocal(rrec_t[:, :], r_t[:, :]).then_inc(s_1v, 1)

                @block.tensor
                def _(tensor):
                    tensor.wait_ge(s_1s, 5)
                    tensor.transpose(ps_v[0:128, 0:64], r_t[:, :],
                                     identf_s[0:64, 0:64]).then_inc(s_1r, 1)

                @block.scalar
                def _(scalar):
                    scalar.wait_ge(s_1r, 1)
                    scalar.copy(r_ml[:, :], ps_v[0:128, 0:64])
                    scalar.activation(r8_ml[:, :], ps_v[0:128, 0:64], AF.Copy, scale=SCALE).then_inc(s_1s, 1)

                @block.gpsimd
                def _(gpsimd):
                    gpsimd.wait_ge(s_1v, 34)
                    gpsimd.dma_start(out=murow_d.rearrange("(a p) -> a p", p=128), in_=mu_t[:, :]).then_inc(s_1g, 16)
                    gpsimd.dma_start(out=rrec_d.rearrange("(a p) -> a p", p=128), in_=rrec_t[:, :]).then_inc(s_1g, 16)
                    gpsimd.wait_ge(s_1g, 192)
                    gpsimd.dma_start(out=stat2[0:1, :], in_=murow_d.rearrange("(o n) -> o n", o=1)).then_inc(s_1g, 16)
                    gpsimd.dma_start(out=rrec_row[:, :], in_=rrec_d.rearrange("(o n) -> o n", o=1)).then_inc(s_1g, 16)
                    gpsimd.wait_ge(s_1g, 224)
                    gpsimd.wait_ge(s_1s, 6)
                    gpsimd.engine_nop().then_inc(s_e0, 1)

        if upto >= 2:
            with nc.psum_tensor("ps_sc", [128, 1024], F32) as ps_sc:

                @block.gpsimd
                def _(gpsimd):
                    for t in range(NT):
                        if t == 0:
                            gpsimd.wait_ge(s_1t, 24)
                        elif t == 1:
                            gpsimd.wait_ge(s_1t, 32)
                        else:
                            gpsimd.wait_ge(s_vd, t - 1)
                        gpsimd.dma_start(out=xbf[t % 2][:, :].rearrange("p (k n) -> p k n", n=TT),
                                         in_=xT.rearrange("(k p) n -> p k n", p=128)[:, :, t * TT:(t + 1) * TT]
                                         ).then_inc(s_xl[t % 2], 16)

                @block.tensor
                def _(tensor):
                    tensor.wait_ge(s_e0, 1)
                    tensor.wait_ge(s_psk, 17)
                    for t in range(NT):
                        xt = xbf[t % 2][:, :].rearrange("p (k n) -> p k n", n=TT)
                        cs = slice(t * TT, (t + 1) * TT)
                        tensor.wait_ge(s_xl[t % 2], 16 * (t // 2 + 3))
                        if t >= 1:
                            tensor.wait_ge(s_evv, t)
                        for m in range(2):
                            for k in range(KC):
                                tensor.matmul(ps_k[:, m * TT:(m + 1) * TT],
                                              wchunk(wk_s, k, m * 128, 128), xt[:, k, :],
                                              start=(k == 0), stop=False)
                            mmk = tensor.matmul(ps_k[:, m * TT:(m + 1) * TT],
                                                fixk[0:1, m * 128:(m + 1) * 128], stat2[0:1, cs],
                                                start=False, stop=True)
                        mmk.then_inc(s_kd, 1)
                        if t >= 1:
                            tensor.wait_ge(s_evq, t)
                        for m in range(2):
                            for k in range(KC):
                                tensor.matmul(ps_q[:, m * TT:(m + 1) * TT],
                                              wchunk(wq_s, k, m * 128, 128), xt[:, k, :],
                                              start=(k == 0), stop=False)
                            mmq = tensor.matmul(ps_q[:, m * TT:(m + 1) * TT],
                                                fixq[0:1, m * 128:(m + 1) * 128], stat2[0:1, cs],
                                                start=False, stop=True)
                        mmq.then_inc(s_qd, 1)
                        tensor.wait_ge(s_evk, t + 1)
                        for s in range(4):
                            for k in range(KC):
                                tensor.matmul(ps_k[:, s * 256:(s + 1) * 256],
                                              xt[:, k, s * 128:(s + 1) * 128], wchunk(wv_s, k),
                                              start=(k == 0), stop=False)
                            mmv = tensor.matmul(ps_k[:, s * 256:(s + 1) * 256],
                                                stat2[0:1, t * TT + s * 128:t * TT + (s + 1) * 128],
                                                fixv[0:1, :], start=False, stop=True)
                        mmv.then_inc(s_vd, 1)
                        if p2mode == 'kqv':
                            tensor.nop().then_inc(s_scp, 1)
                            continue
                        tensor.wait_ge(s_evq, t + 1)
                        if t >= 2 and p2mode == 'full':
                            tensor.wait_ge(s_mst, t - 1)
                        psS = ps_sc if t % 2 == 0 else ps_v
                        qslot = (t % 2) * 2 * TT
                        for s in range(4):
                            g, o = s // 2, (s % 2) * 208
                            if p2mode != 'scores_donly':
                                for h in range(HPC):
                                    hp = h // 2
                                    mms = tensor.matmul(
                                        psS[:, g * 512 + o + h * 52:g * 512 + o + h * 52 + 51],
                                        qt_tmp[:, qslot + hp * TT + s * 128:qslot + hp * TT + (s + 1) * 128],
                                        kst[h][:, :],
                                        start=(h == 0), stop=(p2mode in ('scores_nod', 'scores_even') and h == HPC - 1))
                            if p2mode not in ('scores_nod', 'scores_even'):
                                for h in range(HPC):
                                    mms = tensor.matmul(
                                        psS[:, g * 512 + o + h * 52:g * 512 + o + h * 52 + 51],
                                        rrec_row[0:1, t * TT + s * 128:t * TT + (s + 1) * 128],
                                        drows[0:1, h * 52:h * 52 + 51],
                                        start=(p2mode == 'scores_donly' and h == 0), stop=(h == HPC - 1))
                        mms.then_inc(s_scp, 1)

                @block.scalar
                def _(scalar):
                    scalar.wait_ge(s_e0, 1)
                    for t in range(NT):
                        cs = slice(t * TT, (t + 1) * TT)
                        scalar.wait_ge(s_kd, t + 1)
                        scalar.copy(kT[0][:, cs], ps_k[:, 0:TT])
                        scalar.copy(kT[1][:, cs], ps_k[:, TT:2 * TT]).then_inc(s_evk, 1)
                        scalar.wait_ge(s_qd, t + 1)
                        if t >= 2:
                            scalar.wait_ge(s_scp, t - 1)
                        qslot = (t % 2) * 2 * TT
                        scalar.copy(qt_tmp[:, qslot:qslot + TT], ps_q[:, 0:TT])
                        scalar.copy(qt_tmp[:, qslot + TT:qslot + 2 * TT], ps_q[:, TT:2 * TT]).then_inc(s_evq, 1)
                        scalar.wait_ge(s_vd, t + 1)
                        for s in range(4):
                            c = t * 4 + s
                            av = scalar.activation(
                                vs_all[:, :].rearrange("p (r c a y) -> p r c a y", r=2, c=64, y=65)[:, :, c, :, 0:64],
                                ps_k[:, s * 256:(s + 1) * 256].rearrange("p (r a d) -> p r a d", r=2, d=64),
                                AF.Copy, scale=r_ml[:, c:c + 1])
                        av.then_inc(s_evv, 1)

                @block.vector
                def _(vector):
                    for t in range(NT):
                        if p2mode in ('kqv', 'scores'):
                            break
                        vector.wait_ge(s_scp, t + 1)
                        if t >= 1:
                            vector.wait_ge(s_mst, t)
                        psS = ps_sc if t % 2 == 0 else ps_v
                        for g in range(2):
                            vector.tensor_reduce(
                                scmax_v[:, g * 8:(g + 1) * 8].rearrange("p (a h) -> p a h", h=4),
                                psS[:, g * 512:g * 512 + 416].rearrange("p (a h u) -> p a h u", a=2, u=52)[:, :, :, 0:50],
                                AX.X, ALU.max).then_inc(s_mv, 1)
                            vector.wait_ge(s_mv, 4 * t + 2 * g + 1)
                            vector.tensor_tensor(
                                scmax_v[:, g * 8:(g + 1) * 8].rearrange("p (a h) -> p a h", h=4),
                                scmax_v[:, g * 8:(g + 1) * 8].rearrange("p (a h) -> p a h", h=4),
                                psS[:, g * 512:g * 512 + 416].rearrange("p (a h u) -> p a h u", a=2, u=52)[:, :, :, 50:51],
                                ALU.subtract).then_inc(s_mv, 1)
                        vector.wait_ge(s_mv, 4 * t + 4)
                        for s in range(4):
                            c = t * 4 + s
                            tsv = vector.tensor_scalar(
                                m_buf[:, :].rearrange("p (h c) -> p h c", c=64)[:, :, c:c + 1],
                                scmax_v[:, :].rearrange("p (a h) -> p a h", h=4)[:, s:s + 1, :].rearrange(
                                    "p a h -> p h a"),
                                r_ml[:, c:c + 1], None, ALU.mult)
                        tsv.then_inc(s_mst, 1)

        esA.close()

        x2T = sb("x2T", [128, KC * TSL], F32)
        onesf = sb("onesf", [1, 128], F32)
        esB = contextlib.ExitStack()
        sbB = lambda n, s, d: esB.enter_context(nc.sbuf_tensor(n, s, d))
        iota_sl_s = sbB("iota_sl_s", [128, TSL], F32)
        wo_b = sbB("wo_b", [128, 2 * D], F32)
        idxrows = sbB("idxrows", [1, 224], F32)
        idxu = sbB("idxu", [U, HPC], U32)
        vme = sbB("vme", [1, MSL], F32)
        vm8 = sbB("vm8", [8, 128], F32)
        vmT = sbB("vmT", [128, 8], F32)
        vmb = sbB("vmb", [128, 8], BF16)
        bc8 = sbB("bc8", [8, 128], F32)
        basecol = sbB("basecol", [128, 8], F32)
        baserow = sbB("baserow", [1, D], F32)
        sel32 = sbB("sel32", [128, HPC * 64], F32)
        rankb = sbB("rankb", [128, HPC * 64], F32)
        lo4 = sbB("lo4", [1, 4], F32)
        hi4 = sbB("hi4", [1, 4], F32)
        tau4 = sbB("tau4", [1, 4], F32)
        c4 = sbB("c4", [1, 4], F32)
        g4 = sbB("g4", [1, 4], F32)
        tmp4 = sbB("tmp4", [1, 4], F32)
        tau_rep = sbB("tau_rep", [128, 4], F32)
        maskb = sbB("maskb", [128, HPC * 64], BF16)
        rcomp = [sbB("rcomp%d" % i, [128, HPC * U], F32) for i in range(2)]
        tot4 = sbB("tot4", [128, 4], F32)
        totT = sbB("totT", [4, 128], F32)
        scanT = sbB("scanT", [4, 128], F32)
        zer4 = sbB("zer4", [4, 128], F32)
        offs4 = sbB("offs4", [128, 4], F32)
        woH = sbB("woH", [128, 2 * D], BF16)
        bqrow4 = sbB("bqrow4", [1, MSL], BF16)
        xtopT4 = sbB("xtopT4", [128, HPC * KC * U], BF16)
        qpair = [sbB("qpair0", [128, 2 * U], BF16), sbB("qpair1", [128, 2 * U], BF16)]
        expT = sbB("expT", [128, 4 * HPC * U], BF16)
        cc = sbB("cc", [65, 4 * U], F32)
        ccrec = sbB("ccrec", [1, 4 * U], F32)
        ccn = sbB("ccn", [64, 4 * U], F32)
        ccp = [sbB("ccp0", [128, U], BF16), sbB("ccp1", [128, U], BF16)]
        vmcl = sbB("vmcl", [128, 2], F32)
        vmcr = sbB("vmcr", [1, MSL], F32)
        dstage = [sbB("dst0", [U, D], BF16), sbB("dst1", [U, D], BF16)]
        idx7r = sbB("idx7r", [7, 128], F32)
        idxT7 = sbB("idxT7", [128, 7], F32)

        if upto >= 3:
            ps_tr2 = ps_sc
            cps = [ps_q[0:1, 0:512], ps_k[0:1, 0:512], ps_v[0:1, 0:512], ps_v[0:1, 512:1024]]

            @block.gpsimd
            def _(gpsimd):
                gpsimd.wait_ge(s_evv, NT)
                gpsimd.wait_ge(s_mst, NT)
                gpsimd.wait_ge(s_scp, NT)
                gpsimd.dma_start(out=x2T[:, :].rearrange("p (k n) -> p k n", n=TSL),
                                 in_=xTsl.rearrange("(k p) n -> p k n", p=128)).then_inc(s_g2, 16)
                gpsimd.dma_start(out=iota_sl_s[:, :], in_=iota_sl_in).then_inc(s_g2, 16)

            @block.tensor
            def _(tensor):
                tensor.wait_ge(s_evv, NT)
                tensor.wait_ge(s_mst, NT)
                for r in range(2):
                    for c in range(64):
                        mm = tensor.matmul(ps_tr2[0:1, r * 512:r * 512 + 130], onesc[:, :],
                                           vs_all[:, :].rearrange("p (q x) -> p q x", q=2)[:, r,
                                                                                           c * 130:(c + 1) * 130],
                                           start=(c == 0), stop=(c == 63))
                    mm.then_inc(s_vm, 1)

            @block.vector
            def _(vector):
                vector.wait_ge(s_vm, 2)
                for h in range(HPC):
                    r, odd = h // 2, h % 2
                    vector.tensor_scalar(vme[0:1, h * 64:(h + 1) * 64],
                                         ps_tr2[0:1, r * 512 + odd * 65:r * 512 + odd * 65 + 64],
                                         1.0 / L, None, ALU.mult).then_inc(ch_v, 1)
                vector.wait_ge(ch_v, 13)
                vector.tensor_tensor(vme[0:1, :], vme[0:1, :], bvrow[0:1, :], ALU.add).then_inc(s_vms, 1)
                vector.wait_ge(s_vms, 1)
                vector.tensor_tensor(vmcr[0:1, :], vme[0:1, :], bvrow[0:1, :], ALU.subtract).then_inc(s_4vc, 1)

            @block.gpsimd
            def _(gpsimd):
                gpsimd.wait_ge(s_vms, 1)
                gpsimd.dma_start(out=e1ain[:, :], in_=vme[:, :]).then_inc(s_e1a, 16)
                gpsimd.wait_ge(s_e1a, 16)
                gpsimd.collective_compute(
                    "AllGather", ALU.bypass, replica_groups=GROUPS,
                    ins=[e1ain.ap().opt()], outs=[e1aout.ap().opt()],
                ).then_inc(s_cc, 1)
                gpsimd.wait_ge(s_cc, 2)
                gpsimd.dma_start(out=vm8[:, :],
                                 in_=e1aout.rearrange("r (a p) -> (r a) p", p=128)).then_inc(s_vm8, 16)
                gpsimd.wait_ge(s_vm8, 16)

            @block.tensor
            def _(tensor):
                tensor.wait_ge(s_4vc, 1)
                for a in range(2):
                    tensor.transpose(ps_tr2[0:128, 360 + a:361 + a], vmcr[0:1, a * 128:(a + 1) * 128],
                                     identf_s[0:1, 0:1]).then_inc(s_4vc, 1)

            @block.scalar
            def _(scalar):
                for a in range(2):
                    scalar.wait_ge(s_4vc, 2 + a)
                    scalar.copy(vmcl[:, a:a + 1], ps_tr2[0:128, 360 + a:361 + a]).then_inc(s_4vs, 1)

            @block.vector
            def _(vector):
                vector.wait_ge(s_mst, NT)
                vector.memset(onesf[:, :], 1.0)
                vector.memset(lo4[:, :], -1.0)
                vector.memset(tau4[:, :], 31.5).then_inc(s_b0, 1)

            @block.tensor
            def _(tensor):
                tensor.wait_ge(s_vms, 1)
                tensor.wait_ge(s_4vs, 2)
                for i in range(NBIS + 1):
                    tensor.wait_ge(s_b0, i + 1)
                    src = tau4 if i < NBIS else lo4
                    tensor.matmul(ps_tr2[0:128, 216:220], onesf[0:1, :], src[0:1, :],
                                  start=True, stop=True).then_inc(s_b1, 1)
                    if i < NBIS:
                        tensor.wait_ge(s_b3, i + 1)
                        tensor.matmul(ps_tr2[0:1, 644:900], onesc[:, :], maskb[:, :],
                                      start=True, stop=True).then_inc(s_b4, 1)

            @block.scalar
            def _(scalar):
                for i in range(NBIS + 1):
                    scalar.wait_ge(s_b1, i + 1)
                    scalar.copy(tau_rep[:, :], ps_tr2[0:128, 216:220]).then_inc(s_b2, 1)

            @block.vector
            def _(vector):
                CH0 = 13
                for i in range(NBIS):
                    vector.wait_ge(s_b2, i + 1)
                    for h in range(HPC):
                        mv = vector.tensor_scalar(maskb[:, h * 64:(h + 1) * 64],
                                                  m_buf[:, h * 64:(h + 1) * 64],
                                                  tau_rep[:, h:h + 1], None, ALU.is_gt)
                    mv.then_inc(s_b3, 1)
                    nch = CH0 + 4 * i
                    w = 65.0 / (2.0 ** i)
                    vector.wait_ge(s_b4, i + 1)
                    vector.tensor_reduce(c4[0:1, :],
                                         ps_tr2[0:1, 644:900].rearrange("o (h c) -> o h c", c=64),
                                         AX.X, ALU.add).then_inc(ch_v, 1)
                    vector.wait_ge(ch_v, nch + 1)
                    vector.tensor_scalar(g4[0:1, :], c4[0:1, :], 50.0, None, ALU.is_ge).then_inc(ch_v, 1)
                    vector.wait_ge(ch_v, nch + 2)
                    vector.tensor_scalar(tmp4[0:1, :], g4[0:1, :], w * 0.5, None, ALU.mult).then_inc(ch_v, 1)
                    vector.wait_ge(ch_v, nch + 3)
                    vector.tensor_tensor(lo4[0:1, :], lo4[0:1, :], tmp4[0:1, :], ALU.add).then_inc(ch_v, 1)
                    vector.wait_ge(ch_v, nch + 4)
                    vector.tensor_scalar(tau4[0:1, :], lo4[0:1, :], w * 0.25, None, ALU.add).then_inc(s_b0, 1)

            CHR = 14 + 4 * NBIS

            @block.tensor
            def _(tensor):
                tensor.wait_ge(s_vm8, 16)
                tensor.transpose(ps_tr2[0:128, 200:208], vm8[:, :], identf_s[0:8, 0:8]).then_inc(s_bs, 1)

            @block.scalar
            def _(scalar):
                scalar.wait_ge(s_bs, 1)
                scalar.copy(vmT[:, 0:8], ps_tr2[0:128, 200:208])
                scalar.copy(vmb[:, 0:8], ps_tr2[0:128, 200:208]).then_inc(s_bs, 1)

            @block.sync
            def _(sync):
                sync.wait_ge(s_scp, NT)
                sync.wait_ge(s_mst, NT)
                sync.wait_ge(s_evv, NT)
                for k in range(KC):
                    if k >= 2:
                        sync.wait_ge(s_wo2, k - 1)
                    sync.dma_start(out=wo_b[:, (k % 2) * D:(k % 2 + 1) * D],
                                   in_=woT_in[k * 128:(k + 1) * 128, :]).then_inc(s_wo, 16)

            @block.tensor
            def _(tensor):
                tensor.wait_ge(s_bs, 2)
                for k in range(KC):
                    tensor.wait_ge(s_wo, 32 * (k // 2 + 1))
                    tensor.matmul(cps[3][0:1, 0:TT], vmT[:, k:k + 1], wo_b[:, (k % 2) * D:(k % 2) * D + TT],
                                  start=(k == 0), stop=(k == KC - 1))
                    tensor.matmul(cps[2][0:1, 0:TT], vmT[:, k:k + 1], wo_b[:, (k % 2) * D + TT:(k % 2 + 1) * D],
                                  start=(k == 0), stop=(k == KC - 1)).then_inc(s_wo2, 1)
                tensor.nop().then_inc(s_bs, 2)

            @block.vector
            def _(vector):
                vector.wait_ge(s_bs, 4)
                vector.wait_ge(s_wo2, 8)
                vector.tensor_tensor(baserow[0:1, 0:TT], cps[3][0:1, 0:TT], borow[0:1, 0:TT],
                                     ALU.add).then_inc(ch_v, 1)
                vector.wait_ge(ch_v, 14 + 4 * NBIS)
                vector.tensor_tensor(baserow[0:1, TT:D], cps[2][0:1, 0:TT], borow[0:1, TT:D],
                                     ALU.add).then_inc(s_bs, 1)

            @block.gpsimd
            def _(gpsimd):
                gpsimd.wait_ge(s_bs, 5)
                gpsimd.dma_start(out=base_d.rearrange("(o n) -> o n", o=1), in_=baserow[:, :]).then_inc(s_bd, 16)
                gpsimd.wait_ge(s_bd, 16)
                gpsimd.dma_start(out=bc8[:, :], in_=base_d.rearrange("(a p) -> a p", p=128)).then_inc(s_bd, 16)
                gpsimd.wait_ge(s_bd, 32)

            @block.tensor
            def _(tensor):
                tensor.wait_ge(s_bd, 32)
                tensor.transpose(ps_tr2[0:128, 208:216], bc8[:, :], identf_s[0:8, 0:8]).then_inc(s_bs, 1)

            @block.scalar
            def _(scalar):
                scalar.wait_ge(s_bs, 6)
                scalar.copy(basecol[:, 0:8], ps_tr2[0:128, 208:216]).then_inc(s_bs, 1)

            @block.vector
            def _(vector):
                vector.wait_ge(s_bs, 7)
                vector.wait_ge(s_g2, 32)
                for k in range(KC):
                    tv = vector.tensor_scalar(x2T[:, k * TSL:(k + 1) * TSL], x2T[:, k * TSL:(k + 1) * TSL],
                                              basecol[:, k:k + 1], None, ALU.add)
                tv.then_inc(s_bs, 1)

            @block.vector
            def _(vector):
                vector.wait_ge(s_b2, NBIS + 1)
                for h in range(HPC):
                    sv = vector.tensor_scalar(sel32[:, h * 64:(h + 1) * 64], m_buf[:, h * 64:(h + 1) * 64],
                                              tau_rep[:, h:h + 1], None, ALU.is_gt)
                sv.then_inc(ch_v, 1)
                vector.wait_ge(ch_v, CHR + 1)
                vector.tensor_copy(rankb[:, :], sel32[:, :]).then_inc(ch_v, 1)
                nch = CHR + 2
                for d in (1, 2, 4, 8, 16, 32):
                    vector.wait_ge(ch_v, nch)
                    vector.tensor_tensor(
                        rankb[:, :].rearrange("p (h c) -> p h c", c=64)[:, :, d:64],
                        rankb[:, :].rearrange("p (h c) -> p h c", c=64)[:, :, d:64],
                        rankb[:, :].rearrange("p (h c) -> p h c", c=64)[:, :, 0:64 - d],
                        ALU.add).then_inc(ch_v, 1)
                    nch += 1
                vector.wait_ge(ch_v, CHR + 8)
                vector.tensor_copy(tot4[:, :],
                                   rankb[:, :].rearrange("p (h c) -> p h c", c=64)[:, :, 63]).then_inc(s_r0, 1)

            @block.tensor
            def _(tensor):
                tensor.wait_ge(s_r0, 1)
                tensor.transpose(ps_tr2[0:4, 220:348], tot4[:, :], identf_s[:, :]).then_inc(s_r1, 1)

            @block.scalar
            def _(scalar):
                scalar.wait_ge(s_r1, 1)
                scalar.copy(totT[0:4, :], ps_tr2[0:4, 220:348]).then_inc(s_r2, 1)

            @block.vector
            def _(vector):
                vector.wait_ge(s_r2, 1)
                vector.memset(zer4[0:4, :], 0.0).then_inc(ch_v, 1)
                vector.wait_ge(ch_v, CHR + 9)
                vector.tensor_tensor_scan(scanT[0:4, :], totT[0:4, :], zer4[0:4, :], 0.0,
                                          ALU.add, ALU.add).then_inc(ch_v, 1)
                vector.wait_ge(ch_v, CHR + 10)
                vector.tensor_tensor(scanT[0:4, :], scanT[0:4, :], totT[0:4, :],
                                     ALU.subtract).then_inc(s_r0, 1)

            @block.tensor
            def _(tensor):
                tensor.wait_ge(s_r0, 2)
                tensor.transpose(ps_tr2[0:128, 348:352], scanT[0:4, :], identf_s[0:4, 0:4]).then_inc(s_r1, 1)

            @block.scalar
            def _(scalar):
                scalar.wait_ge(s_r1, 2)
                scalar.copy(offs4[:, :], ps_tr2[0:128, 348:352]).then_inc(s_r2, 1)

            @block.vector
            def _(vector):
                vector.wait_ge(s_r2, 2)
                for h in range(HPC):
                    rv = vector.tensor_scalar(rankb[:, h * 64:(h + 1) * 64], rankb[:, h * 64:(h + 1) * 64],
                                              offs4[:, h:h + 1], None, ALU.add)
                rv.then_inc(ch_v, 1)
                vector.wait_ge(ch_v, CHR + 11)
                vector.tensor_tensor(rankb[:, :], rankb[:, :], sel32[:, :], ALU.subtract).then_inc(ch_v, 1)
                vector.wait_ge(ch_v, CHR + 12)
                vector.tensor_scalar(rankb[:, :], rankb[:, :], BIG, None, ALU.add).then_inc(ch_v, 1)
                vector.wait_ge(ch_v, CHR + 13)
                vector.scalar_tensor_tensor(rankb[:, :], sel32[:, :], -BIG, rankb[:, :],
                                            ALU.mult, ALU.add).then_inc(ch_v, 1)
                vector.wait_ge(ch_v, CHR + 14)
                mvm = None
                for h in range(HPC):
                    mvm = vector.tensor_tensor(sel32[:, h * 64:(h + 1) * 64], sel32[:, h * 64:(h + 1) * 64],
                                               iota_ml_s[:, :], ALU.mult)
                mvm.then_inc(s_r0, 1)

            @block.vector
            def _(vector):
                vector.wait_ge(s_r0, 3)
                for c in range(64):
                    if c >= 2:
                        vector.wait_ge(s_cmt, (c - 1) * HPC)
                    rv = None
                    for h in range(HPC):
                        rv = vector.tensor_scalar(
                            rcomp[c % 2][:, h * U:(h + 1) * U], iota_j_s[:, 0:U],
                            rankb[:, h * 64 + c:h * 64 + c + 1], None, ALU.is_equal)
                    rv.then_inc(s_cmv, 1)

            @block.tensor
            def _(tensor):
                tensor.wait_ge(s_bs, 5)
                for c in range(64):
                    tensor.wait_ge(s_cmv, c + 1)
                    for h in range(HPC):
                        tensor.matmul(cps[h][0:1, 0:U], sel32[:, h * 64 + c:h * 64 + c + 1],
                                      rcomp[c % 2][:, h * U:(h + 1) * U],
                                      start=(c == 0), stop=(c == 63)).then_inc(s_cmt, 1)

            @block.vector
            def _(vector):
                vector.memset(idxrows[:, :], BIG).then_inc(s_cmv, 1)

            @block.scalar
            def _(scalar):
                scalar.wait_ge(s_cmt, 64 * HPC)
                scalar.wait_ge(s_cmv, 65)
                sc = None
                for h in range(HPC):
                    sc = scalar.copy(idxrows[0:1, h * U:(h + 1) * U], cps[h][0:1, 0:U])
                sc.then_inc(s_cms, 1)

            @block.tensor
            def _(tensor):
                tensor.wait_ge(s_cms, 1)
                for h in range(HPC):
                    tensor.transpose(ps_tr2[0:U, 352 + h * 2:352 + h * 2 + 1],
                                     idxrows[0:1, h * U:(h + 1) * U],
                                     identf_s[0:1, 0:1]).then_inc(s_r1, 1)

            @block.scalar
            def _(scalar):
                sc = None
                for h in range(HPC):
                    scalar.wait_ge(s_r1, 3 + h)
                    sc = scalar.copy(idxu[:, h:h + 1], ps_tr2[0:U, 352 + h * 2:352 + h * 2 + 1])
                sc.then_inc(s_cms, 1)

        if upto >= 4:
            def pslot(j):
                t = ps_q if j < 2 else ps_k
                return t[:, (j % 2) * 512:(j % 2) * 512 + 200]

            @block.gpsimd
            def _(gpsimd):
                gpsimd.wait_ge(s_cms, 2)
                gpsimd.dma_start(out=bqrow4[0:1, :], in_=bq_in).then_inc(s_4w, 16)
                gpsimd.dma_start(out=woH[:, :].rearrange("p (r d) -> p r d", d=D),
                                 in_=woR_in.rearrange("(r p) d -> p r d", p=128)).then_inc(s_4w2, 16)
                for h in range(HPC):
                    if h > 0:
                        gpsimd.wait_ge(s_4tt, h * KC)
                    gpsimd.indirect_dma_start(
                        out=xg[0:U, :], out_offset=None, in_=xR,
                        in_offset=bass.IndirectOffsetOnAxis(idxu[:, h:h + 1], 0),
                    ).then_inc(s_4g, 16)

            @block.vector
            def _(vector):
                vector.memset(qpair[0][:, :], 0.0)
                vector.memset(qpair[1][:, :], 0.0)
                vector.memset(ccp[0][:, :], 0.0)
                vector.memset(ccp[1][:, :], 0.0).then_inc(s_4v, 1)
                for h in range(HPC):
                    vector.wait_ge(s_4g, 16 * (h + 1))
                    vector.bn_stats(bn6[0:U, 0:6], xg[0:U, 0:512])
                    vector.bn_stats(bn6[0:U, 6:12], xg[0:U, 512:1024]).then_inc(s_4v, 1)
                    vector.wait_ge(s_4v, 3 * h + 2)
                    vector.bn_aggr(bnag[0:U, :], bn6[0:U, :]).then_inc(s_4v, 1)
                    vector.wait_ge(s_4a, 2 * h + 2)
                    vector.tensor_scalar(xg[0:U, :], xg[0:U, :], bnag[0:U, 0:1],
                                         bnag[0:U, 1:2], ALU.subtract, ALU.mult).then_inc(s_4v, 1)

            @block.tensor
            def _(tensor):
                tensor.wait_ge(s_cms, 1)
                tensor.wait_ge(s_4w, 16)
                for h in range(HPC):
                    tensor.wait_ge(s_4v, 3 * h + 4)
                    for k in range(KC):
                        if h * KC + k > 0:
                            tensor.wait_ge(s_4sk, h * KC + k)
                        tensor.transpose(ps_q[0:128, 0:U], xg[0:U, k * 128:(k + 1) * 128],
                                         identf_s[0:U, 0:U]).then_inc(s_4tt, 1)
                    if h > 0:
                        tensor.wait_ge(s_4sq, h)
                    for k in range(KC):
                        tensor.wait_ge(s_4sk, h * KC + k + 1)
                        tensor.matmul(ps_v[0:64, 512:562], wchunk(wq_s, k, h * 64, 64),
                                      xtopT4[:, (h * KC + k) * U:(h * KC + k + 1) * U],
                                      start=(k == 0), stop=False)
                    tensor.matmul(ps_v[0:64, 512:562], bqrow4[0:1, h * 64:(h + 1) * 64],
                                  ones_row[0:1, 0:U], start=False, stop=True).then_inc(s_4tq, 1)

            @block.scalar
            def _(scalar):
                for h in range(HPC):
                    scalar.wait_ge(s_4v, 3 * h + 3)
                    scalar.activation(bnag[0:U, 1:2], bnag[0:U, 1:2], AF.Ln,
                                      bias=epscol[0:U, 0:1]).then_inc(s_4a, 1)
                    scalar.wait_ge(s_4a, 2 * h + 1)
                    scalar.activation(bnag[0:U, 1:2], bnag[0:U, 1:2], AF.Exp,
                                      scale=-0.5).then_inc(s_4a, 1)
                    for k in range(KC):
                        scalar.wait_ge(s_4tt, h * KC + k + 1)
                        scalar.copy(xtopT4[:, (h * KC + k) * U:(h * KC + k + 1) * U],
                                    ps_q[0:128, 0:U]).then_inc(s_4sk, 1)
                    scalar.wait_ge(s_4tq, h + 1)
                    scalar.wait_ge(s_4v, 1)
                    scalar.copy(qpair[h // 2][(h % 2) * 64:(h % 2) * 64 + 64, (h % 2) * U:(h % 2) * U + U],
                                ps_v[0:64, 512:562]).then_inc(s_4sq, 1)

            @block.tensor
            def _(tensor):
                tensor.wait_ge(s_4sq, 4)
                LAG = 2
                for ci in range(64 + LAG):
                    if ci < 64:
                        if ci >= 4:
                            tensor.wait_ge(s_4se, ci - 3)
                        psl = pslot(ci % 4)
                        for r in range(2):
                            mm = tensor.matmul(psl[:, r * 100:(r + 1) * 100],
                                               kT[r][:, ci * 128:(ci + 1) * 128], qpair[r][:, :],
                                               start=(r == 0), stop=(r == 1))
                        mm.then_inc(s_4tb, 1)
                    j = ci - LAG
                    if j >= 0:
                        tensor.wait_ge(s_4se, j + 1)
                        for h in range(HPC):
                            r, odd = h // 2, h % 2
                            o = r * 8320 + j * 130 + odd * 65
                            mm = tensor.matmul(
                                ps_v[0:65, h * U:(h + 1) * U],
                                vs_all[:, o:o + 65],
                                expT[:, (j % 4) * 200 + r * 100 + odd * U:
                                        (j % 4) * 200 + r * 100 + odd * U + U],
                                start=(j == 0 and h == 0), stop=(j == 63 and h == 3))
                        mm.then_inc(s_4tc, 1)

            @block.scalar
            def _(scalar):
                for ci in range(64):
                    scalar.wait_ge(s_4tb, ci + 1)
                    if ci >= 4:
                        scalar.wait_ge(s_4tc, ci - 3)
                    scalar.activation(expT[:, (ci % 4) * 200:(ci % 4) * 200 + 200],
                                      pslot(ci % 4)[:, 0:200], AF.Exp,
                                      scale=r8_ml[:, ci:ci + 1]).then_inc(s_4se, 1)

            @block.scalar
            def _(scalar):
                scalar.wait_ge(s_4tc, 64)
                scalar.copy(cc[0:65, :], ps_v[0:65, 0:4 * U]).then_inc(s_4ds, 1)
                for h in range(HPC):
                    scalar.wait_ge(s_4td, h + 1)
                    if h >= 2:
                        scalar.wait_ge(s_4dd, 32)
                    scalar.copy(dstage[h % 2][0:U, :], ps_sc[0:U, 0:D]).then_inc(s_4ds, 1)

            @block.vector
            def _(vector):
                vector.wait_ge(s_4vs, 2)
                vector.wait_ge(s_4ds, 1)
                vector.reciprocal(ccrec[0:1, :], cc[64:65, :]).then_inc(s_4cv, 1)
                vector.wait_ge(s_4tr, 1)
                vector.tensor_tensor(ccn[:, :], cc[0:64, :], ps_v[0:64, 562:762],
                                     ALU.mult).then_inc(s_4cv, 1)
                for h in range(HPC):
                    vector.wait_ge(s_4cv, 2)
                    if h >= 2:
                        vector.wait_ge(s_4td, h - 1)
                    vector.tensor_scalar(ccp[h % 2][(h % 2) * 64:(h % 2) * 64 + 64, :],
                                         ccn[:, h * U:(h + 1) * U],
                                         vmcl[(h % 2) * 64:(h % 2) * 64 + 64, (h // 2):(h // 2) + 1],
                                         None, ALU.subtract).then_inc(s_4cv, 1)

            @block.tensor
            def _(tensor):
                tensor.wait_ge(s_cms, 2)
                tensor.wait_ge(s_4w2, 16)
                tensor.wait_ge(s_4cv, 1)
                tensor.matmul(ps_v[0:64, 562:762], onesf[0:1, 0:64], ccrec[0:1, :],
                              start=True, stop=True).then_inc(s_4tr, 1)
                for h in range(HPC):
                    tensor.wait_ge(s_4cv, 3 + h)
                    if h > 0:
                        tensor.wait_ge(s_4ds, h + 1)
                    for n in range(2):
                        mm = tensor.matmul(ps_sc[0:U, n * 512:(n + 1) * 512], ccp[h % 2][:, :],
                                           woH[:, (h // 2) * D + n * 512:(h // 2) * D + n * 512 + 512],
                                           start=True, stop=True)
                    mm.then_inc(s_4td, 1)

            @block.gpsimd
            def _(gpsimd):
                gpsimd.wait_ge(s_cms, 1)
                gpsimd.dma_start(out=e1iin.ap(), in_=idxrows[0:1, :]).then_inc(ch_g, 16)
                gpsimd.wait_ge(ch_g, 16)
                gpsimd.collective_compute(
                    "AllGather", ALU.bypass, replica_groups=GROUPS,
                    ins=[e1iin.ap().opt()], outs=[e1iout.ap().opt()]).then_inc(s_cc, 1)
                for h in range(HPC):
                    gpsimd.wait_ge(s_4ds, h + 2)
                    gpsimd.dma_start(out=e1din.ap()[h * U:(h + 1) * U, :],
                                     in_=dstage[h % 2][0:U, :]).then_inc(s_4dd, 16)
                gpsimd.wait_ge(s_4dd, 64)
                gpsimd.collective_compute(
                    "AllGather", ALU.bypass, replica_groups=GROUPS,
                    ins=[e1din.ap().opt()], outs=[e1dout.ap().opt()]).then_inc(s_cc, 1)

        if upto >= 5:
            pieces = []
            for r4 in range(4):
                c0 = 0
                while c0 < 200:
                    f = r4 * 224 + c0
                    t, p0 = f // 128, f % 128
                    n = min(128 - p0, 200 - c0)
                    pieces.append((r4, c0, t, p0, n))
                    c0 += n
            assert sum(n for *_, n in pieces) == 800 and len(pieces) == 10

            def dtile(t, k):
                return vs_all[0:128, t * 1024 + k * 128:t * 1024 + (k + 1) * 128]

            def tl5(t, n=None):
                base = (t % 4) * TSL
                if n is None:
                    return kT[t // 4][:, base:base + TSL]
                return kT[t // 4][:, base + n * 512:base + (n + 1) * 512]

            def ps5(k, n):
                if k % 2 == 0:
                    tns = ps_q if n < 2 else ps_k
                else:
                    tns = ps_v if n < 2 else ps_sc
                return tns[:, (n % 2) * 512:(n % 2) * 512 + 512]

            @block.sync
            def _(sync):
                sync.wait_ge(s_cc, 3)
                sync.dma_start(out=idx7r[0:7, :],
                               in_=e1iout.ap().rearrange("r c -> (r c)").rearrange(
                                   "(t p) -> t p", p=128)).then_inc(s_5ix, 16)
                sync.wait_ge(s_cc, 4)
                sync.wait_ge(s_5z, 1)
                for (r4, c0, t, p0, n) in pieces:
                    sync.dma_start(out=vs_all[p0:p0 + n, t * 1024:(t + 1) * 1024],
                                   in_=e1dout.ap()[r4 * 200 + c0:r4 * 200 + c0 + n, :]
                                   ).then_inc(s_5d, 16)

            @block.tensor
            def _(tensor):
                tensor.wait_ge(s_5ix, 16)
                tensor.wait_ge(s_4ds, 5)
                tensor.transpose(ps_sc[0:128, 364:371], idx7r[0:7, :],
                                 identf_s[0:7, 0:7]).then_inc(s_5it, 1)

            @block.scalar
            def _(scalar):
                scalar.wait_ge(s_5it, 1)
                scalar.copy(idxT7[:, :], ps_sc[0:128, 364:371]).then_inc(s_5is, 1)

            @block.vector
            def _(vector):
                vector.wait_ge(s_4tc, 64)
                vector.memset(vs_all[:, 0:7168], 0.0).then_inc(s_5z, 1)
                vector.wait_ge(s_4tb, 64)
                vector.wait_ge(s_5is, 1)
                for t in range(7):
                    vector.tensor_scalar(tl5(t), iota_sl_s[:, :], idxT7[:, t:t + 1],
                                         None, ALU.is_equal).then_inc(s_5m, 1)

            @block.tensor
            def _(tensor):
                tensor.wait_ge(s_5m, 7)
                tensor.wait_ge(s_5d, 160)
                tensor.wait_ge(s_4se, 64)
                tensor.wait_ge(s_4cv, 6)
                for k in range(KC):
                    if k >= 2:
                        tensor.wait_ge(s_5v, 4 * (k - 1))
                    for t in range(7):
                        for n in range(4):
                            mm = tensor.matmul(ps5(k, n), dtile(t, k), tl5(t, n),
                                               start=(t == 0), stop=(t == 6))
                            if t == 6:
                                mm.then_inc(s_5t, 1)

            @block.vector
            def _(vector):
                for k in range(KC):
                    for n in range(4):
                        vector.wait_ge(s_5t, 4 * k + n + 1)
                        vector.tensor_tensor(x2T[:, k * TSL + n * 512:k * TSL + (n + 1) * 512],
                                             x2T[:, k * TSL + n * 512:k * TSL + (n + 1) * 512],
                                             ps5(k, n), ALU.add).then_inc(s_5v, 1)

        if upto >= 6:
            esB.close()
            w1_s = sb("w1_s", [128, KC * D], BF16)
            w2_s = kT[1][:, 4096:8192]
            w3_s = kT[0][:, 7168:8192]
            w4_s = sb("w4_s", [128, 2], BF16)
            gw2_s = wq_s[:, 0:1024]
            gw1q = wq_s[:, 1024:2048]
            gw3_s = sb("gw3_s", [128, 2], BF16)
            b1c_s = sb("b1c_s", [128, 8], F32)
            b2c_s = sb("b2c_s", [128, 4], F32)
            b3c_s = sb("b3c_s", [128, 2], F32)
            b4c_s = sb("b4c_s", [1, 1], F32)
            gb1c_s = sb("gb1c_s", [128, 4], F32)
            gb2c_s = sb("gb2c_s", [128, 2], F32)
            gb3c_s = sb("gb3c_s", [1, 1], F32)
            ar_s = sb("ar_s", [1, 1], F32)
            rowA = sb("rowA", [1, TSL], F32)
            rowB = sb("rowB", [1, TSL], F32)
            sqt = kT[1][:, 0:4096]
            rrep_s = sb("rrep_s", [128, 1024], F32)
            murr_s = sb("murr_s", [128, 1024], F32)
            ysum_c = sb("ysum_c", [128, 8], F32)
            ysum_c4 = sb("ysum_c4", [128, 32], F32)
            ysum8 = sb("ysum8", [128, 32], F32)
            gcc = sb("gcc", [128, 8], F32)
            gccb = sb("gccb", [128, 8], BF16)
            g1cb = sb("g1cb", [128, 4], BF16)
            g2cb = sb("g2cb", [128, 2], BF16)
            gfab = sb("gfab", [1, 4], F32)
            tfrow = sb("tfrow", [1, TT], F32)
            crow = sb("crow", [1, TT], F32)
            crep_s = sb("crep_s", [128, 2 * TT], F32)
            tw = [rrep_s[:, 0:512], rrep_s[:, 512:1024],
                  murr_s[:, 0:512], murr_s[:, 512:1024]] + \
                 [sb("tw%d" % i, [128, TT], F32) for i in range(4, 8)]
            outbf = vs_all
            h1_s = kT[0]
            h2_s = kT[0]
            h3_s = kT[0]

            @block.gpsimd
            def _(gpsimd):
                gpsimd.wait_ge(s_5t, 32)
                gpsimd.wait_ge(s_cc, 4)
                gpsimd.dma_start(out=w1_s[:, :].rearrange("p (k j) -> p k j", j=D),
                                 in_=cw1_in.rearrange("(k p) j -> p k j", p=128)).then_inc(s_6w, 16)
                gpsimd.dma_start(out=w2_s.rearrange("p (k j) -> p k j", j=512),
                                 in_=cw2_in.rearrange("(k p) j -> p k j", p=128)).then_inc(s_6w, 16)
                gpsimd.dma_start(out=w3_s.rearrange("p (k j) -> p k j", j=256),
                                 in_=cw3_in.rearrange("(k p) j -> p k j", p=128)).then_inc(s_6w, 16)
                gpsimd.dma_start(out=w4_s[:, :], in_=cw4_in).then_inc(s_6w, 16)
                gpsimd.dma_start(out=gw2_s.rearrange("p (k j) -> p k j", j=256),
                                 in_=gw2_in.rearrange("(k p) j -> p k j", p=128)).then_inc(s_6w, 16)
                gpsimd.dma_start(out=gw3_s[:, :], in_=gw3_in).then_inc(s_6w, 16)

            @block.sync
            def _(sync):
                sync.wait_ge(s_5t, 32)
                for t_, src in ((b1c_s, b1c_in), (b2c_s, b2c_in), (b3c_s, b3c_in), (b4c_s, b4c_in),
                                (gb1c_s, gb1c_in), (gb2c_s, gb2c_in), (gb3c_s, gb3c_in), (ar_s, ar_in)):
                    sync.dma_start(out=t_[:, :], in_=src).then_inc(s_6wb, 16)
                for k in range(4):
                    sync.wait_ge(s_5v, 4 * (k + 1))
                    sync.dma_start(out=out.rearrange("(k p) n -> k p n", p=128)[k, :, :],
                                   in_=x2T[:, k * TSL:(k + 1) * TSL]).then_inc(s_ld, 16)

            @block.scalar
            def _(scalar):
                for k in range(4, KC):
                    scalar.wait_ge(s_5v, 4 * (k + 1))
                    scalar.dma_start(out=out.rearrange("(k p) n -> k p n", p=128)[k, :, :],
                                     in_=x2T[:, k * TSL:(k + 1) * TSL]).then_inc(s_5ix, 16)

            @block.vector
            def _(vector):
                vector.wait_ge(s_5t, 32)
                for k in range(KC):
                    vector.wait_ge(s_5v, 4 * (k + 1))
                    vector.tensor_copy(outbf[:, k * TSL:(k + 1) * TSL],
                                       x2T[:, k * TSL:(k + 1) * TSL]).then_inc(s_6v, 1)
                    if k >= 2:
                        vector.wait_ge(s_6t, k - 1)
                    vector.wait_ge(s_6v, 2 * k + 1)
                    vector.tensor_tensor(sqt[:, (k % 2) * TSL:(k % 2 + 1) * TSL],
                                         outbf[:, k * TSL:(k + 1) * TSL],
                                         outbf[:, k * TSL:(k + 1) * TSL], ALU.mult).then_inc(s_6v, 1)

            @block.tensor
            def _(tensor):
                tensor.wait_ge(s_5v, 28)
                for k in range(KC):
                    tensor.wait_ge(s_6v, 2 * k + 2)
                    for n in range(4):
                        bank = (ps_q if n < 2 else ps_k)
                        bsl = slice((n % 2) * 512, (n % 2) * 512 + 512)
                        tensor.matmul(bank[0:1, bsl],
                                      onesc[:, :], outbf[:, k * TSL + n * 512:k * TSL + n * 512 + 512],
                                      start=(k == 0), stop=(k == KC - 1))
                        mm2 = tensor.matmul(
                            bank[32:33, bsl],
                            onesc[:, :], sqt[:, (k % 2) * TSL + n * 512:(k % 2) * TSL + n * 512 + 512],
                            start=(k == 0), stop=(k == KC - 1))
                    mm2.then_inc(s_6t, 1)

            @block.scalar
            def _(scalar):
                scalar.wait_ge(s_6t, 8)
                for n in range(4):
                    rsl = slice(n * 512, (n + 1) * 512)
                    if n >= 1:
                        scalar.wait_ge(s_6s, 7 * n)
                    scalar.copy(rowA[0:1, rsl],
                                (ps_q if n < 2 else ps_k)[0:1, (n % 2) * 512:(n % 2) * 512 + 512])
                    scalar.copy(rowB[0:1, rsl],
                                (ps_q if n < 2 else ps_k)[32:33, (n % 2) * 512:(n % 2) * 512 + 512]
                                ).then_inc(s_6s, 1)
                    scalar.wait_ge(s_6s, 7 * n + 4)
                    scalar.activation(rowB[0:1, rsl], rowB[0:1, rsl], AF.Ln,
                                      bias=epscol[0:1, 0:1]).then_inc(s_6s, 1)
                    scalar.wait_ge(s_6s, 7 * n + 5)
                    scalar.activation(rowB[0:1, rsl], rowB[0:1, rsl], AF.Exp,
                                      scale=-0.5).then_inc(s_6s, 1)

            @block.vector
            def _(vector):
                prow = lambda n: (ps_q if n < 2 else ps_k)[64:65, (n % 2) * 512:(n % 2) * 512 + 512]
                for n in range(4):
                    rsl = slice(n * 512, (n + 1) * 512)
                    vector.wait_ge(s_6s, 7 * n + 1)
                    vector.tensor_scalar(rowA[0:1, rsl], rowA[0:1, rsl], 1.0 / D, None, ALU.mult)
                    vector.tensor_scalar(rowB[0:1, rsl], rowB[0:1, rsl], 1.0 / D,
                                         None, ALU.mult).then_inc(s_6s, 1)
                    vector.wait_ge(s_6s, 7 * n + 2)
                    vector.tensor_tensor(prow(n), rowA[0:1, rsl], rowA[0:1, rsl],
                                         ALU.mult).then_inc(s_6s, 1)
                    vector.wait_ge(s_6s, 7 * n + 3)
                    vector.tensor_tensor(rowB[0:1, rsl], rowB[0:1, rsl], prow(n),
                                         ALU.subtract).then_inc(s_6s, 1)
                    vector.wait_ge(s_6s, 7 * n + 6)
                    vector.tensor_tensor(rowA[0:1, rsl], rowA[0:1, rsl], rowB[0:1, rsl],
                                         ALU.mult).then_inc(s_6s, 1)

            @block.tensor
            def _(tensor):
                tensor.wait_ge(s_5v, 32)
                for n in range(4):
                    tensor.wait_ge(s_6s, 7 * n + 7)
                    if n >= 2:
                        tensor.wait_ge(s_6re, n - 1)
                    tensor.matmul(ps_v[:, (n % 2) * 512:(n % 2) * 512 + 512], onesf[0:1, :],
                                  rowB[0:1, n * 512:(n + 1) * 512], start=True, stop=True)
                    tensor.matmul(ps_sc[:, (n % 2) * 512:(n % 2) * 512 + 512], onesf[0:1, :],
                                  rowA[0:1, n * 512:(n + 1) * 512], start=True, stop=True).then_inc(s_6rt, 1)

            @block.scalar
            def _(scalar):
                for n in range(4):
                    scalar.wait_ge(s_6rt, n + 1)
                    if n >= 2:
                        scalar.wait_ge(s_6y, 2 * (n - 1))
                        scalar.wait_ge(s_6yg, 2 * (n - 1))
                    scalar.copy(rrep_s[:, (n % 2) * 512:(n % 2) * 512 + 512],
                                ps_v[:, (n % 2) * 512:(n % 2) * 512 + 512])
                    scalar.copy(murr_s[:, (n % 2) * 512:(n % 2) * 512 + 512],
                                ps_sc[:, (n % 2) * 512:(n % 2) * 512 + 512]).then_inc(s_6re, 1)

            @block.vector
            def _(vector):
                vector.wait_ge(s_ld, 160)
                vector.wait_ge(s_5ix, 80)
                for n in range(4):
                    vector.wait_ge(s_6re, n + 1)
                    rr = rrep_s[:, (n % 2) * 512:(n % 2) * 512 + 512]
                    mr = murr_s[:, (n % 2) * 512:(n % 2) * 512 + 512]
                    sl = lambda k: x2T[:, k * TSL + n * 512:k * TSL + (n + 1) * 512]
                    tt = None
                    for k in range(5):
                        tt = vector.tensor_tensor(sl(k), sl(k), rr, ALU.mult)
                    tt.then_inc(s_6y, 1)
                    vector.wait_ge(s_6y, 2 * n + 1)
                    for k in range(5):
                        tt = vector.tensor_tensor(sl(k), sl(k), mr, ALU.subtract)
                    tt.then_inc(s_6y, 1)

            @block.gpsimd
            def _(gpsimd):
                gpsimd.wait_ge(s_ld, 160)
                gpsimd.wait_ge(s_5ix, 80)
                for n in range(4):
                    gpsimd.wait_ge(s_6re, n + 1)
                    rr = rrep_s[:, (n % 2) * 512:(n % 2) * 512 + 512]
                    mr = murr_s[:, (n % 2) * 512:(n % 2) * 512 + 512]
                    sl = lambda k: x2T[:, k * TSL + n * 512:k * TSL + (n + 1) * 512]
                    tt = None
                    for k in (5, 6, 7):
                        tt = gpsimd.tensor_tensor(sl(k), sl(k), rr, ALU.mult)
                    tt.then_inc(s_6yg, 1)
                    gpsimd.wait_ge(s_6yg, 2 * n + 1)
                    for k in (5, 6, 7):
                        tt = gpsimd.tensor_tensor(sl(k), sl(k), mr, ALU.subtract)
                    tt.then_inc(s_6yg, 1)

            @block.scalar
            def _(scalar):
                scalar.wait_ge(s_6t, 8)
                for n in range(4):
                    scalar.wait_ge(s_6y, 2 * n + 2)
                    scalar.wait_ge(s_6yg, 2 * n + 2)
                    sc = None
                    for k in range(KC):
                        sc = scalar.activation(outbf[:, k * TSL + n * 512:k * TSL + (n + 1) * 512],
                                               x2T[:, k * TSL + n * 512:k * TSL + (n + 1) * 512],
                                               AF.Copy, accum_out=ysum_c4[:, k * 4 + n:k * 4 + n + 1])
                    sc.then_inc(s_6ys, 1)

            @block.vector
            def _(vector):
                vector.wait_ge(s_6ys, 4)
                vector.tensor_reduce(ysum_c[:, :], ysum_c4[:, :].rearrange("p (k n) -> p k n", n=4),
                                     AX.X, ALU.add).then_inc(s_6ys, 1)

            @block.gpsimd
            def _(gpsimd):
                gpsimd.wait_ge(s_6ys, 5)
                gpsimd.dma_start(out=e2in.ap(), in_=ysum_c[:, :]).then_inc(s_6g, 16)
                gpsimd.wait_ge(s_6g, 16)
                gpsimd.collective_compute(
                    "AllGather", ALU.bypass, replica_groups=GROUPS,
                    ins=[e2in.ap().opt()], outs=[e2out.ap().opt()]).then_inc(s_cc, 1)
                gpsimd.wait_ge(s_cc, 5)
                gpsimd.dma_start(out=ysum8[:, :].rearrange("p (r k) -> p r k", k=8),
                                 in_=e2out.ap().rearrange("(r p) k -> p r k", p=128)).then_inc(s_6g, 16)
                for q in range(4):
                    if q > 0:
                        gpsimd.wait_ge(s_6gt, 4 * q)
                    gpsimd.dma_start(
                        out=gw1q.rearrange("p (k j) -> p k j", j=512),
                        in_=gw1_in[q * 256:(q + 1) * 256, :].rearrange("(k p) j -> p k j", p=128)
                    ).then_inc(s_6gq, 16)

            @block.vector
            def _(vector):
                vector.wait_ge(s_6g, 32)
                vector.tensor_reduce(gcc[:, :], ysum8[:, :].rearrange("p (r k) -> p k r", k=8),
                                     AX.X, ALU.add).then_inc(s_6gs, 1)

            @block.scalar
            def _(scalar):
                scalar.wait_ge(s_6gs, 1)
                scalar.copy(gccb[:, :], gcc[:, :]).then_inc(s_6gs, 1)

            pg1 = [ps_v[0:128, 0:1], ps_v[0:128, 512:513], ps_sc[0:128, 0:1], ps_sc[0:128, 512:513]]

            def emit_gfnet_tensor(tensor):
                tensor.wait_ge(s_6gs, 2)
                for q in range(4):
                    tensor.wait_ge(s_6gq, 16 * (q + 1))
                    for jb in range(4):
                        for k2 in range(2):
                            mm = tensor.matmul(pg1[jb],
                                               gw1q[:, k2 * 512 + jb * 128:k2 * 512 + jb * 128 + 128],
                                               gccb[:, q * 2 + k2:q * 2 + k2 + 1],
                                               start=(q == 0 and k2 == 0), stop=(q == 3 and k2 == 1))
                        mm.then_inc(s_6gt, 1)
                tensor.wait_ge(s_6gs, 6)
                for jb2 in range(2):
                    for k1 in range(4):
                        mm = tensor.matmul(pg1[jb2],
                                           gw2_s[:, k1 * 256 + jb2 * 128:k1 * 256 + jb2 * 128 + 128],
                                           g1cb[:, k1:k1 + 1], start=(k1 == 0), stop=(k1 == 3))
                    mm.then_inc(s_6gt, 1)
                tensor.wait_ge(s_6gs, 8)
                mm = None
                for k2 in range(2):
                    mm = tensor.matmul(ps_v[0:1, 0:1], gw3_s[:, k2:k2 + 1], g2cb[:, k2:k2 + 1],
                                       start=(k2 == 0), stop=(k2 == 1))
                mm.then_inc(s_6gt, 1)

            def emit_gfnet_scalar(scalar):
                for jb in range(4):
                    scalar.wait_ge(s_6gt, 13 + jb)
                    scalar.activation(g1cb[:, jb:jb + 1], pg1[jb],
                                      AF.Gelu, scale=1.0 / L, bias=gb1c_s[:, jb:jb + 1]).then_inc(s_6gs, 1)
                for jb2 in range(2):
                    scalar.wait_ge(s_6gt, 17 + jb2)
                    scalar.activation(g2cb[:, jb2:jb2 + 1], pg1[jb2],
                                      AF.Gelu, bias=gb2c_s[:, jb2:jb2 + 1]).then_inc(s_6gs, 1)
                scalar.wait_ge(s_6gt, 19)
                scalar.activation(gfab[0:1, 0:1], ps_v[0:1, 0:1], AF.Sigmoid,
                                  bias=gb3c_s[0:1, 0:1]).then_inc(s_6gs, 1)

            if upto == 6:
                @block.tensor
                def _(tensor):
                    emit_gfnet_tensor(tensor)

                @block.scalar
                def _(scalar):
                    emit_gfnet_scalar(scalar)

            @block.vector
            def _(vector):
                vector.wait_ge(s_6gs, 9)
                vector.tensor_scalar(gfab[0:1, 3:4], gfab[0:1, 0:1], 0.7, -0.5,
                                     ALU.mult, ALU.add).then_inc(s_6gs, 1)
                vector.wait_ge(s_6gs, 10)
                vector.tensor_scalar(gfab[0:1, 2:3], gfab[0:1, 3:4], ar_s[0:1, 0:1], 2.0,
                                     ALU.mult, ALU.mult).then_inc(s_6gs, 1)
                vector.wait_ge(s_6gs, 11)
                vector.tensor_scalar(gfab[0:1, 2:3], gfab[0:1, 2:3], 0.1, None, ALU.add).then_inc(s_6gs, 1)
                vector.tensor_scalar(gfab[0:1, 1:2], ar_s[0:1, 0:1], 0.6, None, ALU.mult).then_inc(s_6gs, 1)

            reg_probe("rowA", rowA[0:1, :])
            reg_probe("rowB", rowB[0:1, :])
            reg_probe("ysum_c", ysum_c[:, :])
            reg_probe("gcc", gcc[:, :])
            reg_probe("gfab", gfab[0:1, :])
            reg_probe("ybf0", vs_all[:, 0:TSL])

        if upto >= 7:
            h2o, h3o = 8 * 512, 8 * 512 + 4 * 512

            @block.tensor
            def _(tensor):
                tensor.wait_ge(s_6w, 96)
                tensor.wait_ge(s_6wb, 128)
                for i in range(3):
                    for n in range(4):
                        m = 4 * i + n
                        tb = 16 * m
                        pe = tb - 16
                        if i > 0:
                            tensor.wait_ge(s_7v, 3 * (m - 4) + 3)
                            tensor.wait_ge(s_7g, 3 * (m - 4) + 3)
                        else:
                            tensor.wait_ge(s_6ys, n + 1)
                            if n == 0:
                                tensor.wait_ge(s_6s, 18)
                        for jb in range(8):
                            if jb >= 2:
                                tensor.wait_ge(s_7e, tb + jb - 1)
                            elif m > 0:
                                tensor.wait_ge(s_7e, pe + 7 + jb)
                            for k in range(KC):
                                mm = tensor.matmul(ps_q[:, (jb % 2) * 512:(jb % 2) * 512 + 512],
                                                   w1_s[:, k * D + jb * 128:k * D + jb * 128 + 128],
                                                   outbf[:, k * TSL + n * 512:k * TSL + (n + 1) * 512],
                                                   start=(k == 0), stop=(k == KC - 1))
                            mm.then_inc(s_7t, 1)
                        if i == 0 and n == 0:
                            tensor.wait_ge(s_6s, 25)
                        for jb2 in range(4):
                            if jb2 >= 2:
                                tensor.wait_ge(s_7e, tb + 8 + jb2 - 1)
                            elif m > 0:
                                tensor.wait_ge(s_7e, pe + 11 + jb2)
                            for k1 in range(KC):
                                tensor.wait_ge(s_7e, tb + k1 + 1)
                                mm = tensor.matmul(ps_k[:, (jb2 % 2) * 512:(jb2 % 2) * 512 + 512],
                                                   w2_s[:, k1 * 512 + jb2 * 128:k1 * 512 + jb2 * 128 + 128],
                                                   h1_s[:, k1 * 512:(k1 + 1) * 512],
                                                   start=(k1 == 0), stop=(k1 == KC - 1))
                            mm.then_inc(s_7t, 1)
                        if i == 0 and n == 0:
                            tensor.wait_ge(s_6re, 4)
                            emit_gfnet_tensor(tensor)
                            tensor.wait_ge(s_6gs, 9)
                        for jb3 in range(2):
                            if m > 0:
                                tensor.wait_ge(s_7e, pe + 13 + jb3)
                            for k2 in range(4):
                                tensor.wait_ge(s_7e, tb + 8 + k2 + 1)
                                mm = tensor.matmul(ps_v[:, jb3 * 512:jb3 * 512 + 512],
                                                   w3_s[:, k2 * 256 + jb3 * 128:k2 * 256 + jb3 * 128 + 128],
                                                   h2_s[:, h2o + k2 * 512:h2o + (k2 + 1) * 512],
                                                   start=(k2 == 0), stop=(k2 == 3))
                            mm.then_inc(s_7t, 1)
                        mm = None
                        for k3 in range(2):
                            tensor.wait_ge(s_7e, tb + 12 + k3 + 1)
                            if m > 0 and k3 == 0:
                                tensor.wait_ge(s_7e, pe + 15)
                            mm = tensor.matmul(ps_sc[0:1, 0:512], w4_s[:, k3:k3 + 1],
                                               h3_s[:, h3o + k3 * 512:h3o + (k3 + 1) * 512],
                                               start=(k3 == 0), stop=(k3 == 1))
                        mm.then_inc(s_7t, 1)
                        tensor.wait_ge(s_7tf, m + 1)
                        if m > 0:
                            tensor.wait_ge(s_7e, pe + 16)
                        tensor.matmul(ps_sc[:, 512:1024], onesf[0:1, :], crow[0:1, :],
                                      start=True, stop=True).then_inc(s_7t, 1)

            @block.scalar
            def _(scalar):
                for i in range(3):
                    for n in range(4):
                        m = 4 * i + n
                        tb = 16 * m
                        for jb in range(8):
                            scalar.wait_ge(s_7t, tb + jb + 1)
                            scalar.activation(h1_s[:, jb * 512:(jb + 1) * 512],
                                              ps_q[:, (jb % 2) * 512:(jb % 2) * 512 + 512], AF.Gelu,
                                              bias=b1c_s[:, jb:jb + 1]).then_inc(s_7e, 1)
                        for jb2 in range(4):
                            scalar.wait_ge(s_7t, tb + 8 + jb2 + 1)
                            scalar.activation(h2_s[:, h2o + jb2 * 512:h2o + (jb2 + 1) * 512],
                                              ps_k[:, (jb2 % 2) * 512:(jb2 % 2) * 512 + 512], AF.Gelu,
                                              bias=b2c_s[:, jb2:jb2 + 1]).then_inc(s_7e, 1)
                        if i == 0 and n == 0:
                            emit_gfnet_scalar(scalar)
                        for jb3 in range(2):
                            scalar.wait_ge(s_7t, tb + 12 + jb3 + 1)
                            scalar.activation(h3_s[:, h3o + jb3 * 512:h3o + (jb3 + 1) * 512],
                                              ps_v[:, jb3 * 512:jb3 * 512 + 512], AF.Gelu,
                                              bias=b3c_s[:, jb3:jb3 + 1]).then_inc(s_7e, 1)
                        scalar.wait_ge(s_7t, tb + 15)
                        if m >= 1:
                            scalar.wait_ge(s_7tf, m)
                        scalar.activation(tfrow[0:1, :], ps_sc[0:1, 0:512], AF.Sigmoid,
                                          bias=b4c_s[0:1, 0:1]).then_inc(s_7e, 1)
                        scalar.wait_ge(s_7t, tb + 16)
                        if m >= 2:
                            scalar.wait_ge(s_7v, 3 * (m - 1))
                            scalar.wait_ge(s_7g, 3 * (m - 1))
                        scalar.copy(crep_s[:, (n % 2) * 512:(n % 2) * 512 + 512],
                                    ps_sc[:, 512:1024]).then_inc(s_7e, 1)

            @block.vector
            def _(vector):
                vector.wait_ge(s_6gs, 13)
                for i in range(3):
                    for n in range(4):
                        m = 4 * i + n
                        tb = 16 * m
                        vector.wait_ge(s_7e, tb + 15)
                        if m > 0:
                            vector.wait_ge(s_7t, tb)
                        vector.tensor_scalar(crow[0:1, :], tfrow[0:1, :], gfab[0:1, 1:2],
                                             gfab[0:1, 2:3], ALU.mult, ALU.add).then_inc(s_7tf, 1)
                        vector.wait_ge(s_7e, tb + 16)
                        if i == 0:
                            vector.wait_ge(s_6ys, 5)
                        if m > 0:
                            vector.wait_ge(s_7v, 3 * m)
                        cs = crep_s[:, (n % 2) * 512:(n % 2) * 512 + 512]
                        sl = lambda k: x2T[:, k * TSL + n * 512:k * TSL + (n + 1) * 512]
                        vector.tensor_tensor(tw[0][:, :], cs, sl(4), ALU.mult)
                        vector.tensor_tensor(tw[1][:, :], cs, sl(0), ALU.mult)
                        vector.tensor_tensor(tw[2][:, :], cs, sl(5), ALU.mult)
                        vector.tensor_tensor(tw[3][:, :], cs, sl(1), ALU.mult).then_inc(s_7v, 1)
                        vector.wait_ge(s_7v, 3 * m + 1)
                        vector.tensor_tensor(sl(0), sl(0), tw[0][:, :], ALU.add)
                        vector.tensor_tensor(sl(4), sl(4), tw[1][:, :], ALU.subtract)
                        vector.tensor_tensor(sl(1), sl(1), tw[2][:, :], ALU.add)
                        vector.tensor_tensor(sl(5), sl(5), tw[3][:, :],
                                             ALU.subtract).then_inc(s_7v, 1)
                        vector.wait_ge(s_7v, 3 * m + 2)
                        tc = None
                        for k in (0, 1, 4, 5):
                            tc = vector.tensor_copy(outbf[:, k * TSL + n * 512:k * TSL + (n + 1) * 512],
                                                    sl(k))
                        tc.then_inc(s_7v, 1)

            @block.gpsimd
            def _(gpsimd):
                for i in range(3):
                    for n in range(4):
                        m = 4 * i + n
                        tb = 16 * m
                        gpsimd.wait_ge(s_7e, tb + 16)
                        if i == 0:
                            gpsimd.wait_ge(s_6ys, 5)
                        if m > 0:
                            gpsimd.wait_ge(s_7g, 3 * m)
                        cs = crep_s[:, (n % 2) * 512:(n % 2) * 512 + 512]
                        sl = lambda k: x2T[:, k * TSL + n * 512:k * TSL + (n + 1) * 512]
                        gpsimd.tensor_tensor(tw[4][:, :], cs, sl(6), ALU.mult)
                        gpsimd.tensor_tensor(tw[5][:, :], cs, sl(2), ALU.mult)
                        gpsimd.tensor_tensor(tw[6][:, :], cs, sl(7), ALU.mult)
                        gpsimd.tensor_tensor(tw[7][:, :], cs, sl(3), ALU.mult).then_inc(s_7g, 1)
                        gpsimd.wait_ge(s_7g, 3 * m + 1)
                        gpsimd.tensor_tensor(sl(2), sl(2), tw[4][:, :], ALU.subtract)
                        gpsimd.tensor_tensor(sl(6), sl(6), tw[5][:, :], ALU.add)
                        gpsimd.tensor_tensor(sl(3), sl(3), tw[6][:, :], ALU.subtract)
                        gpsimd.tensor_tensor(sl(7), sl(7), tw[7][:, :],
                                             ALU.add).then_inc(s_7g, 1)
                        gpsimd.wait_ge(s_7g, 3 * m + 2)
                        tc = None
                        for k in (2, 3, 6, 7):
                            tc = gpsimd.tensor_copy(outbf[:, k * TSL + n * 512:k * TSL + (n + 1) * 512],
                                                    sl(k))
                        tc.then_inc(s_7g, 1)

            reg_probe("h1s", kT[0][:, 0:4096])
            reg_probe("tfrow", tfrow[0:1, :])
            reg_probe("crow", crow[0:1, :])

        if upto >= 8:
            @block.gpsimd
            def _(gpsimd):
                gpsimd.wait_ge(s_7v, 36)
                gpsimd.wait_ge(s_7g, 36)
                gpsimd.wait_ge(s_ld, 160)
                gpsimd.wait_ge(s_5ix, 80)
                for k in range(KC):
                    gpsimd.dma_start(out=out.rearrange("(k p) n -> k p n", p=128)[k, :, :],
                                     in_=x2T[:, k * TSL:(k + 1) * TSL],
                                     accum_op=ALU.add).then_inc(s_8d, 16)

        reg_probe("m_buf", m_buf[:, :])
        reg_probe("idxrows", idxrows[:, :])
        reg_probe("idxu", idxu[:, :])
        reg_probe("vme", vme[:, :])
        reg_probe("baserow", baserow[:, :])
        reg_probe("x2s", x2T[:, 0:2048])
        reg_probe("x2s7", x2T[:, 7 * TSL:8 * TSL])
        reg_probe("qpair0", qpair[0][:, :])
        reg_probe("qpair1", qpair[1][:, :])
        reg_probe("expT", expT[:, :])
        reg_probe("cc", cc[:, :])
        reg_probe("ccp0", ccp[0][:, :])
        reg_probe("ccp1", ccp[1][:, :])
        reg_probe("vmcl", vmcl[:, :])
        reg_probe("dst0", dstage[0][:, :])
        reg_probe("dst1", dstage[1][:, :])

        if upto < 6:
            out_stage = sbB("out_stage", [128, TSL], F32)

            @block.vector
            def _(vector):
                vector.memset(out_stage[:, :], 0.0).then_inc(s_st, 1)

        @block.sync
        def _(sync):
            sync.wait_ge(s_psk, 17)
            sync.wait_ge(s_pskb, 1)
            if upto < 6:
                sync.wait_ge(s_st, 1)
            if upto >= 1:
                sync.wait_ge(s_e0, 1)
            if upto >= 2:
                sync.wait_ge(s_mst, NT)
                sync.wait_ge(s_evv, NT)
                sync.wait_ge(s_evk, NT)
                sync.wait_ge(s_evq, NT)
                sync.wait_ge(s_scp, NT)
            if upto >= 3:
                sync.wait_ge(s_bs, 8)
                sync.wait_ge(s_cms, 2)
                sync.wait_ge(s_4vs, 2)
            if upto >= 4:
                sync.wait_ge(s_cc, 4)
                sync.wait_ge(s_4dd, 64)
                sync.wait_ge(s_4se, 64)
                sync.wait_ge(s_4ds, 5)
                sync.wait_ge(s_4cv, 6)
            if upto >= 5:
                sync.wait_ge(s_5v, 32)
            if upto >= 6:
                sync.wait_ge(s_6gs, 13)
                sync.wait_ge(s_6ys, 5)
                sync.wait_ge(s_ld, 160)
                sync.wait_ge(s_5ix, 80)
            if upto >= 7:
                sync.wait_ge(s_7t, 192)
                sync.wait_ge(s_7e, 192)
                sync.wait_ge(s_7v, 36)
                sync.wait_ge(s_7g, 36)
            if upto >= 8:
                sync.wait_ge(s_8d, 128)
            n = 0
            for name, ap in probe_srcs.items():
                sync.dma_start(out=probe_aps[name], in_=ap).then_inc(s_st, 16)
                n += 16
            if upto < 6:
                for k in range(KC):
                    sync.dma_start(out=out.rearrange("(k p) n -> k p n", p=128)[k, :, :],
                                   in_=out_stage[:, :]).then_inc(s_st, 16)
                    n += 16
            base = 1 if upto < 6 else 0
            sync.wait_ge(s_st, base + n)

        if upto < 6:
            esB.close()

    return nc, probe_aps


def _prep_core_inputs(inputs, c):
    b, q = c // 4, c % 4
    x = np.asarray(inputs["L0_attn_x"], np.float32)
    o0, o1 = q * MSL, (q + 1) * MSL
    t0 = q * TSL
    f32c = lambda a: np.ascontiguousarray(a, np.float32)
    m = {}
    m["xT"] = f32c(x[b].T)
    m["xTsl"] = f32c(x[b].T[:, t0:t0 + TSL])
    m["xR"] = f32c(x[b])
    m["wqT"] = f32c(np.asarray(inputs["L0_attn_Wq"])[o0:o1, :].T)
    m["wkT"] = f32c(np.asarray(inputs["L0_attn_Wk"])[o0:o1, :].T)
    m["wvT"] = f32c(np.asarray(inputs["L0_attn_Wv"])[o0:o1, :].T)
    m["woR"] = f32c(np.asarray(inputs["L0_attn_Wo"]).T[o0:o1, :])
    m["woT"] = f32c(np.asarray(inputs["L0_attn_Wo"]).T)
    m["bq"] = f32c(np.asarray(inputs["L0_attn_bq"])[o0:o1].reshape(1, MSL))
    m["bk"] = f32c(np.asarray(inputs["L0_attn_bk"])[o0:o1].reshape(1, MSL))
    m["bv"] = f32c(np.asarray(inputs["L0_attn_bv"])[o0:o1].reshape(1, MSL))
    m["bo"] = f32c(np.asarray(inputs["L0_attn_bo"]).reshape(1, D))
    m["idxk"] = np.ascontiguousarray(np.asarray(inputs["L0_attn_idx_k"], np.int32))
    m["identf"] = np.eye(128, dtype=np.float32)
    m["identb"] = np.eye(128, dtype=np.float32).astype(bfdt)
    m["iota_sl"] = f32c(np.broadcast_to(np.arange(t0, t0 + TSL, dtype=np.float32), (128, TSL)))
    m["iota_j"] = f32c(np.broadcast_to(np.arange(64, dtype=np.float32), (128, 64)))
    m["iota_ml"] = f32c((np.arange(64, dtype=np.float32)[None, :] * 128.0
                         + np.arange(128, dtype=np.float32)[:, None]))
    m["cw1"] = f32c(np.asarray(inputs["L0_ffn_cn_w1"]))
    m["cw2"] = f32c(np.asarray(inputs["L0_ffn_cn_w2"]))
    m["cw3"] = f32c(np.asarray(inputs["L0_ffn_cn_w3"]))
    m["cw4"] = f32c(np.asarray(inputs["L0_ffn_cn_w4"]).reshape(2, 128).T)
    m["gw1"] = f32c(np.asarray(inputs["L0_ffn_gc_w1"]))
    m["gw2"] = f32c(np.asarray(inputs["L0_ffn_gc_w2"]))
    m["gw3"] = f32c(np.asarray(inputs["L0_ffn_gc_w3"]).reshape(2, 128).T)
    m["b1c"] = f32c(np.asarray(inputs["L0_ffn_cn_b1"]).reshape(8, 128).T)
    m["b2c"] = f32c(np.asarray(inputs["L0_ffn_cn_b2"]).reshape(4, 128).T)
    m["b3c"] = f32c(np.asarray(inputs["L0_ffn_cn_b3"]).reshape(2, 128).T)
    m["b4c"] = f32c(np.asarray(inputs["L0_ffn_cn_b4"]).reshape(1, 1))
    m["gb1c"] = f32c(np.asarray(inputs["L0_ffn_gc_b1"]).reshape(4, 128).T)
    m["gb2c"] = f32c(np.asarray(inputs["L0_ffn_gc_b2"]).reshape(2, 128).T)
    m["gb3c"] = f32c(np.asarray(inputs["L0_ffn_gc_b3"]).reshape(1, 1))
    m["arin"] = f32c(np.asarray(inputs["L0_ffn_adaptive_range"]).reshape(1, 1))
    return m


def kernel(**inputs):
    nc, _ = build_nc(upto=8)
    in_maps = [_prep_core_inputs(inputs, c) for c in range(8)]
    res = run_bass_kernel_spmd(nc, in_maps, core_ids=list(range(8)))
    outp = np.zeros((B, L, D), np.float32)
    for c in range(8):
        b, q = c // 4, c % 4
        outp[b, q * TSL:(q + 1) * TSL, :] = np.asarray(res.results[c]["out"]).T
    return outp


# version 07
# speedup vs baseline: 1.1340x; 1.1340x over previous
import math
import contextlib
import numpy as np
import ml_dtypes

import concourse.bass as bass
import concourse.mybir as mybir
from concourse.bass_utils import run_bass_kernel_spmd

F32 = mybir.dt.float32
BF16 = mybir.dt.bfloat16
U32 = mybir.dt.uint32
I32 = mybir.dt.int32
AF = mybir.ActivationFunctionType
ALU = mybir.AluOpType
AX = mybir.AxisListType

B, L, D, H, DK, U = 2, 8192, 1024, 16, 64, 50
TSL = L // 4
NT = 16
TT = 512
KC = 8
HPC = 4
MSL = HPC * DK
SCALE = 0.125
EPS = 1e-5
NBIS = 26
BIG = 65536.0
GROUPS = [[0, 1, 2, 3], [4, 5, 6, 7]]
bfdt = ml_dtypes.bfloat16


def build_nc(upto=2, probes=(), p2mode='full'):
    nc = bass.Bass("TRN2", target_bir_lowering=False, debug=False)

    din = lambda n, s, d: nc.dram_tensor(n, s, d, kind="ExternalInput").ap()
    xT = din("xT", [D, L], F32)
    xTsl = din("xTsl", [D, TSL], F32)
    xR = din("xR", [L, D], F32)
    wqT = din("wqT", [D, MSL], F32)
    wkT = din("wkT", [D, MSL], F32)
    wvT = din("wvT", [D, MSL], F32)
    bq_in = din("bq", [1, MSL], F32)
    bk_in = din("bk", [1, MSL], F32)
    bv_in = din("bv", [1, MSL], F32)
    bo_in = din("bo", [1, D], F32)
    idxk_in = din("idxk", [U], I32)
    identf_in = din("identf", [128, 128], F32)
    identb_in = din("identb", [128, 128], BF16)
    iota_sl_in = din("iota_sl", [128, TSL], F32)
    iota_j_in = din("iota_j", [128, 64], F32)
    iota_ml_in = din("iota_ml", [128, 64], F32)
    woT_in = din("woT", [D, D], F32)
    woR_in = din("woR", [MSL, D], F32)
    cw1_in = din("cw1", [D, D], F32)
    cw2_in = din("cw2", [D, D // 2], F32)
    cw3_in = din("cw3", [D // 2, D // 4], F32)
    cw4_in = din("cw4", [128, 2], F32)
    gw1_in = din("gw1", [D, D // 2], F32)
    gw2_in = din("gw2", [D // 2, D // 4], F32)
    gw3_in = din("gw3", [128, 2], F32)
    b1c_in = din("b1c", [128, 8], F32)
    b2c_in = din("b2c", [128, 4], F32)
    b3c_in = din("b3c", [128, 2], F32)
    b4c_in = din("b4c", [1, 1], F32)
    gb1c_in = din("gb1c", [128, 4], F32)
    gb2c_in = din("gb2c", [128, 2], F32)
    gb3c_in = din("gb3c", [1, 1], F32)
    ar_in = din("arin", [1, 1], F32)

    out = nc.dram_tensor("out", [D, TSL], F32, kind="ExternalOutput").ap()

    e0in = nc.dram_tensor("e0in", [2, TSL], F32)
    e0out = nc.dram_tensor("e0out", [8, TSL], F32)
    murow_d = nc.dram_tensor("murow_d", [L], F32)
    rrec_d = nc.dram_tensor("rrec_d", [L], F32)
    e1ain = nc.dram_tensor("e1ain", [1, MSL], F32)
    e1aout = nc.dram_tensor("e1aout", [4, MSL], F32)
    base_d = nc.dram_tensor("base_d", [L // 8], F32)
    e1din = nc.dram_tensor("e1din", [HPC * U, D], BF16)
    e1dout = nc.dram_tensor("e1dout", [4 * HPC * U, D], BF16)
    e1iin = nc.dram_tensor("e1iin", [1, 224], F32)
    e1iout = nc.dram_tensor("e1iout", [4, 224], F32)
    e2in = nc.dram_tensor("e2in", [128, 8], F32)
    e2out = nc.dram_tensor("e2out", [512, 8], F32)
    woT_in2 = None

    probe_specs = {name: (shape, dtp) for (name, shape, dtp) in probes}
    probe_aps = {}
    for pn, (shape, dtp) in probe_specs.items():
        probe_aps[pn] = nc.dram_tensor("probe_" + pn, shape, dtp, kind="ExternalOutput").ap()
    probe_srcs = {}

    def reg_probe(name, ap):
        if name in probe_aps:
            probe_srcs[name] = ap

    es = contextlib.ExitStack()
    sb = lambda n, s, d: es.enter_context(nc.sbuf_tensor(n, s, d))
    pst = lambda n, s, d: es.enter_context(nc.psum_tensor(n, s, d))
    sem = lambda n: es.enter_context(nc.semaphore(n))

    with es:
        esA = contextlib.ExitStack()
        sbA = lambda n, s, d: esA.enter_context(nc.sbuf_tensor(n, s, d))
        wq_s = sb("wq_s", [128, KC * MSL], BF16)
        kT = [sb("kT0", [128, L], BF16), sb("kT1", [128, L], BF16)]
        vs_all = sb("vs_all", [128, 2 * 64 * 130], BF16)
        m_buf = sb("m_buf", [128, HPC * 64], F32)
        r_ml = sb("r_ml", [128, 64], F32)
        r8_ml = sb("r8_ml", [128, 64], F32)
        kst = [sb("kst%d" % h, [128, 51], BF16) for h in range(HPC)]
        drows = sb("drows", [1, HPC * 52], BF16)
        bvrow = sb("bvrow", [1, MSL], F32)
        borow = sb("borow", [1, D], F32)
        onesc = sb("onesc", [128, 1], BF16)
        ones_row = sb("ones_row", [1, 64], BF16)
        identf_s = sb("identf_s", [128, 128], F32)
        identb_s = sb("identb_s", [128, 128], BF16)
        iota_ml_s = sb("iota_ml_s", [128, 64], F32)
        iota_j_s = sb("iota_j_s", [128, 64], F32)
        xg = sb("xg", [128, D], F32)
        xg2 = sb("xg2", [4, 64], F32)
        xgn = sb("xgn", [128, D], BF16)
        xsampT = sb("xsampT", [128, KC * U], BF16)
        bn6 = sb("bn6", [128, 12], F32)
        bnag = sb("bnag", [128, 2], F32)
        bqcol = sb("bqcol", [128, HPC], BF16)
        kbar_f = sb("kbar_f", [128, 1], F32)
        idxk_s = sb("idxk_s", [U, 1], U32)
        scmax_v = sb("scmax_v", [128, 16], F32)
        epscol = sb("epscol", [128, 1], F32)

        wk_s = sbA("wk_s", [128, KC * MSL], BF16)
        wv_s = sbA("wv_s", [128, KC * MSL], BF16)
        statP = sbA("statP", [128, L], BF16)
        rrec_row = sbA("rrec_row", [1, L], BF16)
        fixq = sbA("fixq", [128, MSL], BF16)
        fixk = sbA("fixk", [128, MSL], BF16)
        fixv = sbA("fixv", [128, MSL], BF16)
        bkrow = sbA("bkrow", [1, MSL], BF16)
        xbf = [sbA("xbf0", [128, KC * TT], BF16), sbA("xbf1", [128, KC * TT], BF16)]
        qt_tmp = sbA("qt_tmp", [128, 4 * TT], BF16)
        stat_sum = sbA("stat_sum", [1, TSL], F32)
        stat_sq = sbA("stat_sq", [1, TSL], F32)
        sums_t = sbA("sums_t", [64, 128], F32)
        sumsq_t = sbA("sumsq_t", [64, 128], F32)
        mu_t = sbA("mu_t", [64, 128], F32)
        var_t = sbA("var_t", [64, 128], F32)
        r_t = sbA("r_t", [64, 128], F32)
        rrec_t = sbA("rrec_t", [64, 128], F32)
        ps_q = pst("ps_q", [128, 1024], F32)
        ps_k = pst("ps_k", [128, 1024], F32)
        ps_v = pst("ps_v", [128, 1024], F32)

        s_ld = sem("s_ld")
        s_gl = sem("s_gl")
        s_xl = [sem("s_xl0"), sem("s_xl1")]
        s_pv = sem("s_pv")
        s_ps = sem("s_ps")
        s_ptc = sem("s_ptc")
        s_ptt = sem("s_ptt")
        s_ptk = sem("s_ptk")
        s_ptd = sem("s_ptd")
        s_psk = sem("s_psk")
        s_pskb = sem("s_pskb")
        s_1v = sem("s_1v")
        s_1t = sem("s_1t")
        s_1q = sem("s_1q")
        s_1s = sem("s_1s")
        s_1g = sem("s_1g")
        s_cc = sem("s_cc")
        s_1r = sem("s_1r")
        s_e0 = sem("s_e0")
        s_kd = sem("s_kd")
        s_qd = sem("s_qd")
        s_vd = sem("s_vd")
        s_scp = sem("s_scp")
        s_evk = sem("s_evk")
        s_evq = sem("s_evq")
        s_evv = sem("s_evv")
        s_mst = sem("s_mst")
        s_st = sem("s_st")
        ch_v = sem("ch_v")
        s_mv = sem("s_mv")
        s_b0 = sem("s_b0")
        s_b1 = sem("s_b1")
        s_b2 = sem("s_b2")
        s_b3 = sem("s_b3")
        s_b4 = sem("s_b4")
        s_r0 = sem("s_r0")
        s_r1 = sem("s_r1")
        s_r2 = sem("s_r2")
        s_cmv = sem("s_cmv")
        s_cmt = sem("s_cmt")
        s_cms = sem("s_cms")
        s_vm = sem("s_vm")
        s_vms = sem("s_vms")
        s_bs = sem("s_bs")
        s_g2 = sem("s_g2")
        s_e1a = sem("s_e1a")
        s_vm8 = sem("s_vm8")
        s_bd = sem("s_bd")
        s_wo = sem("s_wo")
        s_wo2 = sem("s_wo2")
        ch_s = sem("ch_s")
        ch_g = sem("ch_g")
        s_4g = sem("s_4g")
        s_4w = sem("s_4w")
        s_4w2 = sem("s_4w2")
        s_4v = sem("s_4v")
        s_4a = sem("s_4a")
        s_4tt = sem("s_4tt")
        s_4sk = sem("s_4sk")
        s_4tq = sem("s_4tq")
        s_4sq = sem("s_4sq")
        s_4tb = sem("s_4tb")
        s_4se = sem("s_4se")
        s_4tc = sem("s_4tc")
        s_4tr = sem("s_4tr")
        s_4cv = sem("s_4cv")
        s_4ds = sem("s_4ds")
        s_4td = sem("s_4td")
        s_4dd = sem("s_4dd")
        s_4vc = sem("s_4vc")
        s_4vs = sem("s_4vs")
        s_5z = sem("s_5z")
        s_5d = sem("s_5d")
        s_5ix = sem("s_5ix")
        s_5it = sem("s_5it")
        s_5is = sem("s_5is")
        s_5m = sem("s_5m")
        s_5t = sem("s_5t")
        s_5v = sem("s_5v")
        s_6w = sem("s_6w")
        s_6wb = sem("s_6wb")
        s_6v = sem("s_6v")
        s_6t = sem("s_6t")
        s_6s = sem("s_6s")
        s_6rt = sem("s_6rt")
        s_6re = sem("s_6re")
        s_6y = sem("s_6y")
        s_6yg = sem("s_6yg")
        s_6ys = sem("s_6ys")
        s_6g = sem("s_6g")
        s_6gq = sem("s_6gq")
        s_6gt = sem("s_6gt")
        s_6gs = sem("s_6gs")
        s_7t = sem("s_7t")
        s_7e = sem("s_7e")
        s_7tf = sem("s_7tf")
        s_7v = sem("s_7v")
        s_7g = sem("s_7g")
        s_7c = sem("s_7c")
        s_8d = sem("s_8d")

        block = es.enter_context(nc.Block())

        def wchunk(w, k, lo=0, n=MSL):
            return w[:, k * MSL + lo:k * MSL + lo + n]

        with nc.psum_tensor("ps_trb", [128, 128], BF16) as ps_trb, \
             nc.psum_tensor("ps_tr", [128, 128], F32) as ps_tr:

            @block.sync
            def _(sync):
                sync.dma_start(out=identf_s[:, :], in_=identf_in).then_inc(s_ld, 16)
                sync.dma_start(out=iota_ml_s[:, :], in_=iota_ml_in).then_inc(s_ld, 16)
                sync.dma_start(out=iota_j_s[:, :], in_=iota_j_in).then_inc(s_ld, 16)
                sync.dma_start(out=bvrow[:, :], in_=bv_in).then_inc(s_ld, 16)
                sync.dma_start(out=borow[:, :], in_=bo_in).then_inc(s_ld, 16)
                sync.dma_start(out=xg2[0:4, 0:64], in_=bq_in.rearrange("o (h k) -> (o h) k", k=64)).then_inc(s_ld, 16)

            @block.gpsimd
            def _(gpsimd):
                gpsimd.dma_start(out=wq_s[:, :].rearrange("p (k m) -> p k m", m=MSL),
                                 in_=wqT.rearrange("(k p) m -> p k m", p=128)).then_inc(s_gl, 16)
                gpsimd.dma_start(out=wk_s[:, :].rearrange("p (k m) -> p k m", m=MSL),
                                 in_=wkT.rearrange("(k p) m -> p k m", p=128)).then_inc(s_gl, 16)
                gpsimd.dma_start(out=wv_s[:, :].rearrange("p (k m) -> p k m", m=MSL),
                                 in_=wvT.rearrange("(k p) m -> p k m", p=128)).then_inc(s_gl, 16)
                gpsimd.dma_start(out=identb_s[:, :], in_=identb_in).then_inc(s_gl, 16)
                gpsimd.dma_start(out=bkrow[:, :], in_=bk_in).then_inc(s_gl, 16)
                gpsimd.dma_start(out=idxk_s[:, :], in_=idxk_in.rearrange("(p o) -> p o", o=1)).then_inc(s_gl, 16)
                gpsimd.wait_ge(s_gl, 16 * 6)
                gpsimd.indirect_dma_start(
                    out=xg[0:U, :], out_offset=None, in_=xR,
                    in_offset=bass.IndirectOffsetOnAxis(idxk_s[:, 0:1], 0),
                ).then_inc(s_gl, 16)
                gpsimd.wait_ge(s_pv, 1)
                gpsimd.dma_start(out=fixq[1:2, :], in_=bq_in).then_inc(s_gl, 16)

            @block.vector
            def _(vector):
                vector.memset(onesc[:, :], 1.0)
                vector.memset(epscol[:, :], EPS)
                vector.memset(ones_row[:, :], 1.0)
                vector.memset(statP[:, :], 0.0)
                vector.memset(fixq[:, :], 0.0)
                vector.memset(fixk[:, :], 0.0)
                vector.memset(fixv[:, :], 0.0)
                vector.memset(drows[:, :], 0.0)
                for h in range(HPC):
                    vector.memset(kst[h][:, :], 0.0)
                vector.memset(bqcol[:, :], 0.0)
                vector.memset(vs_all[:, :].rearrange("p (r c a y) -> p r c a y", r=2, c=64, y=65)[:, :, :, :, 64:65],
                              1.0).then_inc(s_pv, 1)
                vector.wait_ge(s_gl, 128)
                vector.bn_stats(bn6[0:U, 0:6], xg[0:U, 0:512]).then_inc(ch_v, 1)
                vector.bn_stats(bn6[0:U, 6:12], xg[0:U, 512:1024]).then_inc(ch_v, 1)
                vector.wait_ge(ch_v, 2)
                vector.bn_aggr(bnag[0:U, :], bn6[0:U, :]).then_inc(s_pv, 1)
                vector.wait_ge(s_ps, 1)
                vector.tensor_scalar(xgn[0:U, :], xg[0:U, :], bnag[0:U, 0:1],
                                     bnag[0:U, 1:2], ALU.subtract, ALU.mult).then_inc(s_pv, 1)

            @block.scalar
            def _(scalar):
                scalar.wait_ge(s_pv, 2)
                scalar.activation(bnag[0:U, 1:2], bnag[0:U, 1:2], AF.Ln, bias=epscol[0:U, 0:1]).then_inc(ch_s, 1)
                scalar.wait_ge(ch_s, 1)
                scalar.activation(bnag[0:U, 1:2], bnag[0:U, 1:2], AF.Exp, scale=-0.5).then_inc(s_ps, 1)

            @block.tensor
            def _(tensor):
                tensor.wait_ge(s_gl, 128)
                tensor.wait_ge(s_pv, 1)
                for wi, wsb in enumerate((wq_s, wk_s, wv_s)):
                    if wi > 0:
                        tensor.wait_ge(s_ps, 1 + wi)
                    for k in range(KC):
                        mm = tensor.matmul(ps_v[0:1, 0:MSL], onesc[:, :], wchunk(wsb, k),
                                           start=(k == 0), stop=(k == KC - 1))
                    mm.then_inc(s_ptc, 1)
                tensor.wait_ge(s_pv, 3)
                for k in range(KC):
                    if k > 0:
                        tensor.wait_ge(s_psk, k)
                    tensor.transpose(ps_trb[:, 0:U], xgn[0:U, k * 128:(k + 1) * 128],
                                     identb_s[0:U, 0:U]).then_inc(s_ptt, 1)
                tensor.wait_ge(s_ld, 96)
                tensor.transpose(ps_tr[0:64, 0:4], xg2[0:4, 0:64],
                                 identf_s[0:4, 0:4]).then_inc(s_ptt, 1)
                tensor.wait_ge(s_psk, 8)
                for h in range(HPC):
                    if h > 0:
                        tensor.wait_ge(s_psk, 8 + 1 + h)
                    for k in range(KC):
                        tensor.matmul(ps_q[0:64, 0:U], wchunk(wk_s, k, h * 64, 64),
                                      xsampT[:, k * U:(k + 1) * U], start=(k == 0), stop=False)
                    tensor.matmul(ps_q[0:64, 0:U], bkrow[0:1, h * 64:(h + 1) * 64],
                                  ones_row[0:1, 0:U], start=False, stop=True).then_inc(s_ptk, 1)
                tensor.wait_ge(s_pskb, 1)
                for h in range(HPC):
                    if h > 0:
                        tensor.wait_ge(s_psk, 12 + 1 + h)
                    tensor.matmul(ps_q[0:1, 0:51], bqcol[:, h:h + 1],
                                  kst[h][:, 0:51], start=True,
                                  stop=True).then_inc(s_ptd, 1)

            @block.scalar
            def _(scalar):
                for wi, fx in enumerate((fixq, fixk, fixv)):
                    scalar.wait_ge(s_ptc, 1 + wi)
                    scalar.activation(fx[0:1, :], ps_v[0:1, 0:MSL], AF.Copy, scale=-1.0).then_inc(s_ps, 1)
                for k in range(KC):
                    scalar.wait_ge(s_ptt, 1 + k)
                    scalar.copy(xsampT[:, k * U:(k + 1) * U], ps_trb[:, 0:U]).then_inc(s_psk, 1)
                scalar.wait_ge(s_ptt, 9)
                scalar.wait_ge(s_pv, 1)
                scalar.copy(bqcol[0:64, 0:HPC].rearrange("p (a c) -> p a c", c=2)[:, :, 0:1],
                            ps_tr[0:64, 0:4].rearrange("p (a c) -> p a c", c=2)[:, :, 0:1])
                scalar.copy(bqcol[64:128, 0:HPC].rearrange("p (a c) -> p a c", c=2)[:, :, 1:2],
                            ps_tr[0:64, 0:4].rearrange("p (a c) -> p a c", c=2)[:, :, 1:2]).then_inc(s_psk, 1)
                for h in range(HPC):
                    odd = h % 2
                    scalar.wait_ge(s_ptk, 1 + h)
                    scalar.copy(kst[h][odd * 64:odd * 64 + 64, 0:50], ps_q[0:64, 0:U]).then_inc(s_psk, 1)
                for h in range(HPC):
                    scalar.wait_ge(s_ptd, 1 + h)
                    scalar.copy(drows[0:1, h * 52:h * 52 + 51], ps_q[0:1, 0:51]).then_inc(s_psk, 1)

            @block.vector
            def _(vector):
                vector.wait_ge(s_psk, 13)
                KB0, nkb = 2, 0
                for h in range(HPC):
                    rows = slice((h % 2) * 64, (h % 2) * 64 + 64)
                    vector.tensor_reduce(kbar_f[rows, 0:1], kst[h][rows, 0:50], AX.X, ALU.add).then_inc(ch_v, 1)
                    nkb += 1
                    vector.wait_ge(ch_v, KB0 + nkb)
                    ts = vector.tensor_scalar(kst[h][rows, 50:51], kbar_f[rows, 0:1], 1.0 / U, None, ALU.mult)
                ts.then_inc(s_pskb, 1)

        if upto >= 1:
            with nc.sbuf_tensor("sqtmp", [128, 2 * TT], BF16) as sqtmp:

                @block.gpsimd
                def _(gpsimd):
                    for j in range(4):
                        if j >= 2:
                            gpsimd.wait_ge(s_1t, 8 * (j - 1))
                        gpsimd.dma_start(out=xbf[j % 2][:, :].rearrange("p (k n) -> p k n", n=TT),
                                         in_=xTsl.rearrange("(k p) n -> p k n", p=128)[:, :, j * TT:(j + 1) * TT]
                                         ).then_inc(s_xl[j % 2], 16)

                @block.vector
                def _(vector):
                    for pc in range(32):
                        n, k = pc // KC, pc % KC
                        if k == 0:
                            vector.wait_ge(s_xl[n % 2], 16 * (n // 2 + 1))
                        if pc >= 2:
                            vector.wait_ge(s_1t, pc - 1)
                        xv = xbf[n % 2][:, k * TT:(k + 1) * TT]
                        vector.tensor_tensor(sqtmp[:, (pc % 2) * TT:(pc % 2 + 1) * TT],
                                             xv, xv, ALU.mult).then_inc(s_1v, 1)

                @block.tensor
                def _(tensor):
                    tensor.wait_ge(s_psk, 17)
                    for pc in range(32):
                        n, k = pc // KC, pc % KC
                        if k == 0 and n > 0:
                            tensor.wait_ge(s_1s, n)
                        tensor.wait_ge(s_1v, pc + 1)
                        tensor.matmul(ps_q[0:1, 0:TT], onesc[:, :], xbf[n % 2][:, k * TT:(k + 1) * TT],
                                      start=(k == 0), stop=(k == KC - 1)).then_inc(s_1q, 1)
                        tensor.matmul(ps_k[0:1, 0:TT], onesc[:, :],
                                      sqtmp[:, (pc % 2) * TT:(pc % 2 + 1) * TT],
                                      start=(k == 0), stop=(k == KC - 1)).then_inc(s_1t, 1)

                @block.scalar
                def _(scalar):
                    for n in range(4):
                        scalar.wait_ge(s_1t, (n + 1) * KC)
                        scalar.wait_ge(s_1q, (n + 1) * KC)
                        scalar.copy(stat_sum[0:1, n * TT:(n + 1) * TT], ps_q[0:1, 0:TT])
                        scalar.copy(stat_sq[0:1, n * TT:(n + 1) * TT], ps_k[0:1, 0:TT]).then_inc(s_1s, 1)

                @block.gpsimd
                def _(gpsimd):
                    gpsimd.wait_ge(s_1s, 4)
                    gpsimd.dma_start(out=e0in[0:1, :], in_=stat_sum[:, :]).then_inc(s_1g, 16)
                    gpsimd.dma_start(out=e0in[1:2, :], in_=stat_sq[:, :]).then_inc(s_1g, 16)
                    gpsimd.wait_ge(s_1g, 32)
                    gpsimd.collective_compute(
                        "AllGather", ALU.bypass, replica_groups=GROUPS,
                        ins=[e0in.ap().opt()], outs=[e0out.ap().opt()],
                    ).then_inc(s_cc, 1)
                    gpsimd.wait_ge(s_cc, 1)
                    for r in range(4):
                        gpsimd.dma_start(out=sums_t[16 * r:16 * (r + 1), :],
                                         in_=e0out[2 * r:2 * r + 1, :].rearrange("o (a p) -> (o a) p", p=128)
                                         ).then_inc(s_1g, 16)
                        gpsimd.dma_start(out=sumsq_t[16 * r:16 * (r + 1), :],
                                         in_=e0out[2 * r + 1:2 * r + 2, :].rearrange("o (a p) -> (o a) p", p=128)
                                         ).then_inc(s_1g, 16)
                    gpsimd.wait_ge(s_1g, 160)

                @block.vector
                def _(vector):
                    vector.wait_ge(s_1g, 160)
                    vector.tensor_scalar(mu_t[:, :], sums_t[:, :], 1.0 / D, None, ALU.mult).then_inc(ch_v, 1)
                    vector.tensor_scalar(var_t[:, :], sumsq_t[:, :], 1.0 / D, None, ALU.mult).then_inc(ch_v, 1)
                    vector.wait_ge(ch_v, 8)
                    vector.tensor_tensor(sums_t[:, :], mu_t[:, :], mu_t[:, :], ALU.mult).then_inc(ch_v, 1)
                    vector.wait_ge(ch_v, 9)
                    vector.tensor_tensor(var_t[:, :], var_t[:, :], sums_t[:, :],
                                         ALU.subtract).then_inc(s_1v, 1)

                @block.scalar
                def _(scalar):
                    scalar.wait_ge(s_1v, 33)
                    scalar.activation(r_t[:, :], var_t[:, :], AF.Ln, bias=epscol[0:64, 0:1]).then_inc(ch_s, 1)
                    scalar.wait_ge(ch_s, 2)
                    scalar.activation(r_t[:, :], r_t[:, :], AF.Exp, scale=-0.5).then_inc(s_1s, 1)

                @block.vector
                def _(vector):
                    vector.wait_ge(s_1s, 5)
                    vector.reciprocal(rrec_t[:, :], r_t[:, :]).then_inc(s_1v, 1)

                @block.tensor
                def _(tensor):
                    tensor.wait_ge(s_1s, 5)
                    tensor.transpose(ps_v[0:128, 0:64], r_t[:, :],
                                     identf_s[0:64, 0:64]).then_inc(s_1r, 1)

                @block.scalar
                def _(scalar):
                    scalar.wait_ge(s_1r, 1)
                    scalar.copy(r_ml[:, :], ps_v[0:128, 0:64])
                    scalar.activation(r8_ml[:, :], ps_v[0:128, 0:64], AF.Copy, scale=SCALE).then_inc(s_1s, 1)

                @block.gpsimd
                def _(gpsimd):
                    gpsimd.wait_ge(s_1v, 34)
                    gpsimd.dma_start(out=murow_d.rearrange("(a p) -> a p", p=128), in_=mu_t[:, :]).then_inc(s_1g, 16)
                    gpsimd.dma_start(out=rrec_d.rearrange("(a p) -> a p", p=128), in_=rrec_t[:, :]).then_inc(s_1g, 16)
                    gpsimd.wait_ge(s_1g, 192)
                    gpsimd.dma_start(out=statP[0:1, :], in_=murow_d.rearrange("(o n) -> o n", o=1)).then_inc(s_1g, 16)
                    gpsimd.dma_start(out=statP[1:2, :], in_=rrec_d.rearrange("(o n) -> o n", o=1)).then_inc(s_1g, 16)
                    gpsimd.wait_ge(s_1g, 224)
                    gpsimd.wait_ge(s_1s, 6)
                    gpsimd.engine_nop().then_inc(s_e0, 1)

        if upto >= 2:
            with nc.psum_tensor("ps_sc", [128, 1024], F32) as ps_sc:

                @block.gpsimd
                def _(gpsimd):
                    for t in range(NT):
                        if t == 0:
                            gpsimd.wait_ge(s_1t, 24)
                        elif t == 1:
                            gpsimd.wait_ge(s_1t, 32)
                        else:
                            gpsimd.wait_ge(s_vd, t - 1)
                        gpsimd.dma_start(out=xbf[t % 2][:, :].rearrange("p (k n) -> p k n", n=TT),
                                         in_=xT.rearrange("(k p) n -> p k n", p=128)[:, :, t * TT:(t + 1) * TT]
                                         ).then_inc(s_xl[t % 2], 16)

                @block.tensor
                def _(tensor):
                    tensor.wait_ge(s_e0, 1)
                    tensor.wait_ge(s_psk, 17)
                    for t in range(NT):
                        xt = xbf[t % 2][:, :].rearrange("p (k n) -> p k n", n=TT)
                        cs = slice(t * TT, (t + 1) * TT)
                        tensor.wait_ge(s_xl[t % 2], 16 * (t // 2 + 3))
                        if t >= 1:
                            tensor.wait_ge(s_evv, t)
                        for m in range(2):
                            for k in range(KC):
                                tensor.matmul(ps_k[:, m * TT:(m + 1) * TT],
                                              wchunk(wk_s, k, m * 128, 128), xt[:, k, :],
                                              start=(k == 0), stop=False)
                            mmk = tensor.matmul(ps_k[:, m * TT:(m + 1) * TT],
                                                fixk[:, m * 128:(m + 1) * 128], statP[:, cs],
                                                start=False, stop=True)
                        mmk.then_inc(s_kd, 1)
                        if t >= 1:
                            tensor.wait_ge(s_evq, t)
                        for m in range(2):
                            for k in range(KC):
                                tensor.matmul(ps_q[:, m * TT:(m + 1) * TT],
                                              wchunk(wq_s, k, m * 128, 128), xt[:, k, :],
                                              start=(k == 0), stop=False)
                            mmq = tensor.matmul(ps_q[:, m * TT:(m + 1) * TT],
                                                fixq[:, m * 128:(m + 1) * 128], statP[:, cs],
                                                start=False, stop=True)
                        mmq.then_inc(s_qd, 1)
                        tensor.wait_ge(s_evk, t + 1)
                        for s in range(4):
                            for k in range(KC):
                                tensor.matmul(ps_k[:, s * 256:(s + 1) * 256],
                                              xt[:, k, s * 128:(s + 1) * 128], wchunk(wv_s, k),
                                              start=(k == 0), stop=False)
                            mmv = tensor.matmul(ps_k[:, s * 256:(s + 1) * 256],
                                                statP[:, t * TT + s * 128:t * TT + (s + 1) * 128],
                                                fixv[:, :], start=False, stop=True)
                        mmv.then_inc(s_vd, 1)
                        if p2mode == 'kqv':
                            tensor.nop().then_inc(s_scp, 1)
                            continue
                        tensor.wait_ge(s_evq, t + 1)
                        if t >= 2 and p2mode == 'full':
                            tensor.wait_ge(s_mst, t - 1)
                        psS = ps_sc if t % 2 == 0 else ps_v
                        qslot = (t % 2) * 2 * TT
                        for s in range(4):
                            g, o = s // 2, (s % 2) * 208
                            for h in range(HPC):
                                hp = h // 2
                                mms = tensor.matmul(
                                    psS[:, g * 512 + o + h * 52:g * 512 + o + h * 52 + 51],
                                    qt_tmp[:, qslot + hp * TT + s * 128:qslot + hp * TT + (s + 1) * 128],
                                    kst[h][:, :],
                                    start=(h == 0), stop=(h == HPC - 1))
                        mms.then_inc(s_scp, 1)

                @block.scalar
                def _(scalar):
                    scalar.wait_ge(s_e0, 1)
                    for t in range(NT):
                        cs = slice(t * TT, (t + 1) * TT)
                        scalar.wait_ge(s_kd, t + 1)
                        scalar.copy(kT[0][:, cs], ps_k[:, 0:TT])
                        scalar.copy(kT[1][:, cs], ps_k[:, TT:2 * TT]).then_inc(s_evk, 1)
                        scalar.wait_ge(s_qd, t + 1)
                        if t >= 2:
                            scalar.wait_ge(s_scp, t - 1)
                        qslot = (t % 2) * 2 * TT
                        scalar.copy(qt_tmp[:, qslot:qslot + TT], ps_q[:, 0:TT])
                        scalar.copy(qt_tmp[:, qslot + TT:qslot + 2 * TT], ps_q[:, TT:2 * TT]).then_inc(s_evq, 1)
                        scalar.wait_ge(s_vd, t + 1)
                        for s in range(4):
                            c = t * 4 + s
                            av = scalar.activation(
                                vs_all[:, :].rearrange("p (r c a y) -> p r c a y", r=2, c=64, y=65)[:, :, c, :, 0:64],
                                ps_k[:, s * 256:(s + 1) * 256].rearrange("p (r a d) -> p r a d", r=2, d=64),
                                AF.Copy, scale=r_ml[:, c:c + 1])
                        av.then_inc(s_evv, 1)

                @block.vector
                def _(vector):
                    for t in range(NT):
                        if p2mode in ('kqv', 'scores'):
                            break
                        vector.wait_ge(s_scp, t + 1)
                        if t >= 1:
                            vector.wait_ge(s_mst, t)
                        psS = ps_sc if t % 2 == 0 else ps_v
                        for g in range(2):
                            vector.tensor_reduce(
                                scmax_v[:, g * 8:(g + 1) * 8].rearrange("p (a h) -> p a h", h=4),
                                psS[:, g * 512:g * 512 + 416].rearrange("p (a h u) -> p a h u", a=2, u=52)[:, :, :, 0:50],
                                AX.X, ALU.max).then_inc(s_mv, 1)
                            vector.wait_ge(s_mv, 4 * t + 2 * g + 1)
                            vector.tensor_tensor(
                                scmax_v[:, g * 8:(g + 1) * 8].rearrange("p (a h) -> p a h", h=4),
                                scmax_v[:, g * 8:(g + 1) * 8].rearrange("p (a h) -> p a h", h=4),
                                psS[:, g * 512:g * 512 + 416].rearrange("p (a h u) -> p a h u", a=2, u=52)[:, :, :, 50:51],
                                ALU.subtract).then_inc(s_mv, 1)
                        vector.wait_ge(s_mv, 4 * t + 4)
                        for s in range(4):
                            c = t * 4 + s
                            tsv = vector.tensor_scalar(
                                m_buf[:, :].rearrange("p (h c) -> p h c", c=64)[:, :, c:c + 1],
                                scmax_v[:, :].rearrange("p (a h) -> p a h", h=4)[:, s:s + 1, :].rearrange(
                                    "p a h -> p h a"),
                                r_ml[:, c:c + 1], None, ALU.mult)
                        tsv.then_inc(s_mst, 1)

        esA.close()

        x2T = sb("x2T", [128, KC * TSL], F32)
        onesf = sb("onesf", [1, 128], F32)
        esB = contextlib.ExitStack()
        sbB = lambda n, s, d: esB.enter_context(nc.sbuf_tensor(n, s, d))
        iota_sl_s = sbB("iota_sl_s", [128, TSL], F32)
        wo_b = sbB("wo_b", [128, 2 * D], F32)
        idxrows = sbB("idxrows", [1, 224], F32)
        idxu = sbB("idxu", [U, HPC], U32)
        vme = sbB("vme", [1, MSL], F32)
        vm8 = sbB("vm8", [8, 128], F32)
        vmT = sbB("vmT", [128, 8], F32)
        vmb = sbB("vmb", [128, 8], BF16)
        bc8 = sbB("bc8", [8, 128], F32)
        basecol = sbB("basecol", [128, 8], F32)
        baserow = sbB("baserow", [1, D], F32)
        sel32 = sbB("sel32", [128, HPC * 64], F32)
        rankb = sbB("rankb", [128, HPC * 64], F32)
        lo4 = sbB("lo4", [1, 4], F32)
        hi4 = sbB("hi4", [1, 4], F32)
        tau4 = sbB("tau4", [1, 4], F32)
        c4 = sbB("c4", [1, 4], F32)
        g4 = sbB("g4", [1, 4], F32)
        tmp4 = sbB("tmp4", [1, 4], F32)
        tau_rep = sbB("tau_rep", [128, 4], F32)
        maskb = sbB("maskb", [128, HPC * 64], BF16)
        rcomp = [sbB("rcomp%d" % i, [128, HPC * U], F32) for i in range(2)]
        tot4 = sbB("tot4", [128, 4], F32)
        totT = sbB("totT", [4, 128], F32)
        scanT = sbB("scanT", [4, 128], F32)
        zer4 = sbB("zer4", [4, 128], F32)
        offs4 = sbB("offs4", [128, 4], F32)
        woH = sbB("woH", [128, 2 * D], BF16)
        bqrow4 = sbB("bqrow4", [1, MSL], BF16)
        xtopT4 = sbB("xtopT4", [128, HPC * KC * U], BF16)
        qpair = [sbB("qpair0", [128, 2 * U], BF16), sbB("qpair1", [128, 2 * U], BF16)]
        expT = sbB("expT", [128, 4 * HPC * U], BF16)
        cc = sbB("cc", [65, 4 * U], F32)
        ccrec = sbB("ccrec", [1, 4 * U], F32)
        ccn = sbB("ccn", [64, 4 * U], F32)
        ccp = [sbB("ccp0", [128, U], BF16), sbB("ccp1", [128, U], BF16)]
        vmcl = sbB("vmcl", [128, 2], F32)
        vmcr = sbB("vmcr", [1, MSL], F32)
        dstage = [sbB("dst0", [U, D], BF16), sbB("dst1", [U, D], BF16)]
        idx7r = sbB("idx7r", [7, 128], F32)
        idxT7 = sbB("idxT7", [128, 7], F32)

        if upto >= 3:
            ps_tr2 = ps_sc
            cps = [ps_q[0:1, 0:512], ps_k[0:1, 0:512], ps_v[0:1, 0:512], ps_v[0:1, 512:1024]]

            @block.gpsimd
            def _(gpsimd):
                gpsimd.wait_ge(s_evv, NT)
                gpsimd.wait_ge(s_mst, NT)
                gpsimd.wait_ge(s_scp, NT)
                gpsimd.dma_start(out=x2T[:, :].rearrange("p (k n) -> p k n", n=TSL),
                                 in_=xTsl.rearrange("(k p) n -> p k n", p=128)).then_inc(s_g2, 16)
                gpsimd.dma_start(out=iota_sl_s[:, :], in_=iota_sl_in).then_inc(s_g2, 16)

            @block.tensor
            def _(tensor):
                tensor.wait_ge(s_evv, NT)
                tensor.wait_ge(s_mst, NT)
                for r in range(2):
                    for c in range(64):
                        mm = tensor.matmul(ps_tr2[0:1, r * 512:r * 512 + 130], onesc[:, :],
                                           vs_all[:, :].rearrange("p (q x) -> p q x", q=2)[:, r,
                                                                                           c * 130:(c + 1) * 130],
                                           start=(c == 0), stop=(c == 63))
                    mm.then_inc(s_vm, 1)

            @block.vector
            def _(vector):
                vector.wait_ge(s_vm, 2)
                for h in range(HPC):
                    r, odd = h // 2, h % 2
                    vector.tensor_scalar(vme[0:1, h * 64:(h + 1) * 64],
                                         ps_tr2[0:1, r * 512 + odd * 65:r * 512 + odd * 65 + 64],
                                         1.0 / L, None, ALU.mult).then_inc(ch_v, 1)
                vector.wait_ge(ch_v, 13)
                vector.tensor_tensor(vme[0:1, :], vme[0:1, :], bvrow[0:1, :], ALU.add).then_inc(s_vms, 1)
                vector.wait_ge(s_vms, 1)
                vector.tensor_tensor(vmcr[0:1, :], vme[0:1, :], bvrow[0:1, :], ALU.subtract).then_inc(s_4vc, 1)

            @block.gpsimd
            def _(gpsimd):
                gpsimd.wait_ge(s_vms, 1)
                gpsimd.dma_start(out=e1ain[:, :], in_=vme[:, :]).then_inc(s_e1a, 16)
                gpsimd.wait_ge(s_e1a, 16)
                gpsimd.collective_compute(
                    "AllGather", ALU.bypass, replica_groups=GROUPS,
                    ins=[e1ain.ap().opt()], outs=[e1aout.ap().opt()],
                ).then_inc(s_cc, 1)
                gpsimd.wait_ge(s_cc, 2)
                gpsimd.dma_start(out=vm8[:, :],
                                 in_=e1aout.rearrange("r (a p) -> (r a) p", p=128)).then_inc(s_vm8, 16)
                gpsimd.wait_ge(s_vm8, 16)

            @block.tensor
            def _(tensor):
                tensor.wait_ge(s_4vc, 1)
                for a in range(2):
                    tensor.transpose(ps_tr2[0:128, 360 + a:361 + a], vmcr[0:1, a * 128:(a + 1) * 128],
                                     identf_s[0:1, 0:1]).then_inc(s_4vc, 1)

            @block.scalar
            def _(scalar):
                for a in range(2):
                    scalar.wait_ge(s_4vc, 2 + a)
                    scalar.copy(vmcl[:, a:a + 1], ps_tr2[0:128, 360 + a:361 + a]).then_inc(s_4vs, 1)

            @block.vector
            def _(vector):
                vector.wait_ge(s_mst, NT)
                vector.memset(onesf[:, :], 1.0)
                vector.memset(lo4[:, :], -1.0)
                vector.memset(tau4[:, :], 31.5).then_inc(s_b0, 1)

            @block.tensor
            def _(tensor):
                tensor.wait_ge(s_vms, 1)
                tensor.wait_ge(s_4vs, 2)
                for i in range(NBIS + 1):
                    tensor.wait_ge(s_b0, i + 1)
                    src = tau4 if i < NBIS else lo4
                    tensor.matmul(ps_tr2[0:128, 216:220], onesf[0:1, :], src[0:1, :],
                                  start=True, stop=True).then_inc(s_b1, 1)
                    if i < NBIS:
                        tensor.wait_ge(s_b3, i + 1)
                        tensor.matmul(ps_tr2[0:1, 644:900], onesc[:, :], maskb[:, :],
                                      start=True, stop=True).then_inc(s_b4, 1)

            @block.scalar
            def _(scalar):
                for i in range(NBIS + 1):
                    scalar.wait_ge(s_b1, i + 1)
                    scalar.copy(tau_rep[:, :], ps_tr2[0:128, 216:220]).then_inc(s_b2, 1)

            @block.vector
            def _(vector):
                CH0 = 13
                for i in range(NBIS):
                    vector.wait_ge(s_b2, i + 1)
                    for h in range(HPC):
                        mv = vector.tensor_scalar(maskb[:, h * 64:(h + 1) * 64],
                                                  m_buf[:, h * 64:(h + 1) * 64],
                                                  tau_rep[:, h:h + 1], None, ALU.is_gt)
                    mv.then_inc(s_b3, 1)
                    nch = CH0 + 4 * i
                    w = 65.0 / (2.0 ** i)
                    vector.wait_ge(s_b4, i + 1)
                    vector.tensor_reduce(c4[0:1, :],
                                         ps_tr2[0:1, 644:900].rearrange("o (h c) -> o h c", c=64),
                                         AX.X, ALU.add).then_inc(ch_v, 1)
                    vector.wait_ge(ch_v, nch + 1)
                    vector.tensor_scalar(g4[0:1, :], c4[0:1, :], 50.0, None, ALU.is_ge).then_inc(ch_v, 1)
                    vector.wait_ge(ch_v, nch + 2)
                    vector.tensor_scalar(tmp4[0:1, :], g4[0:1, :], w * 0.5, None, ALU.mult).then_inc(ch_v, 1)
                    vector.wait_ge(ch_v, nch + 3)
                    vector.tensor_tensor(lo4[0:1, :], lo4[0:1, :], tmp4[0:1, :], ALU.add).then_inc(ch_v, 1)
                    vector.wait_ge(ch_v, nch + 4)
                    vector.tensor_scalar(tau4[0:1, :], lo4[0:1, :], w * 0.25, None, ALU.add).then_inc(s_b0, 1)

            CHR = 14 + 4 * NBIS

            @block.tensor
            def _(tensor):
                tensor.wait_ge(s_vm8, 16)
                tensor.transpose(ps_tr2[0:128, 200:208], vm8[:, :], identf_s[0:8, 0:8]).then_inc(s_bs, 1)

            @block.scalar
            def _(scalar):
                scalar.wait_ge(s_bs, 1)
                scalar.copy(vmT[:, 0:8], ps_tr2[0:128, 200:208])
                scalar.copy(vmb[:, 0:8], ps_tr2[0:128, 200:208]).then_inc(s_bs, 1)

            @block.sync
            def _(sync):
                sync.wait_ge(s_scp, NT)
                sync.wait_ge(s_mst, NT)
                sync.wait_ge(s_evv, NT)
                for k in range(KC):
                    if k >= 2:
                        sync.wait_ge(s_wo2, k - 1)
                    sync.dma_start(out=wo_b[:, (k % 2) * D:(k % 2 + 1) * D],
                                   in_=woT_in[k * 128:(k + 1) * 128, :]).then_inc(s_wo, 16)

            @block.tensor
            def _(tensor):
                tensor.wait_ge(s_bs, 2)
                for k in range(KC):
                    tensor.wait_ge(s_wo, 32 * (k // 2 + 1))
                    tensor.matmul(cps[3][0:1, 0:TT], vmT[:, k:k + 1], wo_b[:, (k % 2) * D:(k % 2) * D + TT],
                                  start=(k == 0), stop=(k == KC - 1))
                    tensor.matmul(cps[2][0:1, 0:TT], vmT[:, k:k + 1], wo_b[:, (k % 2) * D + TT:(k % 2 + 1) * D],
                                  start=(k == 0), stop=(k == KC - 1)).then_inc(s_wo2, 1)
                tensor.nop().then_inc(s_bs, 2)

            @block.vector
            def _(vector):
                vector.wait_ge(s_bs, 4)
                vector.wait_ge(s_wo2, 8)
                vector.tensor_tensor(baserow[0:1, 0:TT], cps[3][0:1, 0:TT], borow[0:1, 0:TT],
                                     ALU.add).then_inc(ch_v, 1)
                vector.wait_ge(ch_v, 14 + 4 * NBIS)
                vector.tensor_tensor(baserow[0:1, TT:D], cps[2][0:1, 0:TT], borow[0:1, TT:D],
                                     ALU.add).then_inc(s_bs, 1)

            @block.gpsimd
            def _(gpsimd):
                gpsimd.wait_ge(s_bs, 5)
                gpsimd.dma_start(out=base_d.rearrange("(o n) -> o n", o=1), in_=baserow[:, :]).then_inc(s_bd, 16)
                gpsimd.wait_ge(s_bd, 16)
                gpsimd.dma_start(out=bc8[:, :], in_=base_d.rearrange("(a p) -> a p", p=128)).then_inc(s_bd, 16)
                gpsimd.wait_ge(s_bd, 32)

            @block.tensor
            def _(tensor):
                tensor.wait_ge(s_bd, 32)
                tensor.transpose(ps_tr2[0:128, 208:216], bc8[:, :], identf_s[0:8, 0:8]).then_inc(s_bs, 1)

            @block.scalar
            def _(scalar):
                scalar.wait_ge(s_bs, 6)
                scalar.copy(basecol[:, 0:8], ps_tr2[0:128, 208:216]).then_inc(s_bs, 1)

            @block.vector
            def _(vector):
                vector.wait_ge(s_bs, 7)
                vector.wait_ge(s_g2, 32)
                for k in range(KC):
                    tv = vector.tensor_scalar(x2T[:, k * TSL:(k + 1) * TSL], x2T[:, k * TSL:(k + 1) * TSL],
                                              basecol[:, k:k + 1], None, ALU.add)
                tv.then_inc(s_bs, 1)

            @block.vector
            def _(vector):
                vector.wait_ge(s_b2, NBIS + 1)
                for h in range(HPC):
                    sv = vector.tensor_scalar(sel32[:, h * 64:(h + 1) * 64], m_buf[:, h * 64:(h + 1) * 64],
                                              tau_rep[:, h:h + 1], None, ALU.is_gt)
                sv.then_inc(ch_v, 1)
                vector.wait_ge(ch_v, CHR + 1)
                vector.tensor_copy(rankb[:, :], sel32[:, :]).then_inc(ch_v, 1)
                nch = CHR + 2
                for d in (1, 2, 4, 8, 16, 32):
                    vector.wait_ge(ch_v, nch)
                    vector.tensor_tensor(
                        rankb[:, :].rearrange("p (h c) -> p h c", c=64)[:, :, d:64],
                        rankb[:, :].rearrange("p (h c) -> p h c", c=64)[:, :, d:64],
                        rankb[:, :].rearrange("p (h c) -> p h c", c=64)[:, :, 0:64 - d],
                        ALU.add).then_inc(ch_v, 1)
                    nch += 1
                vector.wait_ge(ch_v, CHR + 8)
                vector.tensor_copy(tot4[:, :],
                                   rankb[:, :].rearrange("p (h c) -> p h c", c=64)[:, :, 63]).then_inc(s_r0, 1)

            @block.tensor
            def _(tensor):
                tensor.wait_ge(s_r0, 1)
                tensor.transpose(ps_tr2[0:4, 220:348], tot4[:, :], identf_s[:, :]).then_inc(s_r1, 1)

            @block.scalar
            def _(scalar):
                scalar.wait_ge(s_r1, 1)
                scalar.copy(totT[0:4, :], ps_tr2[0:4, 220:348]).then_inc(s_r2, 1)

            @block.vector
            def _(vector):
                vector.wait_ge(s_r2, 1)
                vector.memset(zer4[0:4, :], 0.0).then_inc(ch_v, 1)
                vector.wait_ge(ch_v, CHR + 9)
                vector.tensor_tensor_scan(scanT[0:4, :], totT[0:4, :], zer4[0:4, :], 0.0,
                                          ALU.add, ALU.add).then_inc(ch_v, 1)
                vector.wait_ge(ch_v, CHR + 10)
                vector.tensor_tensor(scanT[0:4, :], scanT[0:4, :], totT[0:4, :],
                                     ALU.subtract).then_inc(s_r0, 1)

            @block.tensor
            def _(tensor):
                tensor.wait_ge(s_r0, 2)
                tensor.transpose(ps_tr2[0:128, 348:352], scanT[0:4, :], identf_s[0:4, 0:4]).then_inc(s_r1, 1)

            @block.scalar
            def _(scalar):
                scalar.wait_ge(s_r1, 2)
                scalar.copy(offs4[:, :], ps_tr2[0:128, 348:352]).then_inc(s_r2, 1)

            @block.vector
            def _(vector):
                vector.wait_ge(s_r2, 2)
                for h in range(HPC):
                    rv = vector.tensor_scalar(rankb[:, h * 64:(h + 1) * 64], rankb[:, h * 64:(h + 1) * 64],
                                              offs4[:, h:h + 1], None, ALU.add)
                rv.then_inc(ch_v, 1)
                vector.wait_ge(ch_v, CHR + 11)
                vector.tensor_tensor(rankb[:, :], rankb[:, :], sel32[:, :], ALU.subtract).then_inc(ch_v, 1)
                vector.wait_ge(ch_v, CHR + 12)
                vector.tensor_scalar(rankb[:, :], rankb[:, :], BIG, None, ALU.add).then_inc(ch_v, 1)
                vector.wait_ge(ch_v, CHR + 13)
                vector.scalar_tensor_tensor(rankb[:, :], sel32[:, :], -BIG, rankb[:, :],
                                            ALU.mult, ALU.add).then_inc(ch_v, 1)
                vector.wait_ge(ch_v, CHR + 14)
                mvm = None
                for h in range(HPC):
                    mvm = vector.tensor_tensor(sel32[:, h * 64:(h + 1) * 64], sel32[:, h * 64:(h + 1) * 64],
                                               iota_ml_s[:, :], ALU.mult)
                mvm.then_inc(s_r0, 1)

            @block.vector
            def _(vector):
                vector.wait_ge(s_r0, 3)
                for c in range(64):
                    if c >= 2:
                        vector.wait_ge(s_cmt, (c - 1) * HPC)
                    rv = None
                    for h in range(HPC):
                        rv = vector.tensor_scalar(
                            rcomp[c % 2][:, h * U:(h + 1) * U], iota_j_s[:, 0:U],
                            rankb[:, h * 64 + c:h * 64 + c + 1], None, ALU.is_equal)
                    rv.then_inc(s_cmv, 1)

            @block.tensor
            def _(tensor):
                tensor.wait_ge(s_bs, 5)
                for c in range(64):
                    tensor.wait_ge(s_cmv, c + 1)
                    for h in range(HPC):
                        tensor.matmul(cps[h][0:1, 0:U], sel32[:, h * 64 + c:h * 64 + c + 1],
                                      rcomp[c % 2][:, h * U:(h + 1) * U],
                                      start=(c == 0), stop=(c == 63)).then_inc(s_cmt, 1)

            @block.vector
            def _(vector):
                vector.memset(idxrows[:, :], BIG).then_inc(s_cmv, 1)

            @block.scalar
            def _(scalar):
                scalar.wait_ge(s_cmt, 64 * HPC)
                scalar.wait_ge(s_cmv, 65)
                sc = None
                for h in range(HPC):
                    sc = scalar.copy(idxrows[0:1, h * U:(h + 1) * U], cps[h][0:1, 0:U])
                sc.then_inc(s_cms, 1)

            @block.tensor
            def _(tensor):
                tensor.wait_ge(s_cms, 1)
                for h in range(HPC):
                    tensor.transpose(ps_tr2[0:U, 352 + h * 2:352 + h * 2 + 1],
                                     idxrows[0:1, h * U:(h + 1) * U],
                                     identf_s[0:1, 0:1]).then_inc(s_r1, 1)

            @block.scalar
            def _(scalar):
                sc = None
                for h in range(HPC):
                    scalar.wait_ge(s_r1, 3 + h)
                    sc = scalar.copy(idxu[:, h:h + 1], ps_tr2[0:U, 352 + h * 2:352 + h * 2 + 1])
                sc.then_inc(s_cms, 1)

        if upto >= 4:
            def pslot(j):
                t = ps_q if j < 2 else ps_k
                return t[:, (j % 2) * 512:(j % 2) * 512 + 200]

            @block.gpsimd
            def _(gpsimd):
                gpsimd.wait_ge(s_cms, 2)
                gpsimd.dma_start(out=bqrow4[0:1, :], in_=bq_in).then_inc(s_4w, 16)
                gpsimd.dma_start(out=woH[:, :].rearrange("p (r d) -> p r d", d=D),
                                 in_=woR_in.rearrange("(r p) d -> p r d", p=128)).then_inc(s_4w2, 16)
                for h in range(HPC):
                    if h > 0:
                        gpsimd.wait_ge(s_4tt, h * KC)
                    gpsimd.indirect_dma_start(
                        out=xg[0:U, :], out_offset=None, in_=xR,
                        in_offset=bass.IndirectOffsetOnAxis(idxu[:, h:h + 1], 0),
                    ).then_inc(s_4g, 16)

            @block.vector
            def _(vector):
                vector.memset(qpair[0][:, :], 0.0)
                vector.memset(qpair[1][:, :], 0.0)
                vector.memset(ccp[0][:, :], 0.0)
                vector.memset(ccp[1][:, :], 0.0).then_inc(s_4v, 1)
                for h in range(HPC):
                    vector.wait_ge(s_4g, 16 * (h + 1))
                    vector.bn_stats(bn6[0:U, 0:6], xg[0:U, 0:512])
                    vector.bn_stats(bn6[0:U, 6:12], xg[0:U, 512:1024]).then_inc(s_4v, 1)
                    vector.wait_ge(s_4v, 3 * h + 2)
                    vector.bn_aggr(bnag[0:U, :], bn6[0:U, :]).then_inc(s_4v, 1)
                    vector.wait_ge(s_4a, 2 * h + 2)
                    vector.tensor_scalar(xg[0:U, :], xg[0:U, :], bnag[0:U, 0:1],
                                         bnag[0:U, 1:2], ALU.subtract, ALU.mult).then_inc(s_4v, 1)

            @block.tensor
            def _(tensor):
                tensor.wait_ge(s_cms, 1)
                tensor.wait_ge(s_4w, 16)
                for h in range(HPC):
                    tensor.wait_ge(s_4v, 3 * h + 4)
                    for k in range(KC):
                        if h * KC + k > 0:
                            tensor.wait_ge(s_4sk, h * KC + k)
                        tensor.transpose(ps_q[0:128, 0:U], xg[0:U, k * 128:(k + 1) * 128],
                                         identf_s[0:U, 0:U]).then_inc(s_4tt, 1)
                    if h > 0:
                        tensor.wait_ge(s_4sq, h)
                    for k in range(KC):
                        tensor.wait_ge(s_4sk, h * KC + k + 1)
                        tensor.matmul(ps_v[0:64, 512:562], wchunk(wq_s, k, h * 64, 64),
                                      xtopT4[:, (h * KC + k) * U:(h * KC + k + 1) * U],
                                      start=(k == 0), stop=False)
                    tensor.matmul(ps_v[0:64, 512:562], bqrow4[0:1, h * 64:(h + 1) * 64],
                                  ones_row[0:1, 0:U], start=False, stop=True).then_inc(s_4tq, 1)

            @block.scalar
            def _(scalar):
                for h in range(HPC):
                    scalar.wait_ge(s_4v, 3 * h + 3)
                    scalar.activation(bnag[0:U, 1:2], bnag[0:U, 1:2], AF.Ln,
                                      bias=epscol[0:U, 0:1]).then_inc(s_4a, 1)
                    scalar.wait_ge(s_4a, 2 * h + 1)
                    scalar.activation(bnag[0:U, 1:2], bnag[0:U, 1:2], AF.Exp,
                                      scale=-0.5).then_inc(s_4a, 1)
                    for k in range(KC):
                        scalar.wait_ge(s_4tt, h * KC + k + 1)
                        scalar.copy(xtopT4[:, (h * KC + k) * U:(h * KC + k + 1) * U],
                                    ps_q[0:128, 0:U]).then_inc(s_4sk, 1)
                    scalar.wait_ge(s_4tq, h + 1)
                    scalar.wait_ge(s_4v, 1)
                    scalar.copy(qpair[h // 2][(h % 2) * 64:(h % 2) * 64 + 64, (h % 2) * U:(h % 2) * U + U],
                                ps_v[0:64, 512:562]).then_inc(s_4sq, 1)

            @block.tensor
            def _(tensor):
                tensor.wait_ge(s_4sq, 4)
                LAG = 2
                for ci in range(64 + LAG):
                    if ci < 64:
                        if ci >= 4:
                            tensor.wait_ge(s_4se, ci - 3)
                        psl = pslot(ci % 4)
                        for r in range(2):
                            mm = tensor.matmul(psl[:, r * 100:(r + 1) * 100],
                                               kT[r][:, ci * 128:(ci + 1) * 128], qpair[r][:, :],
                                               start=(r == 0), stop=(r == 1))
                        mm.then_inc(s_4tb, 1)
                    j = ci - LAG
                    if j >= 0:
                        tensor.wait_ge(s_4se, j + 1)
                        for h in range(HPC):
                            r, odd = h // 2, h % 2
                            o = r * 8320 + j * 130 + odd * 65
                            mm = tensor.matmul(
                                ps_v[0:65, h * U:(h + 1) * U],
                                vs_all[:, o:o + 65],
                                expT[:, (j % 4) * 200 + r * 100 + odd * U:
                                        (j % 4) * 200 + r * 100 + odd * U + U],
                                start=(j == 0 and h == 0), stop=(j == 63 and h == 3))
                        mm.then_inc(s_4tc, 1)

            @block.scalar
            def _(scalar):
                for ci in range(64):
                    scalar.wait_ge(s_4tb, ci + 1)
                    if ci >= 4:
                        scalar.wait_ge(s_4tc, ci - 3)
                    scalar.activation(expT[:, (ci % 4) * 200:(ci % 4) * 200 + 200],
                                      pslot(ci % 4)[:, 0:200], AF.Exp,
                                      scale=r8_ml[:, ci:ci + 1]).then_inc(s_4se, 1)

            @block.scalar
            def _(scalar):
                scalar.wait_ge(s_4tc, 64)
                scalar.copy(cc[0:65, :], ps_v[0:65, 0:4 * U]).then_inc(s_4ds, 1)
                for h in range(HPC):
                    scalar.wait_ge(s_4td, h + 1)
                    if h >= 2:
                        scalar.wait_ge(s_4dd, 32)
                    scalar.copy(dstage[h % 2][0:U, :], ps_sc[0:U, 0:D]).then_inc(s_4ds, 1)

            @block.vector
            def _(vector):
                vector.wait_ge(s_4vs, 2)
                vector.wait_ge(s_4ds, 1)
                vector.reciprocal(ccrec[0:1, :], cc[64:65, :]).then_inc(s_4cv, 1)
                vector.wait_ge(s_4tr, 1)
                vector.tensor_tensor(ccn[:, :], cc[0:64, :], ps_v[0:64, 562:762],
                                     ALU.mult).then_inc(s_4cv, 1)
                for h in range(HPC):
                    vector.wait_ge(s_4cv, 2)
                    if h >= 2:
                        vector.wait_ge(s_4td, h - 1)
                    vector.tensor_scalar(ccp[h % 2][(h % 2) * 64:(h % 2) * 64 + 64, :],
                                         ccn[:, h * U:(h + 1) * U],
                                         vmcl[(h % 2) * 64:(h % 2) * 64 + 64, (h // 2):(h // 2) + 1],
                                         None, ALU.subtract).then_inc(s_4cv, 1)

            @block.tensor
            def _(tensor):
                tensor.wait_ge(s_cms, 2)
                tensor.wait_ge(s_4w2, 16)
                tensor.wait_ge(s_4cv, 1)
                tensor.matmul(ps_v[0:64, 562:762], onesf[0:1, 0:64], ccrec[0:1, :],
                              start=True, stop=True).then_inc(s_4tr, 1)
                for h in range(HPC):
                    tensor.wait_ge(s_4cv, 3 + h)
                    if h > 0:
                        tensor.wait_ge(s_4ds, h + 1)
                    for n in range(2):
                        mm = tensor.matmul(ps_sc[0:U, n * 512:(n + 1) * 512], ccp[h % 2][:, :],
                                           woH[:, (h // 2) * D + n * 512:(h // 2) * D + n * 512 + 512],
                                           start=True, stop=True)
                    mm.then_inc(s_4td, 1)

            @block.gpsimd
            def _(gpsimd):
                gpsimd.wait_ge(s_cms, 1)
                gpsimd.dma_start(out=e1iin.ap(), in_=idxrows[0:1, :]).then_inc(ch_g, 16)
                gpsimd.wait_ge(ch_g, 16)
                gpsimd.collective_compute(
                    "AllGather", ALU.bypass, replica_groups=GROUPS,
                    ins=[e1iin.ap().opt()], outs=[e1iout.ap().opt()]).then_inc(s_cc, 1)
                for h in range(HPC):
                    gpsimd.wait_ge(s_4ds, h + 2)
                    gpsimd.dma_start(out=e1din.ap()[h * U:(h + 1) * U, :],
                                     in_=dstage[h % 2][0:U, :]).then_inc(s_4dd, 16)
                gpsimd.wait_ge(s_4dd, 64)
                gpsimd.collective_compute(
                    "AllGather", ALU.bypass, replica_groups=GROUPS,
                    ins=[e1din.ap().opt()], outs=[e1dout.ap().opt()]).then_inc(s_cc, 1)

        if upto >= 5:
            pieces = []
            for r4 in range(4):
                c0 = 0
                while c0 < 200:
                    f = r4 * 224 + c0
                    t, p0 = f // 128, f % 128
                    n = min(128 - p0, 200 - c0)
                    pieces.append((r4, c0, t, p0, n))
                    c0 += n
            assert sum(n for *_, n in pieces) == 800 and len(pieces) == 10

            def dtile(t, k):
                return vs_all[0:128, t * 1024 + k * 128:t * 1024 + (k + 1) * 128]

            def tl5(t, n=None):
                base = (t % 4) * TSL
                if n is None:
                    return kT[t // 4][:, base:base + TSL]
                return kT[t // 4][:, base + n * 512:base + (n + 1) * 512]

            def ps5(k, n):
                if k % 2 == 0:
                    tns = ps_q if n < 2 else ps_k
                else:
                    tns = ps_v if n < 2 else ps_sc
                return tns[:, (n % 2) * 512:(n % 2) * 512 + 512]

            @block.sync
            def _(sync):
                sync.wait_ge(s_cc, 3)
                sync.dma_start(out=idx7r[0:7, :],
                               in_=e1iout.ap().rearrange("r c -> (r c)").rearrange(
                                   "(t p) -> t p", p=128)).then_inc(s_5ix, 16)
                sync.wait_ge(s_cc, 4)
                sync.wait_ge(s_5z, 1)
                for (r4, c0, t, p0, n) in pieces:
                    sync.dma_start(out=vs_all[p0:p0 + n, t * 1024:(t + 1) * 1024],
                                   in_=e1dout.ap()[r4 * 200 + c0:r4 * 200 + c0 + n, :]
                                   ).then_inc(s_5d, 16)

            @block.tensor
            def _(tensor):
                tensor.wait_ge(s_5ix, 16)
                tensor.wait_ge(s_4ds, 5)
                tensor.transpose(ps_sc[0:128, 364:371], idx7r[0:7, :],
                                 identf_s[0:7, 0:7]).then_inc(s_5it, 1)

            @block.scalar
            def _(scalar):
                scalar.wait_ge(s_5it, 1)
                scalar.copy(idxT7[:, :], ps_sc[0:128, 364:371]).then_inc(s_5is, 1)

            @block.vector
            def _(vector):
                vector.wait_ge(s_4tc, 64)
                vector.memset(vs_all[:, 0:7168], 0.0).then_inc(s_5z, 1)
                vector.wait_ge(s_4tb, 64)
                vector.wait_ge(s_5is, 1)
                for t in range(7):
                    vector.tensor_scalar(tl5(t), iota_sl_s[:, :], idxT7[:, t:t + 1],
                                         None, ALU.is_equal).then_inc(s_5m, 1)

            @block.tensor
            def _(tensor):
                tensor.wait_ge(s_5m, 7)
                tensor.wait_ge(s_5d, 160)
                tensor.wait_ge(s_4se, 64)
                tensor.wait_ge(s_4cv, 6)
                for k in range(KC):
                    if k >= 2:
                        tensor.wait_ge(s_5v, 4 * (k - 1))
                    for t in range(7):
                        for n in range(4):
                            mm = tensor.matmul(ps5(k, n), dtile(t, k), tl5(t, n),
                                               start=(t == 0), stop=(t == 6))
                            if t == 6:
                                mm.then_inc(s_5t, 1)

            @block.vector
            def _(vector):
                for k in range(KC):
                    for n in range(4):
                        vector.wait_ge(s_5t, 4 * k + n + 1)
                        vector.tensor_tensor(x2T[:, k * TSL + n * 512:k * TSL + (n + 1) * 512],
                                             x2T[:, k * TSL + n * 512:k * TSL + (n + 1) * 512],
                                             ps5(k, n), ALU.add).then_inc(s_5v, 1)

        if upto >= 6:
            esB.close()
            w1_s = sb("w1_s", [128, KC * D], BF16)
            w2_s = kT[1][:, 4096:8192]
            w3_s = kT[0][:, 7168:8192]
            w4_s = sb("w4_s", [128, 2], BF16)
            w4p = sb("w4p", [128, 256], BF16)
            gw2_s = wq_s[:, 0:1024]
            gw1q = wq_s[:, 1024:2048]
            gw3_s = sb("gw3_s", [128, 2], BF16)
            b1c_s = sb("b1c_s", [128, 8], F32)
            b2c_s = sb("b2c_s", [128, 4], F32)
            b3c_s = sb("b3c_s", [128, 2], F32)
            b4c_s = sb("b4c_s", [1, 1], F32)
            gb1c_s = sb("gb1c_s", [128, 4], F32)
            gb2c_s = sb("gb2c_s", [128, 2], F32)
            gb3c_s = sb("gb3c_s", [1, 1], F32)
            ar_s = sb("ar_s", [1, 1], F32)
            rowA = sb("rowA", [1, TSL], F32)
            rowB = sb("rowB", [1, TSL], F32)
            sqt = kT[1][:, 0:4096]
            rrep_s = sb("rrep_s", [128, 1024], F32)
            murr_s = sb("murr_s", [128, 1024], F32)
            ysum_c = sb("ysum_c", [128, 8], F32)
            ysum_c4 = sb("ysum_c4", [128, 32], F32)
            ysum8 = sb("ysum8", [128, 32], F32)
            gcc = sb("gcc", [128, 8], F32)
            gccb = sb("gccb", [128, 8], BF16)
            g1cb = sb("g1cb", [128, 4], BF16)
            g2cb = sb("g2cb", [128, 2], BF16)
            gfab = sb("gfab", [1, 4], F32)
            tfrow = sb("tfrow", [1, TT], F32)
            crowP = sb("crowP", [128, TT], F32)
            onesf128 = sb("onesf128", [128, 128], F32)
            crep_s = sb("crep_s", [128, 2 * TT], F32)
            tw = [rrep_s[:, 0:512], rrep_s[:, 512:1024],
                  murr_s[:, 0:512], murr_s[:, 512:1024]] + \
                 [sb("tw%d" % i, [128, TT], F32) for i in range(4, 8)]
            outbf = vs_all
            h1_s = kT[0]
            h2_s = kT[0]
            h3_s = kT[0]

            @block.gpsimd
            def _(gpsimd):
                gpsimd.wait_ge(s_5t, 32)
                gpsimd.wait_ge(s_cc, 4)
                gpsimd.dma_start(out=w1_s[:, :].rearrange("p (k j) -> p k j", j=D),
                                 in_=cw1_in.rearrange("(k p) j -> p k j", p=128)).then_inc(s_6w, 16)
                gpsimd.dma_start(out=w2_s.rearrange("p (k j) -> p k j", j=512),
                                 in_=cw2_in.rearrange("(k p) j -> p k j", p=128)).then_inc(s_6w, 16)
                gpsimd.dma_start(out=w3_s.rearrange("p (k j) -> p k j", j=256),
                                 in_=cw3_in.rearrange("(k p) j -> p k j", p=128)).then_inc(s_6w, 16)
                gpsimd.dma_start(out=w4_s[:, :], in_=cw4_in).then_inc(s_6w, 16)
                gpsimd.wait_ge(s_6v, 17)
                gpsimd.dma_start(out=w4p[:, :].rearrange("p (k c) -> p k c", c=128)[:, :, 0:1],
                                 in_=cw4_in.rearrange("p (k o) -> p k o", o=1),
                                 allow_slow_non_contiguous=True).then_inc(s_6w, 16)
                gpsimd.dma_start(out=gw2_s.rearrange("p (k j) -> p k j", j=256),
                                 in_=gw2_in.rearrange("(k p) j -> p k j", p=128)).then_inc(s_6w, 16)
                gpsimd.dma_start(out=gw3_s[:, :], in_=gw3_in).then_inc(s_6w, 16)

            @block.sync
            def _(sync):
                sync.wait_ge(s_5t, 32)
                for t_, src in ((b1c_s, b1c_in), (b2c_s, b2c_in), (b3c_s, b3c_in), (b4c_s, b4c_in),
                                (gb1c_s, gb1c_in), (gb2c_s, gb2c_in), (gb3c_s, gb3c_in), (ar_s, ar_in)):
                    sync.dma_start(out=t_[:, :], in_=src).then_inc(s_6wb, 16)
                for k in range(4):
                    sync.wait_ge(s_5v, 4 * (k + 1))
                    sync.dma_start(out=out.rearrange("(k p) n -> k p n", p=128)[k, :, :],
                                   in_=x2T[:, k * TSL:(k + 1) * TSL]).then_inc(s_ld, 16)

            @block.scalar
            def _(scalar):
                for k in range(4, KC):
                    scalar.wait_ge(s_5v, 4 * (k + 1))
                    scalar.dma_start(out=out.rearrange("(k p) n -> k p n", p=128)[k, :, :],
                                     in_=x2T[:, k * TSL:(k + 1) * TSL]).then_inc(s_5ix, 16)

            @block.vector
            def _(vector):
                vector.memset(w4p[:, :], 0.0)
                vector.memset(crowP[:, :], 0.0)
                vector.memset(onesf128[:, :], 0.0).then_inc(s_6v, 16)
                vector.wait_ge(s_6v, 16)
                vector.memset(onesf128[0:1, :], 1.0).then_inc(s_6v, 1)
                vector.wait_ge(s_5t, 32)
                for k in range(KC):
                    vector.wait_ge(s_5v, 4 * (k + 1))
                    vector.tensor_copy(outbf[:, k * TSL:(k + 1) * TSL],
                                       x2T[:, k * TSL:(k + 1) * TSL]).then_inc(s_6v, 1)
                    if k >= 2:
                        vector.wait_ge(s_6t, k - 1)
                    vector.wait_ge(s_6v, 17 + 2 * k + 1)
                    vector.tensor_tensor(sqt[:, (k % 2) * TSL:(k % 2 + 1) * TSL],
                                         outbf[:, k * TSL:(k + 1) * TSL],
                                         outbf[:, k * TSL:(k + 1) * TSL], ALU.mult).then_inc(s_6v, 1)

            @block.tensor
            def _(tensor):
                tensor.wait_ge(s_5v, 28)
                for k in range(KC):
                    tensor.wait_ge(s_6v, 17 + 2 * k + 2)
                    for n in range(4):
                        bank = (ps_q if n < 2 else ps_k)
                        bsl = slice((n % 2) * 512, (n % 2) * 512 + 512)
                        tensor.matmul(bank[0:1, bsl],
                                      onesc[:, :], outbf[:, k * TSL + n * 512:k * TSL + n * 512 + 512],
                                      start=(k == 0), stop=(k == KC - 1))
                        mm2 = tensor.matmul(
                            bank[32:33, bsl],
                            onesc[:, :], sqt[:, (k % 2) * TSL + n * 512:(k % 2) * TSL + n * 512 + 512],
                            start=(k == 0), stop=(k == KC - 1))
                    mm2.then_inc(s_6t, 1)

            @block.scalar
            def _(scalar):
                scalar.wait_ge(s_6t, 8)
                for n in range(4):
                    rsl = slice(n * 512, (n + 1) * 512)
                    if n >= 1:
                        scalar.wait_ge(s_6s, 7 * n)
                    scalar.copy(rowA[0:1, rsl],
                                (ps_q if n < 2 else ps_k)[0:1, (n % 2) * 512:(n % 2) * 512 + 512])
                    scalar.copy(rowB[0:1, rsl],
                                (ps_q if n < 2 else ps_k)[32:33, (n % 2) * 512:(n % 2) * 512 + 512]
                                ).then_inc(s_6s, 1)
                    scalar.wait_ge(s_6s, 7 * n + 4)
                    scalar.activation(rowB[0:1, rsl], rowB[0:1, rsl], AF.Ln,
                                      bias=epscol[0:1, 0:1]).then_inc(s_6s, 1)
                    scalar.wait_ge(s_6s, 7 * n + 5)
                    scalar.activation(rowB[0:1, rsl], rowB[0:1, rsl], AF.Exp,
                                      scale=-0.5).then_inc(s_6s, 1)

            @block.vector
            def _(vector):
                prow = lambda n: (ps_q if n < 2 else ps_k)[64:65, (n % 2) * 512:(n % 2) * 512 + 512]
                for n in range(4):
                    rsl = slice(n * 512, (n + 1) * 512)
                    vector.wait_ge(s_6s, 7 * n + 1)
                    vector.tensor_scalar(rowA[0:1, rsl], rowA[0:1, rsl], 1.0 / D, None, ALU.mult)
                    vector.tensor_scalar(rowB[0:1, rsl], rowB[0:1, rsl], 1.0 / D,
                                         None, ALU.mult).then_inc(s_6s, 1)
                    vector.wait_ge(s_6s, 7 * n + 2)
                    vector.tensor_tensor(prow(n), rowA[0:1, rsl], rowA[0:1, rsl],
                                         ALU.mult).then_inc(s_6s, 1)
                    vector.wait_ge(s_6s, 7 * n + 3)
                    vector.tensor_tensor(rowB[0:1, rsl], rowB[0:1, rsl], prow(n),
                                         ALU.subtract).then_inc(s_6s, 1)
                    vector.wait_ge(s_6s, 7 * n + 6)
                    vector.tensor_tensor(rowA[0:1, rsl], rowA[0:1, rsl], rowB[0:1, rsl],
                                         ALU.mult).then_inc(s_6s, 1)

            @block.tensor
            def _(tensor):
                tensor.wait_ge(s_5v, 32)
                for n in range(4):
                    tensor.wait_ge(s_6s, 7 * n + 7)
                    if n >= 2:
                        tensor.wait_ge(s_6re, n - 1)
                    tensor.matmul(ps_v[:, (n % 2) * 512:(n % 2) * 512 + 512], onesf[0:1, :],
                                  rowB[0:1, n * 512:(n + 1) * 512], start=True, stop=True)
                    tensor.matmul(ps_sc[:, (n % 2) * 512:(n % 2) * 512 + 512], onesf[0:1, :],
                                  rowA[0:1, n * 512:(n + 1) * 512], start=True, stop=True).then_inc(s_6rt, 1)

            @block.scalar
            def _(scalar):
                for n in range(4):
                    scalar.wait_ge(s_6rt, n + 1)
                    if n >= 2:
                        scalar.wait_ge(s_6y, 2 * (n - 1))
                        scalar.wait_ge(s_6yg, 2 * (n - 1))
                    scalar.copy(rrep_s[:, (n % 2) * 512:(n % 2) * 512 + 512],
                                ps_v[:, (n % 2) * 512:(n % 2) * 512 + 512])
                    scalar.copy(murr_s[:, (n % 2) * 512:(n % 2) * 512 + 512],
                                ps_sc[:, (n % 2) * 512:(n % 2) * 512 + 512]).then_inc(s_6re, 1)

            @block.vector
            def _(vector):
                vector.wait_ge(s_ld, 160)
                vector.wait_ge(s_5ix, 80)
                for n in range(4):
                    vector.wait_ge(s_6re, n + 1)
                    rr = rrep_s[:, (n % 2) * 512:(n % 2) * 512 + 512]
                    mr = murr_s[:, (n % 2) * 512:(n % 2) * 512 + 512]
                    sl = lambda k: x2T[:, k * TSL + n * 512:k * TSL + (n + 1) * 512]
                    tt = None
                    for k in range(5):
                        tt = vector.tensor_tensor(sl(k), sl(k), rr, ALU.mult)
                    tt.then_inc(s_6y, 1)
                    vector.wait_ge(s_6y, 2 * n + 1)
                    for k in range(5):
                        tt = vector.tensor_tensor(sl(k), sl(k), mr, ALU.subtract)
                    tt.then_inc(s_6y, 1)

            @block.gpsimd
            def _(gpsimd):
                gpsimd.wait_ge(s_ld, 160)
                gpsimd.wait_ge(s_5ix, 80)
                for n in range(4):
                    gpsimd.wait_ge(s_6re, n + 1)
                    rr = rrep_s[:, (n % 2) * 512:(n % 2) * 512 + 512]
                    mr = murr_s[:, (n % 2) * 512:(n % 2) * 512 + 512]
                    sl = lambda k: x2T[:, k * TSL + n * 512:k * TSL + (n + 1) * 512]
                    tt = None
                    for k in (5, 6, 7):
                        tt = gpsimd.tensor_tensor(sl(k), sl(k), rr, ALU.mult)
                    tt.then_inc(s_6yg, 1)
                    gpsimd.wait_ge(s_6yg, 2 * n + 1)
                    for k in (5, 6, 7):
                        tt = gpsimd.tensor_tensor(sl(k), sl(k), mr, ALU.subtract)
                    tt.then_inc(s_6yg, 1)

            @block.scalar
            def _(scalar):
                scalar.wait_ge(s_6t, 8)
                for n in range(4):
                    scalar.wait_ge(s_6y, 2 * n + 2)
                    scalar.wait_ge(s_6yg, 2 * n + 2)
                    sc = None
                    for k in range(KC):
                        sc = scalar.activation(outbf[:, k * TSL + n * 512:k * TSL + (n + 1) * 512],
                                               x2T[:, k * TSL + n * 512:k * TSL + (n + 1) * 512],
                                               AF.Copy, accum_out=ysum_c4[:, k * 4 + n:k * 4 + n + 1])
                    sc.then_inc(s_6ys, 1)

            @block.vector
            def _(vector):
                vector.wait_ge(s_6ys, 4)
                vector.tensor_reduce(ysum_c[:, :], ysum_c4[:, :].rearrange("p (k n) -> p k n", n=4),
                                     AX.X, ALU.add).then_inc(s_6ys, 1)

            @block.gpsimd
            def _(gpsimd):
                gpsimd.wait_ge(s_6ys, 5)
                gpsimd.dma_start(out=e2in.ap(), in_=ysum_c[:, :]).then_inc(s_6g, 16)
                gpsimd.wait_ge(s_6g, 16)
                gpsimd.collective_compute(
                    "AllGather", ALU.bypass, replica_groups=GROUPS,
                    ins=[e2in.ap().opt()], outs=[e2out.ap().opt()]).then_inc(s_cc, 1)
                gpsimd.wait_ge(s_cc, 5)
                gpsimd.dma_start(out=ysum8[:, :].rearrange("p (r k) -> p r k", k=8),
                                 in_=e2out.ap().rearrange("(r p) k -> p r k", p=128)).then_inc(s_6g, 16)
                for q in range(4):
                    if q > 0:
                        gpsimd.wait_ge(s_6gt, 4 * q)
                    gpsimd.dma_start(
                        out=gw1q.rearrange("p (k j) -> p k j", j=512),
                        in_=gw1_in[q * 256:(q + 1) * 256, :].rearrange("(k p) j -> p k j", p=128)
                    ).then_inc(s_6gq, 16)

            @block.vector
            def _(vector):
                vector.wait_ge(s_6g, 32)
                vector.tensor_reduce(gcc[:, :], ysum8[:, :].rearrange("p (r k) -> p k r", k=8),
                                     AX.X, ALU.add).then_inc(s_6gs, 1)

            @block.scalar
            def _(scalar):
                scalar.wait_ge(s_6gs, 1)
                scalar.copy(gccb[:, :], gcc[:, :]).then_inc(s_6gs, 1)

            pg1 = [ps_v[0:128, 0:1], ps_v[0:128, 512:513], ps_sc[0:128, 0:1], ps_sc[0:128, 512:513]]

            def emit_gfnet_tensor(tensor):
                tensor.wait_ge(s_6gs, 2)
                for q in range(4):
                    tensor.wait_ge(s_6gq, 16 * (q + 1))
                    for jb in range(4):
                        for k2 in range(2):
                            mm = tensor.matmul(pg1[jb],
                                               gw1q[:, k2 * 512 + jb * 128:k2 * 512 + jb * 128 + 128],
                                               gccb[:, q * 2 + k2:q * 2 + k2 + 1],
                                               start=(q == 0 and k2 == 0), stop=(q == 3 and k2 == 1))
                        mm.then_inc(s_6gt, 1)
                tensor.wait_ge(s_6gs, 6)
                for jb2 in range(2):
                    for k1 in range(4):
                        mm = tensor.matmul(pg1[jb2],
                                           gw2_s[:, k1 * 256 + jb2 * 128:k1 * 256 + jb2 * 128 + 128],
                                           g1cb[:, k1:k1 + 1], start=(k1 == 0), stop=(k1 == 3))
                    mm.then_inc(s_6gt, 1)
                tensor.wait_ge(s_6gs, 8)
                mm = None
                for k2 in range(2):
                    mm = tensor.matmul(ps_v[0:1, 0:1], gw3_s[:, k2:k2 + 1], g2cb[:, k2:k2 + 1],
                                       start=(k2 == 0), stop=(k2 == 1))
                mm.then_inc(s_6gt, 1)

            def emit_gfnet_scalar(scalar):
                for jb in range(4):
                    scalar.wait_ge(s_6gt, 13 + jb)
                    scalar.activation(g1cb[:, jb:jb + 1], pg1[jb],
                                      AF.Gelu, scale=1.0 / L, bias=gb1c_s[:, jb:jb + 1]).then_inc(s_6gs, 1)
                for jb2 in range(2):
                    scalar.wait_ge(s_6gt, 17 + jb2)
                    scalar.activation(g2cb[:, jb2:jb2 + 1], pg1[jb2],
                                      AF.Gelu, bias=gb2c_s[:, jb2:jb2 + 1]).then_inc(s_6gs, 1)
                scalar.wait_ge(s_6gt, 19)
                scalar.activation(gfab[0:1, 0:1], ps_v[0:1, 0:1], AF.Sigmoid,
                                  bias=gb3c_s[0:1, 0:1]).then_inc(s_6gs, 1)

            if upto == 6:
                @block.tensor
                def _(tensor):
                    emit_gfnet_tensor(tensor)

                @block.scalar
                def _(scalar):
                    emit_gfnet_scalar(scalar)

            @block.vector
            def _(vector):
                vector.wait_ge(s_6gs, 9)
                vector.tensor_scalar(gfab[0:1, 3:4], gfab[0:1, 0:1], 0.7, -0.5,
                                     ALU.mult, ALU.add).then_inc(s_6gs, 1)
                vector.wait_ge(s_6gs, 10)
                vector.tensor_scalar(gfab[0:1, 2:3], gfab[0:1, 3:4], ar_s[0:1, 0:1], 2.0,
                                     ALU.mult, ALU.mult).then_inc(s_6gs, 1)
                vector.wait_ge(s_6gs, 11)
                vector.tensor_scalar(gfab[0:1, 2:3], gfab[0:1, 2:3], 0.1, None, ALU.add).then_inc(s_6gs, 1)
                vector.tensor_scalar(gfab[0:1, 1:2], ar_s[0:1, 0:1], 0.6, None, ALU.mult).then_inc(s_6gs, 1)

            reg_probe("rowA", rowA[0:1, :])
            reg_probe("rowB", rowB[0:1, :])
            reg_probe("ysum_c", ysum_c[:, :])
            reg_probe("gcc", gcc[:, :])
            reg_probe("gfab", gfab[0:1, :])
            reg_probe("ybf0", vs_all[:, 0:TSL])

        if upto >= 7:
            h2o, h3o = 8 * 512, 8 * 512 + 4 * 512

            @block.tensor
            def _(tensor):
                tensor.wait_ge(s_6w, 112)
                tensor.wait_ge(s_6wb, 128)
                for i in range(3):
                    for n in range(4):
                        m = 4 * i + n
                        tb = 16 * m
                        pe = tb - 16
                        if i > 0:
                            tensor.wait_ge(s_7v, 3 * (m - 4) + 3)
                            tensor.wait_ge(s_7g, 3 * (m - 4) + 3)
                        else:
                            tensor.wait_ge(s_6ys, n + 1)
                            if n == 0:
                                tensor.wait_ge(s_6s, 18)
                        for jb in range(8):
                            if jb >= 2:
                                tensor.wait_ge(s_7e, tb + jb - 1)
                            elif m > 0:
                                tensor.wait_ge(s_7e, pe + 7 + jb)
                            for k in range(KC):
                                mm = tensor.matmul(ps_q[:, (jb % 2) * 512:(jb % 2) * 512 + 512],
                                                   w1_s[:, k * D + jb * 128:k * D + jb * 128 + 128],
                                                   outbf[:, k * TSL + n * 512:k * TSL + (n + 1) * 512],
                                                   start=(k == 0), stop=(k == KC - 1))
                            mm.then_inc(s_7t, 1)
                        if i == 0 and n == 0:
                            tensor.wait_ge(s_6s, 25)
                        for jb2 in range(4):
                            if jb2 >= 2:
                                tensor.wait_ge(s_7e, tb + 8 + jb2 - 1)
                            elif m > 0:
                                tensor.wait_ge(s_7e, pe + 11 + jb2)
                            for k1 in range(KC):
                                tensor.wait_ge(s_7e, tb + k1 + 1)
                                mm = tensor.matmul(ps_k[:, (jb2 % 2) * 512:(jb2 % 2) * 512 + 512],
                                                   w2_s[:, k1 * 512 + jb2 * 128:k1 * 512 + jb2 * 128 + 128],
                                                   h1_s[:, k1 * 512:(k1 + 1) * 512],
                                                   start=(k1 == 0), stop=(k1 == KC - 1))
                            mm.then_inc(s_7t, 1)
                        if i == 0 and n == 0:
                            tensor.wait_ge(s_6re, 4)
                            emit_gfnet_tensor(tensor)
                            tensor.wait_ge(s_6gs, 9)
                        for jb3 in range(2):
                            if m > 0:
                                tensor.wait_ge(s_7e, pe + 13 + jb3)
                            for k2 in range(4):
                                tensor.wait_ge(s_7e, tb + 8 + k2 + 1)
                                mm = tensor.matmul(ps_v[:, jb3 * 512:jb3 * 512 + 512],
                                                   w3_s[:, k2 * 256 + jb3 * 128:k2 * 256 + jb3 * 128 + 128],
                                                   h2_s[:, h2o + k2 * 512:h2o + (k2 + 1) * 512],
                                                   start=(k2 == 0), stop=(k2 == 3))
                            mm.then_inc(s_7t, 1)
                        mm = None
                        for k3 in range(2):
                            tensor.wait_ge(s_7e, tb + 12 + k3 + 1)
                            if m > 0 and k3 == 0:
                                tensor.wait_ge(s_7e, pe + 15)
                            mm = tensor.matmul(ps_sc[0:128, 0:512], w4p[:, k3 * 128:(k3 + 1) * 128],
                                               h3_s[:, h3o + k3 * 512:h3o + (k3 + 1) * 512],
                                               start=(k3 == 0), stop=(k3 == 1))
                        mm.then_inc(s_7t, 1)
                        tensor.wait_ge(s_7tf, m + 1)
                        if m > 0:
                            tensor.wait_ge(s_7e, pe + 16)
                        tensor.matmul(ps_sc[:, 512:1024], onesf128[:, :], crowP[:, :],
                                      start=True, stop=True).then_inc(s_7t, 1)

            @block.scalar
            def _(scalar):
                for i in range(3):
                    for n in range(4):
                        m = 4 * i + n
                        tb = 16 * m
                        for jb in range(8):
                            scalar.wait_ge(s_7t, tb + jb + 1)
                            scalar.activation(h1_s[:, jb * 512:(jb + 1) * 512],
                                              ps_q[:, (jb % 2) * 512:(jb % 2) * 512 + 512], AF.Gelu,
                                              bias=b1c_s[:, jb:jb + 1]).then_inc(s_7e, 1)
                        for jb2 in range(4):
                            scalar.wait_ge(s_7t, tb + 8 + jb2 + 1)
                            scalar.activation(h2_s[:, h2o + jb2 * 512:h2o + (jb2 + 1) * 512],
                                              ps_k[:, (jb2 % 2) * 512:(jb2 % 2) * 512 + 512], AF.Gelu,
                                              bias=b2c_s[:, jb2:jb2 + 1]).then_inc(s_7e, 1)
                        if i == 0 and n == 0:
                            emit_gfnet_scalar(scalar)
                        for jb3 in range(2):
                            scalar.wait_ge(s_7t, tb + 12 + jb3 + 1)
                            scalar.activation(h3_s[:, h3o + jb3 * 512:h3o + (jb3 + 1) * 512],
                                              ps_v[:, jb3 * 512:jb3 * 512 + 512], AF.Gelu,
                                              bias=b3c_s[:, jb3:jb3 + 1]).then_inc(s_7e, 1)
                        scalar.wait_ge(s_7t, tb + 15)
                        if m >= 1:
                            scalar.wait_ge(s_7tf, m)
                        scalar.activation(tfrow[0:1, :], ps_sc[0:1, 0:512], AF.Sigmoid,
                                          bias=b4c_s[0:1, 0:1]).then_inc(s_7e, 1)
                        scalar.wait_ge(s_7t, tb + 16)
                        if m >= 2:
                            scalar.wait_ge(s_7v, 3 * (m - 1))
                            scalar.wait_ge(s_7g, 3 * (m - 1))
                        scalar.copy(crep_s[:, (n % 2) * 512:(n % 2) * 512 + 512],
                                    ps_sc[:, 512:1024]).then_inc(s_7e, 1)

            @block.vector
            def _(vector):
                vector.wait_ge(s_6gs, 13)
                for i in range(3):
                    for n in range(4):
                        m = 4 * i + n
                        tb = 16 * m
                        vector.wait_ge(s_7e, tb + 15)
                        if m > 0:
                            vector.wait_ge(s_7t, tb)
                        vector.tensor_scalar(crowP[0:1, :], tfrow[0:1, :], gfab[0:1, 1:2],
                                             gfab[0:1, 2:3], ALU.mult, ALU.add).then_inc(s_7tf, 1)
                        vector.wait_ge(s_7e, tb + 16)
                        if i == 0:
                            vector.wait_ge(s_6ys, 5)
                        if m > 0:
                            vector.wait_ge(s_7v, 3 * m)
                        cs = crep_s[:, (n % 2) * 512:(n % 2) * 512 + 512]
                        sl = lambda k: x2T[:, k * TSL + n * 512:k * TSL + (n + 1) * 512]
                        vector.tensor_tensor(tw[0][:, :], cs, sl(4), ALU.mult)
                        vector.tensor_tensor(tw[1][:, :], cs, sl(0), ALU.mult)
                        vector.tensor_tensor(tw[2][:, :], cs, sl(5), ALU.mult)
                        vector.tensor_tensor(tw[3][:, :], cs, sl(1), ALU.mult).then_inc(s_7v, 1)
                        vector.wait_ge(s_7v, 3 * m + 1)
                        vector.tensor_tensor(sl(0), sl(0), tw[0][:, :], ALU.add)
                        vector.tensor_tensor(sl(4), sl(4), tw[1][:, :], ALU.subtract)
                        vector.tensor_tensor(sl(1), sl(1), tw[2][:, :], ALU.add)
                        vector.tensor_tensor(sl(5), sl(5), tw[3][:, :],
                                             ALU.subtract).then_inc(s_7v, 1)
                        vector.wait_ge(s_7v, 3 * m + 2)
                        tc = None
                        for k in (0, 1, 4, 5):
                            tc = vector.tensor_copy(outbf[:, k * TSL + n * 512:k * TSL + (n + 1) * 512],
                                                    sl(k))
                        tc.then_inc(s_7v, 1)

            @block.gpsimd
            def _(gpsimd):
                for i in range(3):
                    for n in range(4):
                        m = 4 * i + n
                        tb = 16 * m
                        gpsimd.wait_ge(s_7e, tb + 16)
                        if i == 0:
                            gpsimd.wait_ge(s_6ys, 5)
                        if m > 0:
                            gpsimd.wait_ge(s_7g, 3 * m)
                        cs = crep_s[:, (n % 2) * 512:(n % 2) * 512 + 512]
                        sl = lambda k: x2T[:, k * TSL + n * 512:k * TSL + (n + 1) * 512]
                        gpsimd.tensor_tensor(tw[4][:, :], cs, sl(6), ALU.mult)
                        gpsimd.tensor_tensor(tw[5][:, :], cs, sl(2), ALU.mult)
                        gpsimd.tensor_tensor(tw[6][:, :], cs, sl(7), ALU.mult)
                        gpsimd.tensor_tensor(tw[7][:, :], cs, sl(3), ALU.mult).then_inc(s_7g, 1)
                        gpsimd.wait_ge(s_7g, 3 * m + 1)
                        gpsimd.tensor_tensor(sl(2), sl(2), tw[4][:, :], ALU.subtract)
                        gpsimd.tensor_tensor(sl(6), sl(6), tw[5][:, :], ALU.add)
                        gpsimd.tensor_tensor(sl(3), sl(3), tw[6][:, :], ALU.subtract)
                        gpsimd.tensor_tensor(sl(7), sl(7), tw[7][:, :],
                                             ALU.add).then_inc(s_7g, 1)
                        gpsimd.wait_ge(s_7g, 3 * m + 2)
                        tc = None
                        for k in (2, 3, 6, 7):
                            tc = gpsimd.tensor_copy(outbf[:, k * TSL + n * 512:k * TSL + (n + 1) * 512],
                                                    sl(k))
                        tc.then_inc(s_7g, 1)

            reg_probe("h1s", kT[0][:, 0:4096])
            reg_probe("tfrow", tfrow[0:1, :])
            reg_probe("crow", crowP[0:1, :])

        if upto >= 8:
            @block.gpsimd
            def _(gpsimd):
                gpsimd.wait_ge(s_7v, 36)
                gpsimd.wait_ge(s_7g, 36)
                gpsimd.wait_ge(s_ld, 160)
                gpsimd.wait_ge(s_5ix, 80)
                for k in range(KC):
                    gpsimd.dma_start(out=out.rearrange("(k p) n -> k p n", p=128)[k, :, :],
                                     in_=x2T[:, k * TSL:(k + 1) * TSL],
                                     accum_op=ALU.add).then_inc(s_8d, 16)

        reg_probe("m_buf", m_buf[:, :])
        reg_probe("idxrows", idxrows[:, :])
        reg_probe("idxu", idxu[:, :])
        reg_probe("vme", vme[:, :])
        reg_probe("baserow", baserow[:, :])
        reg_probe("x2s", x2T[:, 0:2048])
        reg_probe("x2s7", x2T[:, 7 * TSL:8 * TSL])
        reg_probe("qpair0", qpair[0][:, :])
        reg_probe("qpair1", qpair[1][:, :])
        reg_probe("expT", expT[:, :])
        reg_probe("cc", cc[:, :])
        reg_probe("ccp0", ccp[0][:, :])
        reg_probe("ccp1", ccp[1][:, :])
        reg_probe("vmcl", vmcl[:, :])
        reg_probe("dst0", dstage[0][:, :])
        reg_probe("dst1", dstage[1][:, :])

        if upto < 6:
            out_stage = sbB("out_stage", [128, TSL], F32)

            @block.vector
            def _(vector):
                vector.memset(out_stage[:, :], 0.0).then_inc(s_st, 1)

        @block.sync
        def _(sync):
            sync.wait_ge(s_psk, 17)
            sync.wait_ge(s_pskb, 1)
            if upto < 6:
                sync.wait_ge(s_st, 1)
            if upto >= 1:
                sync.wait_ge(s_e0, 1)
            if upto >= 2:
                sync.wait_ge(s_mst, NT)
                sync.wait_ge(s_evv, NT)
                sync.wait_ge(s_evk, NT)
                sync.wait_ge(s_evq, NT)
                sync.wait_ge(s_scp, NT)
            if upto >= 3:
                sync.wait_ge(s_bs, 8)
                sync.wait_ge(s_cms, 2)
                sync.wait_ge(s_4vs, 2)
            if upto >= 4:
                sync.wait_ge(s_cc, 4)
                sync.wait_ge(s_4dd, 64)
                sync.wait_ge(s_4se, 64)
                sync.wait_ge(s_4ds, 5)
                sync.wait_ge(s_4cv, 6)
            if upto >= 5:
                sync.wait_ge(s_5v, 32)
            if upto >= 6:
                sync.wait_ge(s_6gs, 13)
                sync.wait_ge(s_6ys, 5)
                sync.wait_ge(s_ld, 160)
                sync.wait_ge(s_5ix, 80)
            if upto >= 7:
                sync.wait_ge(s_7t, 192)
                sync.wait_ge(s_7e, 192)
                sync.wait_ge(s_7v, 36)
                sync.wait_ge(s_7g, 36)
            if upto >= 8:
                sync.wait_ge(s_8d, 128)
            n = 0
            for name, ap in probe_srcs.items():
                sync.dma_start(out=probe_aps[name], in_=ap).then_inc(s_st, 16)
                n += 16
            if upto < 6:
                for k in range(KC):
                    sync.dma_start(out=out.rearrange("(k p) n -> k p n", p=128)[k, :, :],
                                   in_=out_stage[:, :]).then_inc(s_st, 16)
                    n += 16
            base = 1 if upto < 6 else 0
            sync.wait_ge(s_st, base + n)

        if upto < 6:
            esB.close()

    return nc, probe_aps


def _prep_core_inputs(inputs, c):
    b, q = c // 4, c % 4
    x = np.asarray(inputs["L0_attn_x"], np.float32)
    o0, o1 = q * MSL, (q + 1) * MSL
    t0 = q * TSL
    f32c = lambda a: np.ascontiguousarray(a, np.float32)
    m = {}
    m["xT"] = f32c(x[b].T)
    m["xTsl"] = f32c(x[b].T[:, t0:t0 + TSL])
    m["xR"] = f32c(x[b])
    m["wqT"] = f32c(np.asarray(inputs["L0_attn_Wq"])[o0:o1, :].T)
    m["wkT"] = f32c(np.asarray(inputs["L0_attn_Wk"])[o0:o1, :].T)
    m["wvT"] = f32c(np.asarray(inputs["L0_attn_Wv"])[o0:o1, :].T)
    m["woR"] = f32c(np.asarray(inputs["L0_attn_Wo"]).T[o0:o1, :])
    m["woT"] = f32c(np.asarray(inputs["L0_attn_Wo"]).T)
    m["bq"] = f32c(np.asarray(inputs["L0_attn_bq"])[o0:o1].reshape(1, MSL))
    m["bk"] = f32c(np.asarray(inputs["L0_attn_bk"])[o0:o1].reshape(1, MSL))
    m["bv"] = f32c(np.asarray(inputs["L0_attn_bv"])[o0:o1].reshape(1, MSL))
    m["bo"] = f32c(np.asarray(inputs["L0_attn_bo"]).reshape(1, D))
    m["idxk"] = np.ascontiguousarray(np.asarray(inputs["L0_attn_idx_k"], np.int32))
    m["identf"] = np.eye(128, dtype=np.float32)
    m["identb"] = np.eye(128, dtype=np.float32).astype(bfdt)
    m["iota_sl"] = f32c(np.broadcast_to(np.arange(t0, t0 + TSL, dtype=np.float32), (128, TSL)))
    m["iota_j"] = f32c(np.broadcast_to(np.arange(64, dtype=np.float32), (128, 64)))
    m["iota_ml"] = f32c((np.arange(64, dtype=np.float32)[None, :] * 128.0
                         + np.arange(128, dtype=np.float32)[:, None]))
    m["cw1"] = f32c(np.asarray(inputs["L0_ffn_cn_w1"]))
    m["cw2"] = f32c(np.asarray(inputs["L0_ffn_cn_w2"]))
    m["cw3"] = f32c(np.asarray(inputs["L0_ffn_cn_w3"]))
    m["cw4"] = f32c(np.asarray(inputs["L0_ffn_cn_w4"]).reshape(2, 128).T)
    m["gw1"] = f32c(np.asarray(inputs["L0_ffn_gc_w1"]))
    m["gw2"] = f32c(np.asarray(inputs["L0_ffn_gc_w2"]))
    m["gw3"] = f32c(np.asarray(inputs["L0_ffn_gc_w3"]).reshape(2, 128).T)
    m["b1c"] = f32c(np.asarray(inputs["L0_ffn_cn_b1"]).reshape(8, 128).T)
    m["b2c"] = f32c(np.asarray(inputs["L0_ffn_cn_b2"]).reshape(4, 128).T)
    m["b3c"] = f32c(np.asarray(inputs["L0_ffn_cn_b3"]).reshape(2, 128).T)
    m["b4c"] = f32c(np.asarray(inputs["L0_ffn_cn_b4"]).reshape(1, 1))
    m["gb1c"] = f32c(np.asarray(inputs["L0_ffn_gc_b1"]).reshape(4, 128).T)
    m["gb2c"] = f32c(np.asarray(inputs["L0_ffn_gc_b2"]).reshape(2, 128).T)
    m["gb3c"] = f32c(np.asarray(inputs["L0_ffn_gc_b3"]).reshape(1, 1))
    m["arin"] = f32c(np.asarray(inputs["L0_ffn_adaptive_range"]).reshape(1, 1))
    return m


def kernel(**inputs):
    nc, _ = build_nc(upto=8)
    in_maps = [_prep_core_inputs(inputs, c) for c in range(8)]
    res = run_bass_kernel_spmd(nc, in_maps, core_ids=list(range(8)))
    outp = np.zeros((B, L, D), np.float32)
    for c in range(8):
        b, q = c // 4, c % 4
        outp[b, q * TSL:(q + 1) * TSL, :] = np.asarray(res.results[c]["out"]).T
    return outp
